# version 13
# baseline (speedup 1.0000x reference)
import math
import numpy as np
import concourse.bass as bass
import concourse.mybir as mybir
from concourse.bass_utils import run_bass_kernel_spmd

F32 = mybir.dt.float32
BF16 = mybir.dt.bfloat16
U32 = mybir.dt.uint32
I32 = mybir.dt.int32
U8 = mybir.dt.uint8
AF = mybir.ActivationFunctionType
ALU = mybir.AluOpType
AX = mybir.AxisListType

ENG = ("sp", "pe", "act", "dve", "pool")

D = 4096
TOK = 1024
NKD = 32
EPS = 1e-5


class _Rec:
    __slots__ = ("method", "args", "kwargs", "waits", "incs")

    def __init__(self, method, args, kwargs, waits, incs):
        self.method = method
        self.args = args
        self.kwargs = kwargs
        self.waits = waits
        self.incs = incs


class Prog:

    def __init__(self, nc):
        self.nc = nc
        self.q = {e: [] for e in ENG}
        self.counts = {}
        self._last_wait = {e: {} for e in ENG}
        self.streams = {}
        self.stream_idx = {}
        self.stream_last = {}

    def _norm_waits(self, eng, waits):
        out = []
        for t in waits:
            if t is None:
                continue
            sem, val = t
            if self._last_wait[eng].get(sem, -1) >= val:
                continue
            self._last_wait[eng][sem] = val
            out.append((sem, val))
        return out

    def emit(self, eng, method, *args, waits=(), inc=None, inc_n=1, **kwargs):
        w = self._norm_waits(eng, waits)
        token = None
        incs = []
        if inc is not None:
            self.counts[inc] = self.counts.get(inc, 0) + inc_n
            token = (inc, self.counts[inc])
            incs = [(inc, inc_n)]
        self.q[eng].append(_Rec(method, args, kwargs, w, incs))
        return token

    def stream(self, name, rot=2):
        self.streams[name] = rot
        self.stream_idx.setdefault(name, 0)
        for i in range(rot):
            self.stream_last.setdefault(f"{name}{i}", None)

    def dma(self, eng, out, in_, stream, waits=(), **kwargs):
        i = self.stream_idx[stream]
        self.stream_idx[stream] = i + 1
        sem = f"{stream}{i % self.streams[stream]}"
        w = list(waits)
        prev = self.stream_last[sem]
        if prev is not None:
            w.append(prev)
        t = self.emit(eng, "dma_start", waits=w, inc=sem, inc_n=16,
                      out=out, in_=in_, **kwargs)
        self.stream_last[sem] = t
        return t

    def stream_all(self, name):
        return [self.tok(f"{name}{i}") for i in range(self.streams[name])]

    def tok(self, sem):
        return (sem, self.counts.get(sem, 0))

    def barrier(self, eng, waits):
        return self.emit(eng, "engine_nop", waits=waits)

    def raw(self, eng, fn, waits=(), inc=None, inc_n=1):
        w = self._norm_waits(eng, waits)
        token = None
        incs = []
        if inc is not None:
            self.counts[inc] = self.counts.get(inc, 0) + inc_n
            token = (inc, self.counts[inc])
            incs = [(inc, inc_n)]
        self.q[eng].append(_Rec(None, (fn,), {}, w, incs))
        return token

    def replay(self, sem_handles):
        nc = self.nc
        with nc.Block() as block:
            def mk(eng_name):
                def body(engobj):
                    for rec in self.q[eng_name]:
                        for (s, v) in rec.waits:
                            engobj.wait_ge(sem_handles[s], v)
                        if rec.method is None:
                            inst = rec.args[0](engobj)
                        else:
                            inst = getattr(engobj, rec.method)(*rec.args, **rec.kwargs)
                        for (s, v) in rec.incs:
                            inst.then_inc(sem_handles[s], v)
                return body
            block.sync(mk("sp"))
            block.tensor(mk("pe"))
            block.scalar(mk("act"))
            block.vector(mk("dve"))
            block.gpsimd(mk("pool"))


def alloc_sems(nc, names):
    return {n: nc.alloc_semaphore(n) for n in names}


ENGINE_SEMS = ["s_pe", "s_act", "s_dve", "s_pool", "s_sp", "s_cc"]


def prog_sems(p):
    names = list(ENGINE_SEMS)
    for name, rot in p.streams.items():
        names += [f"{name}{i}" for i in range(rot)]
    return names

DT_SIZE = {F32: 4, BF16: 2, U32: 4, I32: 4, U8: 1}


class SBufArena:

    def __init__(self, nc, nbytes, name="arena"):
        self.t = nc.alloc_sbuf_tensor(name, [128, nbytes], U8)
        self.nbytes = nbytes
        self.off = 0
        self.war = []

    def alloc(self, shape, dtype, align=64):
        assert len(shape) == 2 and shape[0] <= 128
        sz = shape[1] * DT_SIZE[dtype]
        off = (self.off + align - 1) // align * align
        assert off + sz <= self.nbytes, f"SBUF arena overflow: {off}+{sz} > {self.nbytes}"
        self.off = off + sz
        ap = self.t.ap()[0:shape[0], off:off + sz].bitcast(dtype)
        return ap

    def alloc128(self, shape, dtype, align=64):
        return self.alloc(shape, dtype, align)

    def mark(self):
        return self.off

    def reset(self, mark=0):
        self.off = mark


def phase_war(p, ar, stores=()):
    toks = [p.tok("s_pe"), p.tok("s_act"), p.tok("s_dve"), p.tok("s_pool")]
    for s in stores:
        toks += p.stream_all(s)
    ar.war = [t for t in toks if t[1] > 0]


def psum_wait(ctx, bank):
    return ctx["psum_last"].get(bank)


def psum_claim(ctx, bank, token):
    ctx["psum_last"][bank] = token



def emit_ln_transpose(p, ctx, *, x_ap, xnT, rms_only=False, mamba_rms=False,
                      gamma=None, x_ready=(), n_tt=8, nkd=NKD, dname="x",
                      xnT_war=(), light=False):
    ar = ctx["ar"]
    m0 = ar.mark()
    dd = nkd * 128
    tt = n_tt * 128
    nxb = 1 if light else 2
    xb = [ar.alloc([128, dd], F32) for _ in range(nxb)]
    xn = [ar.alloc([128, dd], BF16) for _ in range(2)]
    stats = [ar.alloc([128, 8], F32) for _ in range(2)]
    ps = ctx["psum"]
    ident = ctx["ident_bf"]

    sq_tok = {}
    ss_tok = {}
    apply_tok = {}
    tr_tok = {}
    ev_tok = {}
    last = None
    for t in range(n_tt):
        par = t % 2
        xpar = par % nxb
        ld_waits = list(x_ready)
        if t < 2:
            ld_waits += ar.war
        else:
            ld_waits += [sq_tok[t - 2], apply_tok[t - 2]]
        if light and t >= 1:
            ld_waits += [sq_tok[t - 1], apply_tok[t - 1]]
        ld = p.dma("sp", xb[xpar][:, :], x_ap[t * 128:(t + 1) * 128, :], dname,
                   waits=ld_waits)
        s_ = stats[par]
        xn_war = [tr_tok[t - 2]] if t >= 2 else []
        sq = xn[par]
        sq_waits = [ld] + ([ss_tok[t - 1]] if t >= 1 else []) + xn_war
        t_sq = p.emit("act", "activation", sq[:, :], xb[xpar][:, :], AF.Square,
                      waits=sq_waits, inc="s_act")
        sq_tok[t] = t_sq
        stat_war = [apply_tok[t - 2]] if t >= 2 else []
        t_ss = p.emit("dve", "tensor_reduce", s_[:, 1:2], sq[:, :], AX.X, ALU.add,
                      waits=[t_sq] + stat_war, inc="s_dve")
        ss_tok[t] = t_ss
        if rms_only:
            t_pre = p.emit("dve", "tensor_scalar", waits=[t_ss], inc="s_dve",
                           out=s_[:, 6:7], in0=s_[:, 1:2], scalar1=1.0 / dd,
                           scalar2=EPS, op0=ALU.mult, op1=ALU.add)
            t_v = p.emit("act", "activation", s_[:, 4:5], s_[:, 6:7], AF.Ln,
                         waits=[t_pre], inc="s_act")
            t_r = p.emit("act", "activation", s_[:, 5:6], s_[:, 4:5], AF.Exp,
                         scale=-0.5, waits=[t_v], inc="s_act")
            t_ap = p.emit("dve", "tensor_scalar", waits=[t_r, ld] + xn_war,
                          inc="s_dve",
                          out=xn[par][:, :], in0=xb[xpar][:, :],
                          scalar1=s_[:, 5:6], scalar2=None, op0=ALU.mult)
        else:
            t_s = p.emit("dve", "tensor_reduce", s_[:, 0:1], xb[xpar][:, :], AX.X,
                         ALU.add, waits=[ld] + stat_war, inc="s_dve")
            t_m = p.emit("dve", "tensor_scalar", waits=[t_s], inc="s_dve",
                         out=s_[:, 2:3], in0=s_[:, 0:1], scalar1=1.0 / dd,
                         scalar2=None, op0=ALU.mult)
            t_nm = p.emit("dve", "tensor_scalar", waits=[t_s], inc="s_dve",
                          out=s_[:, 3:4], in0=s_[:, 0:1], scalar1=-1.0 / dd,
                          scalar2=None, op0=ALU.mult)
            t_e2 = p.emit("dve", "tensor_scalar", waits=[t_ss], inc="s_dve",
                          out=s_[:, 6:7], in0=s_[:, 1:2], scalar1=1.0 / dd,
                          scalar2=EPS, op0=ALU.mult, op1=ALU.add)
            t_v0 = p.emit("dve", "scalar_tensor_tensor", waits=[t_m, t_e2],
                          inc="s_dve",
                          out=s_[:, 4:5], in0=s_[:, 2:3], scalar=s_[:, 2:3],
                          in1=s_[:, 6:7], op0=ALU.mult, op1=ALU.subtract)
            t_v = p.emit("act", "activation", s_[:, 2:3], s_[:, 4:5], AF.Ln,
                         scale=-1.0, waits=[t_v0], inc="s_act")
            t_r = p.emit("act", "activation", s_[:, 5:6], s_[:, 2:3], AF.Exp,
                         scale=-0.5, waits=[t_v], inc="s_act")
            if mamba_rms:
                t_u = p.emit("dve", "tensor_scalar", waits=[t_v0], inc="s_dve",
                             out=s_[:, 6:7], in0=s_[:, 4:5], scalar1=-1.0,
                             scalar2=None, op0=ALU.mult)
                t_ru = p.emit("dve", "reciprocal", s_[:, 7:8], s_[:, 6:7],
                              waits=[t_u], inc="s_dve")
                t_w = p.emit("dve", "tensor_scalar", waits=[t_u], inc="s_dve",
                             out=s_[:, 6:7], in0=s_[:, 6:7], scalar1=-EPS,
                             scalar2=None, op0=ALU.add)
                t_w2 = p.emit("dve", "tensor_scalar", waits=[t_w, t_ru], inc="s_dve",
                              out=s_[:, 6:7], in0=s_[:, 6:7], scalar1=s_[:, 7:8],
                              scalar2=EPS, op0=ALU.mult, op1=ALU.add)
                t_lw = p.emit("act", "activation", s_[:, 6:7], s_[:, 6:7], AF.Ln,
                              waits=[t_w2], inc="s_act")
                t_ew = p.emit("act", "activation", s_[:, 6:7], s_[:, 6:7], AF.Exp,
                              scale=-0.5, waits=[t_lw], inc="s_act")
                t_r = p.emit("dve", "tensor_mul", waits=[t_r, t_ew], inc="s_dve",
                             out=s_[:, 5:6], in0=s_[:, 5:6], in1=s_[:, 6:7])
            t_ap = p.emit("dve", "tensor_scalar", waits=[t_r, ld] + xn_war,
                          inc="s_dve",
                          out=xn[par][:, :], in0=xb[xpar][:, :],
                          scalar1=s_[:, 3:4], scalar2=s_[:, 5:6],
                          op0=ALU.add, op1=ALU.mult)
        apply_tok[t] = t_ap
        for kg in range(nkd // 4):
            bank = 4 + (kg % 4)
            pb = ps[bank].bitcast(BF16)
            pw = psum_wait(ctx, bank)
            tr = None
            for j in range(4):
                k = kg * 4 + j
                tr = p.emit("pe", "transpose", pb[:, j * 128:(j + 1) * 128],
                            xn[par][:, k * 128:(k + 1) * 128], ident,
                            waits=[t_ap] + ([pw] if pw and j == 0 else []),
                            inc=("s_pe" if j == 3 else None))
            if gamma is not None:
                ev = None
                for j in range(4):
                    k = kg * 4 + j
                    dstk = xnT[:, k * tt + t * 128: k * tt + (t + 1) * 128]
                    ev = p.emit("act", "activation", dstk,
                                pb[:, j * 128:(j + 1) * 128], AF.Copy,
                                scale=gamma[:, k:k + 1],
                                waits=[tr] + list(xnT_war), inc="s_act")
            else:
                dst = xnT.rearrange("p (k c) -> p k c", c=tt)[:, kg * 4:kg * 4 + 4,
                                                             t * 128:(t + 1) * 128]
                ev = p.emit("dve", "tensor_copy", waits=[tr] + list(xnT_war),
                            inc="s_dve", out=dst, in_=pb[:, :512])
            psum_claim(ctx, bank, ev)
            last = ev
        tr_tok[t] = last
    ar.reset(m0)
    phase_war(p, ar)
    return list(tr_tok.values()), last


def emit_ffn_mm1(p, ctx, *, wcache_ap=None, xnT, h_out_ap, x_ready_toks,
                 n_pairs=86, nk=NKD, tok=TOK, pairs=None, hook=None):
    ar = ctx["ar"]
    m0 = ar.mark()
    wbufs = [ar.alloc([128, 2 * nk * 128], BF16) for _ in range(2)]
    silu_tmps = [ar.alloc([128, tok], F32) for _ in range(2)]
    h_tiles = [ar.alloc([128, tok], BF16) for _ in range(2)]
    ps = ctx["psum"]

    if pairs is None:
        pairs = [(m, wcache_ap[m], ()) for m in range(n_pairs)]
    pe_done, act_done, dve_done, st_tok = {}, {}, {}, {}
    for idx, (m, wc_ap, exw) in enumerate(pairs):
        par = idx % 2
        waits = list(exw)
        if idx < 2:
            waits += ar.war
        else:
            waits.append(("s_pe", pe_done[idx - 2]))
        wtok = p.dma("sp", wbufs[par][:, :], wc_ap, "w", waits=waits)

        pe_waits = [wtok] + list(x_ready_toks)
        if idx >= 2:
            pe_waits.append(("s_act", act_done[idx - 2]))
            pe_waits.append(("s_dve", dve_done[idx - 2]))
        groups = []
        for gu in range(2):
            for h in range(2):
                bank = par * 4 + gu * 2 + h
                psb = ps[bank]
                w0 = list(pe_waits)
                pw = psum_wait(ctx, bank)
                if pw is not None:
                    w0.append(pw)
                for k in range(nk):
                    w_tile = wbufs[par][:, (gu * nk + k) * 128:(gu * nk + k + 1) * 128]
                    x_tile = xnT[:, k * tok + h * 512: k * tok + h * 512 + 512]
                    t = p.emit("pe", "matmul", psb[:, :512], w_tile, x_tile,
                               waits=w0 if k == 0 else (),
                               inc=("s_pe" if k == nk - 1 else None),
                               start=(k == 0), stop=(k == nk - 1))
                groups.append(t)
        pe_done[idx] = groups[-1][1]
        act_waits_war = [("s_dve", dve_done[idx - 2])] if idx >= 2 else []
        dve_waits_war = [st_tok[idx - 2]] if idx >= 2 else []
        for h in range(2):
            gate_bank = par * 4 + h
            up_bank = par * 4 + 2 + h
            ta = p.emit("act", "activation", silu_tmps[par][:, h * 512:(h + 1) * 512],
                        ps[gate_bank][:, :512], AF.Silu,
                        waits=[groups[h]] + act_waits_war, inc="s_act")
            td = p.emit("dve", "tensor_mul", waits=[ta, groups[2 + h]] + dve_waits_war,
                        inc="s_dve",
                        out=h_tiles[par][:, h * 512:(h + 1) * 512],
                        in0=silu_tmps[par][:, h * 512:(h + 1) * 512],
                        in1=ps[up_bank][:, :512])
            psum_claim(ctx, gate_bank, ta)
            psum_claim(ctx, up_bank, td)
        act_done[idx] = ta[1]
        dve_done[idx] = td[1]
        st_tok[idx] = p.dma("pool", h_out_ap[m], h_tiles[par][:, :], "hout",
                            waits=[td])
        if hook is not None:
            hook(idx)
    if hook is not None:
        for extra in range(len(pairs), len(pairs) + 4):
            hook(extra)
    phase_war(p, ar, stores=("hout",))
    ar.reset(m0)
    return p.stream_all("hout")


def emit_ffn_mm2(p, ctx, *, wd_cache_ap, h_ap, x_ap, xout_ap, h_ready_toks,
                 nk=86, n_oc=8, n_tt=8, scale=0.5, w_ready=()):
    ar = ctx["ar"]
    m0 = ar.mark()
    wb = [ar.alloc([128, 512], BF16) for _ in range(3)]
    hb = [ar.alloc([128, TOK], BF16) for _ in range(3)]
    xb = [ar.alloc([128, 512], F32) for _ in range(2 * n_tt)]
    ob = [ar.alloc([128, 512], F32) for _ in range(2 * n_tt)]
    ps = ctx["psum"]

    hrdy = list(h_ready_toks)
    ev_done = {}
    st_done = {}
    pe_hist = {}
    for oc in range(n_oc):
        ocp = oc % 2
        xtoks = []
        for t in range(n_tt):
            xw = []
            if oc < 2:
                xw += ar.war
            if (oc - 2, t) in st_done:
                xw.append(st_done[(oc - 2, t)])
            xtoks.append(p.dma("sp", xb[ocp * n_tt + t][:, :],
                               x_ap[t * 128:(t + 1) * 128, oc * 512:(oc + 1) * 512],
                               "x", waits=xw))
        tk = None
        for k in range(nk):
            kb = (oc * nk + k) % 3
            wwar = []
            if (oc * nk + k) < 3:
                wwar += ar.war + list(w_ready)
            else:
                wwar.append(("s_pe", pe_hist[(oc * nk + k) - 3]))
            wd_t = (wd_cache_ap(oc, k) if callable(wd_cache_ap)
                    else wd_cache_ap[oc, k])
            wtok = p.dma("sp", wb[kb][:, :], wd_t, "w", waits=wwar)
            htok = p.dma("act", hb[kb][:, :], h_ap[k], "ha",
                         waits=hrdy + wwar)
            for t in range(n_tt):
                pe_waits = [wtok, htok]
                if k == 0:
                    pw = psum_wait(ctx, t)
                    if pw is not None:
                        pe_waits.append(pw)
                tk = p.emit("pe", "matmul", ps[t][:, :512],
                            hb[kb][:, t * 128:(t + 1) * 128], wb[kb][:, :],
                            waits=pe_waits,
                            inc=("s_pe" if t == n_tt - 1 else None),
                            start=(k == 0), stop=(k == nk - 1))
            pe_hist[oc * nk + k] = tk[1]
        for t in range(n_tt):
            ev = p.emit("dve", "scalar_tensor_tensor",
                        waits=[tk, xtoks[t]], inc="s_dve",
                        out=ob[ocp * n_tt + t][:, :], in0=ps[t][:, :512],
                        scalar=scale, in1=xb[ocp * n_tt + t][:, :],
                        op0=ALU.mult, op1=ALU.add)
            ev_done[(oc, t)] = ev
            psum_claim(ctx, t, ev)
            st = p.dma("pool", xout_ap[t * 128:(t + 1) * 128, oc * 512:(oc + 1) * 512],
                       ob[ocp * n_tt + t][:, :], "xst", waits=[ev])
            st_done[(oc, t)] = st
    phase_war(p, ar, stores=("xst",))
    ar.reset(m0)
    return p.stream_all("xst")


def make_ctx(nc, p, arena_bytes=None):
    if arena_bytes is None:
        arena_bytes = ((nc.sbuf_bytes_remaining - 2048) // 64) * 64
    ctx = {}
    ctx["ar"] = SBufArena(nc, arena_bytes)
    ctx["psum"] = [nc.alloc_psum_tensor(f"psb{i}", [128, 512], F32).ap()
                   for i in range(8)]
    ctx["psum_last"] = {}
    for name, rot in (("x", 2), ("w", 3), ("hout", 2),
                      ("xst", 2), ("misc", 2), ("cst", 2), ("yst", 2),
                      ("ha", 3),
                      ("hin", 3), ("pb", 2), ("cx", 2),
                      ("cl", 2), ("cs", 2), ("hs", 2)):
        p.stream(name, rot)
    return ctx



def emit_onehot_tables(p, ctx, *, gidx_ap, iota_ap, oneh, n_groups, nkd, rep,
                       stream="misc"):
    t = None
    for k in range(nkd):
        ld = p.dma("sp", rep, gidx_ap[k * 128:(k + 1) * 128].unsqueeze(0)
                   .broadcast_to([n_groups, 128]), stream,
                   waits=[t] if t is not None else ctx["ar"].war)
        t = p.emit("dve", "tensor_scalar", waits=[ld], inc="s_dve",
                   out=oneh[:, k * 128:(k + 1) * 128], in0=rep,
                   scalar1=iota_ap, scalar2=None, op0=ALU.is_equal)
    return t


def emit_dequant_gu(p, ctx, *, qw_ap, qz_ap, sc_ap, oneh, mask_ap, pow2_ap,
                    out_slice_ap, n_pairs=11, nkd=NKD):
    ar = ctx["ar"]
    m0 = ar.mark()
    CW = n_pairs * 128
    qz_s = ar.alloc([32, 2 * CW // 8], U32)
    zq_s = ar.alloc([32, 2 * CW], F32)
    sc_s = ar.alloc([32, 2 * CW], F32)
    off_s = ar.alloc([32, 2 * CW], F32)
    scb_s = ar.alloc([32, 2 * CW], BF16)
    offb_s = ar.alloc([32, 2 * CW], BF16)
    qw_b = [ar.alloc([128, CW], U32) for _ in range(2)]
    msk_b = [ar.alloc([128, CW], U32) for _ in range(2)]
    nib_b = [b.bitcast(F32) for b in msk_b]
    w_b = [b.bitcast(F32) for b in qw_b]
    wo_b = [ar.alloc([128, CW], BF16) for _ in range(2)]
    ps = ctx["psum"]

    ldz = p.dma("sp", qz_s, qz_ap, "misc", waits=ar.war)
    lds = p.dma("sp", sc_s, sc_ap, "misc", waits=ar.war)
    zq_u = zq_s.bitcast(U32)
    tz = None
    for j in range(8):
        tz = p.emit("dve", "tensor_scalar", waits=[ldz], inc="s_dve",
                    out=zq_u[:, j::8], in0=qz_s,
                    scalar1=4 * j, scalar2=15,
                    op0=ALU.logical_shift_right, op1=ALU.bitwise_and)
    tz1 = p.emit("dve", "tensor_scalar", waits=[tz], inc="s_dve",
                 out=zq_s, in0=zq_u, scalar1=1.0, scalar2=None, op0=ALU.add)
    toff = p.emit("dve", "tensor_mul", waits=[tz1, lds], inc="s_dve",
                  out=off_s, in0=zq_s, in1=sc_s)
    tscb = p.emit("dve", "tensor_copy", waits=[lds], inc="s_dve",
                  out=scb_s, in_=sc_s)
    toffb = p.emit("dve", "tensor_copy", waits=[toff], inc="s_dve",
                   out=offb_s, in_=off_s)

    st_toks = {}
    prev = {}
    i = 0
    for gu in range(2):
        for k in range(nkd):
            par = i % 2
            w_ld = list(ar.war) if i < 2 else [prev[i - 2]["sub"]]
            ldq = p.dma("sp", qw_b[par],
                        qw_ap[16 * k:16 * (k + 1), gu * CW:(gu + 1) * CW]
                        .unsqueeze(1).broadcast_to([16, 8, CW]),
                        "w", waits=w_ld)
            t_msk = p.emit("dve", "tensor_scalar",
                           waits=[ldq] + ([prev[i - 2]["nib"]] if i >= 2 else []),
                           inc="s_dve",
                           out=msk_b[par], in0=qw_b[par],
                           scalar1=mask_ap, scalar2=None, op0=ALU.bitwise_and)
            t_nib = p.emit("dve", "tensor_scalar",
                           waits=[t_msk] + ([prev[i - 2]["mul"]] if i >= 2 else []),
                           inc="s_dve",
                           out=nib_b[par], in0=msk_b[par],
                           scalar1=pow2_ap, scalar2=None, op0=ALU.mult)
            t_mul = None
            t_sub = None
            for c3 in range((CW + 511) // 512):
                ncol = min(512, CW - c3 * 512)
                scp = ps[c3]
                offp = ps[3 + c3]
                mm_w = [ctx["oneh_tok"], tscb]
                pw = psum_wait(ctx, c3)
                if pw is not None:
                    mm_w.append(pw)
                m1 = p.emit("pe", "matmul", scp[:, :ncol], oneh[:, k * 128:(k + 1) * 128],
                            scb_s[:, gu * CW + c3 * 512: gu * CW + c3 * 512 + ncol],
                            waits=mm_w, inc="s_pe", start=True, stop=True)
                pw = psum_wait(ctx, 3 + c3)
                m2 = p.emit("pe", "matmul", offp[:, :ncol], oneh[:, k * 128:(k + 1) * 128],
                            offb_s[:, gu * CW + c3 * 512: gu * CW + c3 * 512 + ncol],
                            waits=[toffb] + ([pw] if pw is not None else []),
                            inc="s_pe", start=True, stop=True)
                t_mul = p.emit("dve", "tensor_mul", waits=[t_nib, m1], inc="s_dve",
                               out=w_b[par][:, c3 * 512:c3 * 512 + ncol],
                               in0=nib_b[par][:, c3 * 512:c3 * 512 + ncol],
                               in1=scp[:, :ncol])
                psum_claim(ctx, c3, t_mul)
                wsub = [t_mul, m2] + ([prev[i - 2]["st"]] if i >= 2 else [])
                t_sub = p.emit("dve", "tensor_sub", waits=wsub, inc="s_dve",
                               out=wo_b[par][:, c3 * 512:c3 * 512 + ncol],
                               in0=w_b[par][:, c3 * 512:c3 * 512 + ncol],
                               in1=offp[:, :ncol])
                psum_claim(ctx, 3 + c3, t_sub)
            st = p.dma("pool",
                       out_slice_ap[:, :, (gu * nkd + k) * 128:(gu * nkd + k + 1) * 128]
                       .rearrange("a p m -> p a m"),
                       wo_b[par], "cst", waits=[t_sub])
            prev[i] = {"msk": t_msk, "nib": t_nib, "mul": t_mul, "st": st,
                       "sub": t_sub}
            st_toks[i] = st
            i += 1
    phase_war(p, ar, stores=("cst",))
    ar.reset(m0)
    return p.stream_all("cst")


def emit_dequant_wd(p, ctx, *, qw_ap, sc_full_ap, qz_full_ap, oneh_d, mask_ap,
                    pow2_ap, out_slice_ap, n_kl=11, n_groups=86, n_oc=8):
    ar = ctx["ar"]
    m0 = ar.mark()
    W = 4096
    qz_s = ar.alloc([n_groups, W // 8], U32)
    zq_s = ar.alloc([n_groups, W], F32)
    sc_s = ar.alloc([n_groups, W], F32)
    off_s = ar.alloc([n_groups, W], F32)
    scb_s = ar.alloc([n_groups, W], BF16)
    offb_s = ar.alloc([n_groups, W], BF16)
    qw_b = [ar.alloc([128, W], U32) for _ in range(2)]
    msk_b = [ar.alloc([128, W], U32) for _ in range(2)]
    nib_b = [b.bitcast(F32) for b in msk_b]
    w_b = [b.bitcast(F32) for b in qw_b]
    wo_b = [ar.alloc([128, W], BF16) for _ in range(2)]
    ps = ctx["psum"]

    ldz = p.dma("sp", qz_s, qz_full_ap, "misc", waits=ar.war)
    lds = p.dma("sp", sc_s, sc_full_ap, "misc", waits=ar.war)
    zq_u = zq_s.bitcast(U32)
    tz = None
    for j in range(8):
        tz = p.emit("dve", "tensor_scalar", waits=[ldz], inc="s_dve",
                    out=zq_u[:, j::8], in0=qz_s,
                    scalar1=4 * j, scalar2=15,
                    op0=ALU.logical_shift_right, op1=ALU.bitwise_and)
    tz1 = p.emit("dve", "tensor_scalar", waits=[tz], inc="s_dve",
                 out=zq_s, in0=zq_u, scalar1=1.0, scalar2=None, op0=ALU.add)
    toff = p.emit("dve", "tensor_mul", waits=[tz1, lds], inc="s_dve",
                  out=off_s, in0=zq_s, in1=sc_s)
    tscb = p.emit("dve", "tensor_copy", waits=[lds], inc="s_dve",
                  out=scb_s, in_=sc_s)
    toffb = p.emit("dve", "tensor_copy", waits=[toff], inc="s_dve",
                   out=offb_s, in_=off_s)

    prev = {}
    for kl in range(n_kl):
        par = kl % 2
        w_ld = list(ar.war) if kl < 2 else [prev[kl - 2]["sub"]]
        ldq = p.dma("sp", qw_b[par],
                    qw_ap[16 * kl:16 * (kl + 1), :]
                    .unsqueeze(1).broadcast_to([16, 8, W]),
                    "w", waits=w_ld)
        t_msk = p.emit("dve", "tensor_scalar",
                       waits=[ldq] + ([prev[kl - 2]["nib"]] if kl >= 2 else []),
                       inc="s_dve",
                       out=msk_b[par], in0=qw_b[par],
                       scalar1=mask_ap, scalar2=None, op0=ALU.bitwise_and)
        t_nib = p.emit("dve", "tensor_scalar",
                       waits=[t_msk] + ([prev[kl - 2]["mul"]] if kl >= 2 else []),
                       inc="s_dve",
                       out=nib_b[par], in0=msk_b[par],
                       scalar1=pow2_ap, scalar2=None, op0=ALU.mult)
        t_mul = None
        t_sub = None
        st = None
        for c in range(n_oc):
            scp = ps[c % 3]
            offp = ps[3 + c % 3]
            mm_w = [ctx["oneh_d_tok"], tscb]
            pw = psum_wait(ctx, c % 3)
            if pw is not None:
                mm_w.append(pw)
            m1 = p.emit("pe", "matmul", scp[:, :512], oneh_d[:, kl * 128:(kl + 1) * 128],
                        scb_s[:, c * 512:(c + 1) * 512],
                        waits=mm_w, inc="s_pe", start=True, stop=True)
            pw = psum_wait(ctx, 3 + c % 3)
            m2 = p.emit("pe", "matmul", offp[:, :512], oneh_d[:, kl * 128:(kl + 1) * 128],
                        offb_s[:, c * 512:(c + 1) * 512],
                        waits=[toffb] + ([pw] if pw is not None else []),
                        inc="s_pe", start=True, stop=True)
            t_mul = p.emit("dve", "tensor_mul", waits=[t_nib, m1], inc="s_dve",
                           out=w_b[par][:, c * 512:(c + 1) * 512],
                           in0=nib_b[par][:, c * 512:(c + 1) * 512],
                           in1=scp[:, :512])
            psum_claim(ctx, c % 3, t_mul)
            wsub = [t_mul, m2] + (prev[kl - 2]["st_all"] if kl >= 2 else [])
            t_sub = p.emit("dve", "tensor_sub", waits=wsub, inc="s_dve",
                           out=wo_b[par][:, c * 512:(c + 1) * 512],
                           in0=w_b[par][:, c * 512:(c + 1) * 512],
                           in1=offp[:, :512])
            psum_claim(ctx, 3 + c % 3, t_sub)
            st = p.dma("pool", out_slice_ap[c, kl],
                       wo_b[par][:, c * 512:(c + 1) * 512], "cst", waits=[t_sub])
        prev[kl] = {"msk": t_msk, "nib": t_nib, "mul": t_mul, "sub": t_sub,
                    "st_all": p.stream_all("cst")}
    phase_war(p, ar, stores=("cst",))
    ar.reset(m0)
    return p.stream_all("cst")


H_M, P_M, G_M, N_M = 128, 64, 8, 128
D_INNER = H_M * P_M
CONV_DIM = D_INNER + 2 * G_M * N_M
NMB = 145
NMB_PAD = 152
NCHUNK = 8


def emit_inproj(p, ctx, mc, *, cache_ap, xnT, x_ready_toks, hook=None):
    ar = ctx["ar"]
    m0 = ar.mark()
    wbufs = [ar.alloc([128, NKD * 128], BF16) for _ in range(2)]
    ev_f = [ar.alloc([128, TOK], F32) for _ in range(2)]
    ev_b = [ar.alloc([128, TOK], BF16) for _ in range(2)]
    stage = ar.alloc([128, 8 * TOK], BF16)
    ps = ctx["psum"]

    order = list(range(64, 144)) + [144] + list(range(0, 64))
    pe_done, ev_done = {}, {}
    evb_reader = {}
    evf_reader = {}
    last_flush = []
    ng = 0
    for idx, mb in enumerate(order):
        par = idx % 2
        waits = list(ar.war) if idx < 2 else [("s_pe", pe_done[idx - 2])]
        wtok = p.dma("sp", wbufs[par][:, :], cache_ap[mb], "w", waits=waits)
        pe_waits = [wtok] + list(x_ready_toks)
        if idx >= 2:
            pe_waits.append(ev_done[idx - 2])
        groups = []
        for h in range(2):
            bank = par * 2 + h
            psb = ps[bank]
            w0 = list(pe_waits)
            pw = psum_wait(ctx, bank)
            if pw is not None:
                w0.append(pw)
            for k in range(NKD):
                t = p.emit("pe", "matmul", psb[:, :512],
                           wbufs[par][:, k * 128:(k + 1) * 128],
                           xnT[:, k * TOK + h * 512: k * TOK + h * 512 + 512],
                           waits=w0 if k == 0 else (),
                           inc=("s_pe" if k == NKD - 1 else None),
                           start=(k == 0), stop=(k == NKD - 1))
            groups.append(t)
        pe_done[idx] = groups[-1][1]
        war_b = [evb_reader[idx - 2]] if idx - 2 in evb_reader else []
        war_f = [evf_reader[idx - 2]] if idx - 2 in evf_reader else []

        if 64 <= mb < 144:
            ct = mb - 64
            evs = []
            for h in range(2):
                bank = par * 2 + h
                ev = p.emit("act", "activation", ev_b[par][:, h * 512:(h + 1) * 512],
                            ps[bank][:, :512], AF.Copy,
                            waits=[groups[h]] + war_b, inc="s_act")
                psum_claim(ctx, bank, ev)
                evs.append(ev)
            ev_done[idx] = evs[-1]
            hb = p.emit("dve", "tensor_copy", waits=[evs[1]], inc="s_dve",
                        out=mc["halo_out"][:, ct * 3:(ct + 1) * 3],
                        in_=ev_b[par][:, 1021:1024])
            mc.setdefault("halo_toks", []).append(hb)
            st = p.dma("pool", mc["hbc_spill"][ct], ev_b[par][:, :], "hout",
                       waits=[evs[1], hb])
            evb_reader[idx] = st
        elif mb == 144:
            evs = []
            for h in range(2):
                bank = par * 2 + h
                ev = p.emit("act", "activation", mc["dt_raw"][:, h * 512:(h + 1) * 512],
                            ps[bank][:, :512], AF.Copy,
                            waits=[groups[h]], inc="s_act")
                psum_claim(ctx, bank, ev)
                evs.append(ev)
            mc["dt_raw_tok"] = evs[-1]
            ev_done[idx] = evs[-1]
            if idx - 2 in evb_reader:
                evb_reader[idx] = evb_reader[idx - 2]
            if idx - 2 in evf_reader:
                evf_reader[idx] = evf_reader[idx - 2]
        else:
            evs = []
            for h in range(2):
                bank = par * 2 + h
                ev = p.emit("act", "activation", ev_f[par][:, h * 512:(h + 1) * 512],
                            ps[bank][:, :512], AF.Silu,
                            waits=[groups[h]] + war_f, inc="s_act")
                psum_claim(ctx, bank, ev)
                evs.append(ev)
            ev_done[idx] = evs[-1]
            tg = p.emit("dve", "tensor_copy",
                        waits=evs + war_b, inc="s_dve",
                        out=ev_b[par][:, :], in_=ev_f[par][:, :])
            evf_reader[idx] = tg
            col = (ng % 8) * 128
            trs = {}
            for i in range(NCHUNK):
                bank = 4 + (i % 4)
                pb = ps[bank].bitcast(BF16)
                pw = psum_wait(ctx, bank)
                tr = p.emit("pe", "transpose",
                            pb[:, (i // 4) * 128:(i // 4) * 128 + 128],
                            ev_b[par][:, i * 128:(i + 1) * 128], mc["ident_bf"],
                            waits=[tg] + ([pw] if pw else []),
                            inc=("s_pe" if i % 4 == 3 else None))
                if i % 4 == 3:
                    trs[i // 4] = tr
            evb_reader[idx] = trs[1]
            evt = None
            for half in range(2):
                for i in range(4):
                    chunk = half * 4 + i
                    bank = 4 + (chunk % 4)
                    pb = ps[bank].bitcast(BF16)
                    flw = list(last_flush)
                    evt = p.emit("dve", "tensor_copy", waits=[trs[half]] + flw,
                                 inc="s_dve",
                                 out=stage[:, chunk * TOK + col: chunk * TOK + col + 128],
                                 in_=pb[:, half * 128: half * 128 + 128])
                    psum_claim(ctx, bank, evt)
            ng += 1
            if ng % 8 == 0:
                base = (ng // 8 - 1) * 1024
                for i in range(NCHUNK):
                    p.dma("pool",
                          mc["sgate_tok_spill"][i][:, base:base + 1024],
                          stage[:, i * TOK:(i + 1) * TOK], "hout",
                          waits=[evt])
                last_flush = p.stream_all("hout")
        if hook is not None:
            hook(idx)
    phase_war(p, ar, stores=("hout",))
    ar.reset(m0)
    return ("s_pe", pe_done[len(order) - 1])


def make_conv_stepper(p, ctx, mc):
    ar = ctx["ar"]
    ps = ctx["psum"]
    st = {"ct": 0, "prev": {}, "last_flush": [], "hb_rdy": None,
          "hflag": None}
    xcat = [ar.alloc([128, 1027 + 13], F32) for _ in range(2)]
    ones_t = ar.alloc([128, 128], F32)
    mc["ones_t"] = ones_t
    t_ot = p.emit("dve", "memset", ones_t, 1.0, waits=ar.war, inc="s_dve")
    taps = [ar.alloc([128, TOK], F32) for _ in range(2)]
    hs_b = [ar.alloc([128, TOK], BF16) for _ in range(2)]
    stage = ar.alloc([128, 8 * TOK], BF16)
    halo_in = ar.alloc([128, 240], F32)
    cw_ld = p.dma("sp", mc["convw"], mc["convw_d"], "misc", waits=ar.war)
    cb_ld = p.dma("sp", mc["convb"], mc["convb_d"], "misc", waits=ar.war)
    t_ca = p.emit("act", "activation", mc["cA"], mc["A_log"][:, 0:1], AF.Exp,
                  waits=ar.war, inc="s_act")
    st["t_ca"] = p.emit("dve", "tensor_scalar", waits=[t_ca], inc="s_dve",
                        out=mc["cA"], in0=mc["cA"], scalar1=-1.0, scalar2=None,
                        op0=ALU.mult)

    def emit_halo():
        b1 = p.dma("pool", mc["halo_bounce_d"], mc["halo_out"], "pb",
                   waits=list(mc["halo_toks"]))
        ag = p.raw("pool", lambda g: g.collective_compute(
            "AllGather", ALU.bypass,
            replica_groups=[list(range(mc["ncores"]))],
            ins=[mc["halo_bounce_d"].opt()], outs=[mc["halo_all_d"].opt()]),
            waits=[b1], inc="s_cc")
        ncr = mc["ncores"]

        def rd(g):
            prev = (g.partition_id() + (ncr - 1)) % ncr
            return g.dma_start(out=halo_in,
                               in_=mc["halo_all_d"][bass.ts(prev, 128), :])
        h2 = p.raw("pool", rd, waits=[ag], inc="hin0", inc_n=16)
        st["hflag"] = p.emit("dve", "tensor_scalar", waits=[h2], inc="s_dve",
                             out=halo_in, in0=halo_in,
                             scalar1=mc["flag"][:, 0:1],
                             scalar2=None, op0=ALU.mult)
        st["hb_rdy"] = p.stream_all("hout")

    def emit_dt():
        dtr = mc["dt_raw"]
        t_a = p.emit("act", "activation", mc["sp_a"], dtr, AF.Abs,
                     bias=mc["dt_bias"][:, 0:1], waits=[mc["dt_raw_tok"]],
                     inc="s_act")
        t_e = p.emit("act", "activation", mc["sp_a"], mc["sp_a"], AF.Exp,
                     scale=-1.0, waits=[t_a], inc="s_act")
        t_l = p.emit("act", "activation", mc["sp_a"], mc["sp_a"], AF.Ln,
                     bias=1.0, waits=[t_e], inc="s_act")
        t_m = p.emit("dve", "tensor_scalar", waits=[mc["dt_raw_tok"]],
                     inc="s_dve", out=mc["sp_m"], in0=dtr,
                     scalar1=mc["dt_bias"][:, 0:1],
                     scalar2=0.0, op0=ALU.add, op1=ALU.max)
        t_dt = p.emit("dve", "tensor_add", waits=[t_l, t_m], inc="s_dve",
                      out=mc["dt"], in0=mc["sp_a"], in1=mc["sp_m"])
        mc["dt_tok"] = t_dt
        t_af = p.emit("dve", "tensor_scalar", waits=[t_dt, st["t_ca"]],
                      inc="s_dve",
                      out=mc["A_full"], in0=mc["dt"],
                      scalar1=mc["cA"][:, 0:1],
                      scalar2=None, op0=ALU.mult)
        t_cs = None
        for i in range(NCHUNK):
            t_cs = p.emit("dve", "tensor_tensor_scan", waits=[t_af, t_ot],
                          inc="s_dve",
                          out=mc["A_cs"][:, i * 128:(i + 1) * 128],
                          data0=ones_t[:, :128],
                          data1=mc["A_full"][:, i * 128:(i + 1) * 128],
                          initial=0.0, op0=ALU.mult, op1=ALU.add)
        mc["A_cs_tok"] = t_cs

    def emit_ct():
        ct = st["ct"]
        if ct >= 80:
            return
        st["ct"] = ct + 1
        prev = st["prev"]
        par = ct % 2
        eng = "dve"
        w_ld = [prev[ct - 2]["tap3"]] if ct - 2 in prev else list(ar.war)
        ld = p.dma("pool", xcat[par][:, 3:1027], mc["hbc_spill"][ct], "cx",
                   waits=w_ld + st["hb_rdy"])
        hl = p.emit(eng, "tensor_copy", waits=[st["hflag"], ld] +
                    ([prev[ct - 2]["tap3"]] if ct - 2 in prev else []),
                    inc="s_" + eng,
                    out=xcat[par][:, 0:3], in_=halo_in[:, ct * 3:(ct + 1) * 3])
        c4 = ct * 4
        t0 = p.emit(eng, "tensor_scalar", waits=[ld, cw_ld, hl],
                    inc="s_" + eng,
                    out=taps[par], in0=xcat[par][:, 0:1024],
                    scalar1=mc["convw"][:, c4:c4 + 1], scalar2=None,
                    op0=ALU.mult)
        acc = t0
        for k in range(1, 4):
            acc = p.emit(eng, "scalar_tensor_tensor", waits=[acc, hl],
                         inc="s_" + eng,
                         out=taps[par], in0=xcat[par][:, k:k + 1024],
                         scalar=mc["convw"][:, c4 + k:c4 + k + 1],
                         in1=taps[par], op0=ALU.mult, op1=ALU.add)
        prev[ct] = {"tap3": acc}
        if ct >= 64:
            dst = (mc["BT"] if ct < 72 else mc["CT"])
            gi = (ct - 64) % 8
            t_si = p.emit("act", "activation", dst[:, gi * TOK:(gi + 1) * TOK],
                          taps[par], AF.Silu, bias=mc["convb"][:, ct:ct + 1],
                          waits=[acc, cb_ld], inc="s_act")
            mc.setdefault("bc_toks", []).append(t_si)
            prev[ct]["tap3"] = t_si
        else:
            war_h = ([prev[ct - 2]["hsb"]]
                     if (ct - 2 in prev and "hsb" in prev[ct - 2]) else [])
            t_si = p.emit("act", "activation", hs_b[par], taps[par], AF.Silu,
                          bias=mc["convb"][:, ct:ct + 1],
                          waits=[acc, cb_ld] + war_h, inc="s_act")
            prev[ct]["tap3"] = t_si
            col = (ct % 8) * 128
            trs = {}
            for i in range(NCHUNK):
                bank = 4 + (i % 4)
                pb = ps[bank].bitcast(BF16)
                pw = psum_wait(ctx, bank)
                tr = p.emit("pe", "transpose",
                            pb[:, (i // 4) * 128:(i // 4) * 128 + 128],
                            hs_b[par][:, i * 128:(i + 1) * 128],
                            mc["ident_bf"],
                            waits=[t_si] + ([pw] if pw else []),
                            inc=("s_pe" if i % 4 == 3 else None))
                if i % 4 == 3:
                    trs[i // 4] = tr
            prev[ct]["hsb"] = trs[1]
            evt = None
            for half in range(2):
                for i in range(4):
                    chunk = half * 4 + i
                    bank = 4 + (chunk % 4)
                    pb = ps[bank].bitcast(BF16)
                    evt = p.emit("dve", "tensor_copy",
                                 waits=[trs[half]] + list(st["last_flush"]),
                                 inc="s_dve",
                                 out=stage[:, chunk * TOK + col:
                                           chunk * TOK + col + 128],
                                 in_=pb[:, half * 128:half * 128 + 128])
                    psum_claim(ctx, bank, evt)
            if ct % 8 == 7:
                base = (ct // 8) * 1024
                for i in range(NCHUNK):
                    p.dma("pool", mc["hid_tok_spill"][i][:, base:base + 1024],
                          stage[:, i * TOK:(i + 1) * TOK], "hout", waits=[evt])
                st["last_flush"] = p.stream_all("hout")

    def on_idx(idx):
        if idx == 79:
            emit_halo()
        elif idx == 80:
            emit_dt()
        elif idx >= 81:
            emit_ct()
            emit_ct()

    def finish():
        while st["ct"] < 80:
            emit_ct()
        mc["hid_spill_toks"] = p.stream_all("hout")
    return on_idx, finish


def emit_mamba_scan(p, ctx, mc):
    ar = ctx["ar"]
    m0 = ar.mark()
    ps = ctx["psum"]
    R = mc["R"]
    Rb = ar.alloc([128, 8192], BF16)
    dp_b = ar.alloc([128, 8192], BF16)
    hid_b = ar.alloc([128, 8192], BF16)
    hc = ar.alloc([128, 8192], BF16)
    hten = hc
    ysc = ar.alloc([128, 8192], BF16)
    abuf = [ar.alloc([1, 2048], F32) for _ in range(2)]
    m1row = ar.alloc([1, 128], F32)
    dtT = ar.alloc([128, 128], BF16)
    acsT = ar.alloc([128, 128], F32)
    EcsT = ar.alloc([128, 128], F32)
    dec_d = ar.alloc([128, 128], F32)
    dec_t = ar.alloc([128, 128], BF16)
    el_b = ar.alloc([128, 128], F32)
    el_row = ar.alloc([1, 128], F32)
    B_tok = ar.alloc([128, 1024], BF16)
    Mexp = ar.alloc([128, 512], BF16)
    Mq = ar.alloc([128, 512], BF16)
    Mq2 = ar.alloc([128, 512], BF16)
    cbt = [ar.alloc([128, 128], BF16) for _ in range(2)]
    tq = ar.alloc([128, 256], F32)
    ones1 = ar.alloc([1, 128], F32)

    t_on = p.emit("dve", "memset", ones1, 1.0, waits=ar.war, inc="s_dve")
    t_r0 = p.emit("dve", "memset", R, 0.0, waits=ar.war, inc="s_dve")
    msLR = [p.emit("dve", "memset", m1row, -1.0, waits=ar.war, inc="s_dve")]
    dp_row = ar.alloc([1, 512], F32)
    t_dp = None
    mprev = {}
    for c in range(16):
        half = 0
        dl = p.dma("sp", dp_row[0:1, half * 512:(half + 1) * 512],
                   mc["dp_row_d"][0:1, c * 512:(c + 1) * 512], "misc",
                   waits=(ar.war if c < 1 else [mprev[c - 1]]))
        pw = psum_wait(ctx, half)
        mmw = [dl, t_on] + ([pw] if pw else [])
        m = p.emit("pe", "matmul", ps[half][:, :512], ones1,
                   dp_row[0:1, half * 512:(half + 1) * 512],
                   waits=mmw, inc="s_pe", start=True, stop=True)
        mprev[c] = m
        t_dp = p.emit("dve", "tensor_copy", waits=[m], inc="s_dve",
                      out=dp_b[:, c * 512:(c + 1) * 512], in_=ps[half][:, :512])
        psum_claim(ctx, half, t_dp)
    mc["dp_tok"] = t_dp

    info = {}
    for i in range(NCHUNK):
        hw = list(mc["hid_spill_toks"]) if i == 0 else [info[i - 1]["hid_rd"]]
        ld_h = p.dma("sp", hid_b, mc["hid_tok_spill"][i], "w", waits=hw)
        b_acs = p.dma("pool", mc["acs_d"], mc["A_cs"][:, i * 128:(i + 1) * 128],
                      "hin", waits=[mc["A_cs_tok"]] + p.stream_all("hin"))
        r_al = p.dma("pool", el_row, mc["acs_d"][:, 127:128], "hin",
                     waits=[b_acs] + ([info[i - 1]["el_rd"]] if i >= 1 else []),
                     allow_slow_non_contiguous=True)
        t_el = p.emit("act", "activation", el_row, el_row, AF.Exp,
                      waits=[r_al], inc="s_act")
        pw6a = psum_wait(ctx, 6)
        m_al = p.emit("pe", "matmul", ps[6][:, 256:384], ones1, el_row,
                      waits=[t_el, t_on] + ([pw6a] if pw6a else []), inc="s_pe",
                      start=True, stop=True)
        t_elb = p.emit("dve", "tensor_copy", waits=[m_al] +
                       ([info[i - 1]["elb_rd"]] if i >= 1 else []), inc="s_dve",
                       out=el_b, in_=ps[6][:, 256:384])
        r_al2 = p.dma("pool", el_row, mc["acs_d"][:, 127:128], "hin",
                      waits=[m_al], allow_slow_non_contiguous=True)
        m_al2 = p.emit("pe", "matmul", ps[6][:, 384:512], ones1, el_row,
                       waits=[r_al2, t_on], inc="s_pe", start=True, stop=True)
        pw6 = psum_wait(ctx, 6)
        tr_dt = p.emit("pe", "transpose", ps[6][:, :128],
                       mc["dt"][:, i * 128:(i + 1) * 128], mc["ident_f"],
                       waits=[mc["dt_tok"]] + ([pw6] if pw6 else []), inc="s_pe")
        ev_dt = p.emit("dve", "tensor_copy",
                       waits=[tr_dt] + ([info[i - 1]["dtT_rd"]] if i >= 1 else []),
                       inc="s_dve", out=dtT, in_=ps[6][:, :128])
        tr_ac = p.emit("pe", "transpose", ps[6][:, 128:256],
                       mc["A_cs"][:, i * 128:(i + 1) * 128], mc["ident_f"],
                       waits=[mc["A_cs_tok"], ev_dt] +
                       ([info[i - 1]["acsT_rd"]] if i >= 1 else []), inc="s_pe")
        ev_ac = p.emit("dve", "tensor_copy", waits=[tr_ac], inc="s_dve",
                       out=acsT, in_=ps[6][:, 128:256])
        psum_claim(ctx, 6, ev_ac)
        t_dd = p.emit("dve", "tensor_sub", waits=[m_al2, ev_ac] +
                      ([info[i - 1]["decd_rd"]] if i >= 1 else []), inc="s_dve",
                      out=dec_d, in0=ps[6][:, 384:512], in1=acsT)
        psum_claim(ctx, 6, t_dd)
        t_dect = p.emit("act", "activation", dec_t, dec_d, AF.Exp,
                        waits=[t_dd] + ([info[i - 1]["dect_rd"]] if i >= 1 else []),
                        inc="s_act")
        t_ecs = p.emit("act", "activation", EcsT, acsT, AF.Exp,
                       waits=[ev_ac] + ([info[i - 1]["ecs_rd"]] if i >= 1 else []),
                       inc="s_act")
        dtb = dtT.rearrange("p (h o) -> p h o", o=1).broadcast_to([128, 128, 64])
        t_hc = p.emit("dve", "tensor_tensor", waits=[ld_h, ev_dt] +
                      ([info[i - 1]["hc_rd"]] if i >= 1 else []), inc="s_dve",
                      out=hc, in0=hid_b, in1=dtb, op=ALU.mult)
        t_rb = p.emit("dve", "tensor_copy",
                      waits=[t_r0 if i == 0 else info[i - 1]["r_done"]],
                      inc="s_dve", out=Rb, in_=R)
        ev_bt = None
        for g in range(8):
            pw7 = psum_wait(ctx, 7)
            tr = p.emit("pe", "transpose",
                        ps[7].bitcast(BF16)[:, (g % 4) * 128:(g % 4) * 128 + 128],
                        mc["BT"][:, g * TOK + i * 128: g * TOK + i * 128 + 128],
                        mc["ident_bf"],
                        waits=mc["bc_toks"] + ([pw7] if pw7 else []), inc="s_pe")
            ev_bt = p.emit("dve", "tensor_copy", waits=[tr] +
                           ([info[i - 1]["btok_rd"]] if i >= 1 and g == 0 else []),
                           inc="s_dve",
                           out=B_tok[:, g * 128:(g + 1) * 128],
                           in_=ps[7].bitcast(BF16)[:, (g % 4) * 128:(g % 4) * 128 + 128])
            psum_claim(ctx, 7, ev_bt)
        t_dres = p.emit("dve", "tensor_tensor", waits=[ld_h, mc["dp_tok"]] +
                        ([info[i - 1]["ysc_st"]] if i >= 1 else []), inc="s_dve",
                        out=ysc, in0=hid_b, in1=dp_b, op=ALU.mult)

        r_gl = {}
        md_of = dict(mc.get("scan_md_hist", {}))

        def issue_buflr(qg):
            par = qg % 2
            key = (i, qg - 2) if qg >= 2 else (i - 1, qg + 6)
            w_extra = [md_of[key]] if key in md_of else list(msLR)
            r_gl[qg] = [
                p.dma("pool", abuf[par],
                      mc["acs_d"][16 * qg:16 * (qg + 1), :], "hin",
                      waits=[b_acs] + w_extra),
                t_on,
            ]
        issue_buflr(0)
        t_cb = {}
        my = None
        ysc_tt = t_dres
        mqs = [Mq, Mq2]
        tmq_of = {}
        for q in range(33):
            if q < 32:
                g = q // 4
                qg, qi = q // 4, q % 4
                par = qg % 2
                bdq = q % 2
                if qi == 0:
                    if qg + 1 < 8:
                        issue_buflr(qg + 1)
                    pw3 = psum_wait(ctx, 3)
                    m_cb = p.emit("pe", "matmul", ps[3][:, (qg % 2) * 128:
                                                        (qg % 2) * 128 + 128],
                                  mc["BT"][:, g * TOK + i * 128:
                                           g * TOK + i * 128 + 128],
                                  mc["CT"][:, g * TOK + i * 128:
                                           g * TOK + i * 128 + 128],
                                  waits=mc["bc_toks"] + ([pw3] if pw3 else []),
                                  inc="s_pe", start=True, stop=True)
                    t_cb[qg] = p.emit("dve", "tensor_tensor",
                                      waits=[m_cb, mc["utri_tok"]],
                                      inc="s_dve", out=cbt[qg % 2],
                                      in0=ps[3][:, (qg % 2) * 128:
                                                (qg % 2) * 128 + 128],
                                      in1=mc["utri"], op=ALU.mult)
                    psum_claim(ctx, 3, t_cb[qg])
                pw0 = psum_wait(ctx, bdq)
                md = None
                for j in range(4):
                    col = (qi * 4 + j) * 128
                    p.emit("pe", "matmul",
                           ps[bdq][:, j * 128:(j + 1) * 128],
                           ones1, abuf[par][0:1, col:col + 128],
                           waits=r_gl[qg] + ([pw0] if pw0 and j == 0
                                             else []),
                           inc=None, start=True, stop=False)
                    md = p.emit("pe", "matmul",
                                ps[bdq][:, j * 128:(j + 1) * 128],
                                abuf[par][0:1, col:col + 128], m1row,
                                waits=(), inc=("s_pe" if j == 3 else None),
                                start=False, stop=True)
                if qi == 3:
                    md_of[(i, qg)] = md
                t_cl = p.emit("dve", "tensor_scalar", waits=[md], inc="s_dve",
                              out=ps[bdq][:, :512], in0=ps[bdq][:, :512],
                              scalar1=0.0, scalar2=None, op0=ALU.min)
                t_me = p.emit("act", "activation", Mexp, ps[bdq][:, :512],
                              AF.Exp, waits=[t_cl], inc="s_act")
                psum_claim(ctx, bdq, t_me)
                cbb = cbt[qg % 2].unsqueeze(1).broadcast_to([128, 4, 128])
                tmq_of[q] = p.emit("dve", "tensor_tensor",
                                   waits=[t_me, t_cb[qg]], inc="s_dve",
                                   out=mqs[q % 2], in0=Mexp, in1=cbb,
                                   op=ALU.mult)
            if q >= 1:
                qq = q - 1
                gq = qq // 4
                pw1 = psum_wait(ctx, 2)
                for j in range(4):
                    h = 4 * qq + j
                    p.emit("pe", "matmul", ps[2][:, j * 64:(j + 1) * 64],
                           mqs[qq % 2][:, j * 128:(j + 1) * 128],
                           hc[:, h * 64:(h + 1) * 64],
                           waits=[tmq_of[qq], t_hc] +
                           ([pw1] if pw1 and j == 0 else []),
                           inc=None, start=True, stop=True)
                mo = p.emit("pe", "matmul", ps[2][:, 256:512],
                            mc["CT"][:, gq * TOK + i * 128:
                                     gq * TOK + i * 128 + 128],
                            Rb[:, qq * 256:(qq + 1) * 256],
                            waits=[t_rb] + mc["bc_toks"], inc="s_pe",
                            start=True, stop=True)
                my = mo
                ecsb = EcsT[:, 4 * qq:4 * qq + 4]\
                    .rearrange("p (h o) -> p h o", o=1)\
                    .broadcast_to([128, 4, 64])
                t_t1 = p.emit("dve", "tensor_tensor",
                              waits=[mo, t_ecs, ysc_tt] +
                              ([info[i - 1]["tq_rd"]] if i >= 1 and qq == 0
                               else []),
                              inc="s_dve",
                              out=tq, in0=ps[2][:, 256:512], in1=ecsb,
                              op=ALU.mult)
                t_t2 = p.emit("dve", "tensor_add", waits=[t_t1], inc="s_dve",
                              out=tq, in0=tq, in1=ps[2][:, 0:256])
                psum_claim(ctx, 2, t_t2)
                ysc_tt = p.emit("dve", "tensor_add", waits=[t_t2, t_dres],
                                inc="s_dve",
                                out=ysc[:, qq * 256:(qq + 1) * 256],
                                in0=ysc[:, qq * 256:(qq + 1) * 256], in1=tq)
        mc["scan_md_hist"] = {k: v for k, v in md_of.items() if k[0] == i}
        diff_last = my
        t_ht = p.emit("dve", "tensor_tensor",
                      waits=[t_hc, t_dect, ("s_pe", p.counts["s_pe"])],
                      inc="s_dve",
                      out=hten, in0=hc,
                      in1=dec_t.rearrange("p (h o) -> p h o", o=1)
                      .broadcast_to([128, 128, 64]), op=ALU.mult)
        r_done = None
        ms2 = None
        for g in range(8):
            pw4 = psum_wait(ctx, 4)
            pw5 = psum_wait(ctx, 5)
            ms1 = p.emit("pe", "matmul", ps[4][:, :512],
                         B_tok[:, g * 128:(g + 1) * 128],
                         hten[:, g * 1024:g * 1024 + 512],
                         waits=[ev_bt, t_ht] + ([pw4] if pw4 else []),
                         inc="s_pe", start=True, stop=True)
            ms2 = p.emit("pe", "matmul", ps[5][:, :512],
                         B_tok[:, g * 128:(g + 1) * 128],
                         hten[:, g * 1024 + 512:(g + 1) * 1024],
                         waits=[ev_bt, t_ht] + ([pw5] if pw5 else []),
                         inc="s_pe", start=True, stop=True)
            elb = el_b[:, 16 * g:16 * g + 16].rearrange("p (h o) -> p h o", o=1)\
                .broadcast_to([128, 16, 64])
            t_r1 = p.emit("dve", "tensor_tensor",
                          waits=[t_elb, t_rb] + ([r_done] if r_done else []),
                          inc="s_dve",
                          out=R[:, g * 1024:(g + 1) * 1024],
                          in0=R[:, g * 1024:(g + 1) * 1024], in1=elb, op=ALU.mult)
            t_r2 = p.emit("dve", "tensor_add", waits=[t_r1, ms1], inc="s_dve",
                          out=R[:, g * 1024:g * 1024 + 512],
                          in0=R[:, g * 1024:g * 1024 + 512], in1=ps[4][:, :512])
            t_r3 = p.emit("dve", "tensor_add", waits=[t_r1, ms2], inc="s_dve",
                          out=R[:, g * 1024 + 512:(g + 1) * 1024],
                          in0=R[:, g * 1024 + 512:(g + 1) * 1024], in1=ps[5][:, :512])
            psum_claim(ctx, 4, t_r2)
            psum_claim(ctx, 5, t_r3)
            r_done = t_r3
        st = p.dma("pool", mc["ysc_spill"][i], ysc, "yst", waits=[ysc_tt])
        info[i] = {"hid_rd": t_ht, "el_rd": m_al2, "elb_rd": r_done,
                   "dtT_rd": t_hc, "acsT_rd": t_ecs, "decd_rd": t_dect,
                   "dect_rd": t_ht, "ecs_rd": t_t1, "hc_rd": t_ht,
                   "btok_rd": ms2, "tq_rd": ysc_tt, "ysc_st": st,
                   "r_done": r_done, "diff_last": diff_last}
    mc["r_final_tok"] = r_done
    mc["ysc_st_all"] = p.stream_all("yst")
    phase_war(p, ar, stores=("yst",))
    ar.reset(m0)


def emit_mamba_finish(p, ctx, mc):
    ar = ctx["ar"]
    m0 = ar.mark()
    ps = ctx["psum"]
    R = mc["R"]
    S0G = ar.alloc([128, 8192], BF16)
    S0b = ar.alloc([128, 8192], BF16)
    ysb = ar.alloc([128, 8192], BF16)
    sg_b = ar.alloc([128, 8192], BF16)
    yfin = ar.alloc([128, 8192], BF16)
    sq8 = ar.alloc([128, 8192], BF16)
    acsT = ar.alloc([128, 128], F32)
    EcsT = ar.alloc([128, 128], F32)
    g_row = ar.alloc([1, 128], F32)
    el_row = ar.alloc([1, 128], F32)
    tq = ar.alloc([128, 256], F32)
    gsum = ar.alloc([128, 16], F32)
    ones1 = ar.alloc([1, 128], F32)

    t_on = p.emit("dve", "memset", ones1, 1.0, waits=ar.war, inc="s_dve")
    b1 = p.dma("pool", mc["r_bounce_d"], R, "pb", waits=[mc["r_final_tok"]])
    ag = p.raw("pool", lambda g: g.collective_compute(
        "AllGather", ALU.bypass, replica_groups=[list(range(mc["ncores"]))],
        ins=[mc["r_bounce_d"].opt()], outs=[mc["r_all_d"].opt()]),
        waits=[b1], inc="s_cc")
    ncr = mc["ncores"]
    def rd(g):
        prev = (g.partition_id() + (ncr - 1)) % ncr
        return g.dma_start(out=R, in_=mc["r_all_d"][bass.ts(prev, 128), :])
    t_rdR = p.raw("pool", rd, waits=[ag], inc="hin0", inc_n=16)
    t_s0 = p.emit("dve", "tensor_scalar", waits=[t_rdR, mc["flag_tok"]],
                  inc="s_dve", out=S0b, in0=R,
                  scalar1=mc["flag"][:, 0:1], scalar2=None, op0=ALU.mult)
    t_g = p.emit("dve", "memset", g_row, 1.0, waits=ar.war, inc="s_dve")

    info = {}
    for i in range(NCHUNK):
        yw = list(mc["ysc_st_all"]) if i == 0 else [info[i - 1]["y_rd"]]
        ld_y = p.dma("sp", ysb, mc["ysc_spill"][i], "w", waits=yw)
        gw = list(ar.war) if i == 0 else [info[i - 1]["g_rd"]]
        ld_g = p.dma("sp", sg_b, mc["sgate_tok_spill"][i], "x", waits=gw)
        b_acs = p.dma("pool", mc["acs_d"], mc["A_cs"][:, i * 128:(i + 1) * 128],
                      "hin", waits=[mc["A_cs_tok"]] + p.stream_all("hin"))
        pw6 = psum_wait(ctx, 6)
        tr_ac = p.emit("pe", "transpose", ps[6][:, :128],
                       mc["A_cs"][:, i * 128:(i + 1) * 128], mc["ident_f"],
                       waits=[mc["A_cs_tok"]] + ([pw6] if pw6 else []), inc="s_pe")
        ev_ac = p.emit("dve", "tensor_copy", waits=[tr_ac] +
                       ([info[i - 1]["acsT_rd"]] if i >= 1 else []), inc="s_dve",
                       out=acsT, in_=ps[6][:, :128])
        psum_claim(ctx, 6, ev_ac)
        t_ecs = p.emit("act", "activation", EcsT, acsT, AF.Exp,
                       waits=[ev_ac] + ([info[i - 1]["ecs_rd"]] if i >= 1 else []),
                       inc="s_act")
        pw3 = psum_wait(ctx, 3)
        m_g = p.emit("pe", "matmul", ps[3][:, :128], ones1, g_row,
                     waits=[t_g, t_on] + ([pw3] if pw3 else []), inc="s_pe",
                     start=True, stop=True)
        gb = ps[3][:, :128].rearrange("p (h o) -> p h o", o=1).broadcast_to([128, 128, 64])
        t_s0g = p.emit("dve", "tensor_tensor", waits=[t_s0, m_g] +
                       ([info[i - 1]["s0g_rd"]] if i >= 1 else []), inc="s_dve",
                       out=S0G, in0=S0b, in1=gb, op=ALU.mult)
        psum_claim(ctx, 3, t_s0g)
        r3 = p.dma("pool", el_row, mc["acs_d"][:, 127:128], "hin",
                   waits=[b_acs, m_g] + ([info[i - 1]["el2_rd"]] if i >= 1 else []),
                   allow_slow_non_contiguous=True)
        t_el2 = p.emit("act", "activation", el_row, el_row, AF.Exp,
                       waits=[r3], inc="s_act")
        t_g = p.emit("dve", "tensor_mul", waits=[t_g, t_el2, t_s0g], inc="s_dve",
                     out=g_row, in0=g_row, in1=el_row)
        t_ys = None
        mi = None
        for q in range(32):
            g = q // 4
            pw1 = psum_wait(ctx, 1)
            mi = p.emit("pe", "matmul", ps[1][:, :256],
                        mc["CT"][:, g * TOK + i * 128: g * TOK + i * 128 + 128],
                        S0G[:, q * 256:(q + 1) * 256],
                        waits=[t_s0g] + mc["bc_toks"] + ([pw1] if pw1 else []),
                        inc="s_pe", start=True, stop=True)
            ecsb = EcsT[:, 4 * q:4 * q + 4].rearrange("p (h o) -> p h o", o=1)\
                .broadcast_to([128, 4, 64])
            t_t1 = p.emit("dve", "tensor_tensor", waits=[mi, t_ecs] +
                          ([t_ys] if t_ys is not None else []) +
                          ([info[i - 1]["tq_rd"]] if i >= 1 and q == 0 else []),
                          inc="s_dve",
                          out=tq, in0=ps[1][:, :256], in1=ecsb, op=ALU.mult)
            psum_claim(ctx, 1, t_t1)
            t_ys = p.emit("dve", "tensor_add", waits=[t_t1, ld_y], inc="s_dve",
                          out=ysb[:, q * 256:(q + 1) * 256],
                          in0=ysb[:, q * 256:(q + 1) * 256], in1=tq)
        t_yg = p.emit("dve", "tensor_mul", waits=[t_ys, ld_g], inc="s_dve",
                      out=ysb, in0=ysb, in1=sg_b)
        t_sq = p.emit("act", "activation", sq8, ysb, AF.Square,
                      waits=[t_yg] + ([info[i - 1]["sq_rd"]] if i >= 1 else []),
                      inc="s_act")
        t_gs = None
        for g in range(8):
            t_gs = p.emit("dve", "tensor_reduce", gsum[:, g:g + 1],
                          sq8[:, g * 1024:(g + 1) * 1024], AX.X, ALU.add,
                          waits=[t_sq], inc="s_dve")
        t_pre = p.emit("dve", "tensor_scalar", waits=[t_gs], inc="s_dve",
                       out=gsum[:, 8:16], in0=gsum[:, 0:8], scalar1=1.0 / 1024,
                       scalar2=EPS, op0=ALU.mult, op1=ALU.add)
        t_ln = p.emit("act", "activation", gsum[:, 8:16], gsum[:, 8:16], AF.Ln,
                      waits=[t_pre], inc="s_act")
        t_rs = p.emit("act", "activation", gsum[:, 8:16], gsum[:, 8:16], AF.Exp,
                      scale=-0.5, waits=[t_ln], inc="s_act")
        t_ap = None
        st_prev = [info[i - 1]["yst_st"]] if i >= 1 else []
        for g in range(8):
            t_ap = p.emit("dve", "tensor_scalar", waits=[t_rs] +
                          (st_prev + [info[i - 1]["yf_rd"]] if i >= 1 and g == 0 else []),
                          inc="s_dve",
                          out=yfin[:, g * 1024:(g + 1) * 1024],
                          in0=ysb[:, g * 1024:(g + 1) * 1024],
                          scalar1=gsum[:, 8 + g:9 + g], scalar2=None, op0=ALU.mult)
        evt = None
        tr = None
        for ct in range(64):
            bank = 4 + (ct % 2)
            pb = ps[bank].bitcast(BF16)
            pw = psum_wait(ctx, bank)
            tr = p.emit("pe", "transpose", pb[:, :128],
                        yfin[:, ct * 128:(ct + 1) * 128], mc["ident_bf"],
                        waits=[t_ap] + ([pw] if pw else []), inc="s_pe")
            evt = p.emit("dve", "tensor_copy", waits=[tr], inc="s_dve",
                         out=yfin[:, ct * 128:(ct + 1) * 128], in_=pb[:, :128])
            psum_claim(ctx, bank, evt)
        st = p.dma("pool", mc["ych_spill"][i], yfin, "yst", waits=[evt])
        info[i] = {"y_rd": t_ap, "g_rd": t_yg, "acsT_rd": t_ecs, "ecs_rd": t_t1,
                   "s0g_rd": mi, "el2_rd": t_el2, "sq_rd": t_gs, "yf_rd": tr,
                   "yst_st": st, "tq_rd": t_ys}
    mc["ych_st_all"] = p.stream_all("yst")
    phase_war(p, ar, stores=("yst",))
    ar.reset(m0)


def mamba_dram(nc, name="", ncores=8):
    d = {}
    d["hbc_spill"] = nc.dram_tensor(name + "hbc_sp", [80, 128, TOK], BF16).ap()
    d["sgate_tok_spill"] = nc.dram_tensor(name + "sg_sp", [8, 128, 8192], BF16).ap()
    d["hid_tok_spill"] = nc.dram_tensor(name + "hid_sp", [8, 128, 8192], BF16).ap()
    d["ysc_spill"] = nc.dram_tensor(name + "ysc_sp", [8, 128, 8192], BF16).ap()
    d["ych_spill"] = nc.dram_tensor(name + "ych_sp", [8, 128, 8192], BF16).ap()
    d["acs_d"] = nc.dram_tensor(name + "acs_d", [128, 128], F32).ap()
    d["halo_bounce_d"] = nc.dram_tensor(name + "halo_b", [128, 240], F32).ap()
    d["halo_all_d"] = nc.dram_tensor(name + "halo_a", [ncores * 128, 240], F32).ap()
    d["r_bounce_d"] = nc.dram_tensor(name + "r_b", [128, 8192], F32).ap()
    d["r_all_d"] = nc.dram_tensor(name + "r_a", [ncores * 128, 8192], F32).ap()
    return d


def emit_mamba_sublayer(p, ctx, nc, *, x_ap, xout_ap, inproj_cache_ap,
                        outproj_cache_ap, dins, dd, ncores=8, op_w_ready=()):
    ar = ctx["ar"]
    mpers = ar.mark()
    mc = dict(dd)
    mc["ncores"] = ncores
    mc["BT"] = ar.alloc([128, 8 * TOK], BF16)
    mc["CT"] = ar.alloc([128, 8 * TOK], BF16)
    mc["dt_raw"] = ar.alloc([128, TOK], F32)
    mc["dt"] = ar.alloc([128, TOK], F32)
    mc["A_full"] = ar.alloc([128, TOK], F32)
    mc["A_cs"] = ar.alloc([128, TOK], F32)
    mc["sp_a"] = ar.alloc([128, TOK], F32)
    mc["sp_m"] = ar.alloc([128, TOK], F32)
    mc["halo_out"] = ar.alloc([128, 240], F32)
    mc["gnw"] = ar.alloc([128, 64], F32)
    mc["convw"] = ar.alloc([128, 320], F32)
    mc["convb"] = ar.alloc([128, 80], F32)
    mc["dt_bias"] = ar.alloc([128, 1], F32)
    mc["A_log"] = ar.alloc([128, 1], F32)
    mc["cA"] = ar.alloc([128, 1], F32)
    mc["flag"] = ar.alloc([128, 1], F32)
    mc["utri"] = ar.alloc([128, 128], BF16)
    mc["ident_bf"] = ctx["ident_bf"]
    mc["ident_f"] = ctx["ident_f"]
    gamma = ar.alloc([128, NKD], F32)
    mc["dp_row_d"] = dins["dp_row_d"]
    mc["convw_d"] = dins["convw_d"]
    mc["convb_d"] = dins["convb_d"]

    toks = []
    for (dst, key) in ((gamma, "norm_w_g"), (mc["gnw"], "gnw_d"),
                       (mc["dt_bias"], "dt_bias_d"), (mc["A_log"], "A_log_d"),
                       (mc["flag"], "flag_d"), (mc["utri"], "utri_d")):
        toks.append(p.dma("sp", dst, dins[key], "misc", waits=ar.war))
    mc["flag_tok"] = toks[4]
    mc["utri_tok"] = toks[5]
    war0 = list(ar.war)
    ar.war = ar.war + toks

    xnT_mark = ar.mark()
    xnT = ar.alloc([128, NKD * TOK], BF16)
    k_ready, ln_last = emit_ln_transpose(p, ctx, x_ap=x_ap, xnT=xnT,
                                         mamba_rms=True, gamma=gamma,
                                         x_ready=[toks[0]], light=True)
    ar.war = ar.war + [ln_last]
    conv_on_idx, conv_finish = make_conv_stepper(p, ctx, mc)
    ip_tok = emit_inproj(p, ctx, mc, cache_ap=inproj_cache_ap, xnT=xnT,
                         x_ready_toks=[ln_last], hook=conv_on_idx)
    conv_finish()
    phase_war(p, ar, stores=("hout",))
    ar.reset(xnT_mark)
    mc["R"] = ar.alloc([128, 8192], F32)
    emit_mamba_scan(p, ctx, mc)
    emit_mamba_finish(p, ctx, mc)
    ych_view = dd["ych_spill"].rearrange("i p (c m) -> c p i m", m=128)
    emit_ffn_mm2(p, ctx, wd_cache_ap=outproj_cache_ap, h_ap=ych_view,
                 x_ap=x_ap, xout_ap=xout_ap, h_ready_toks=mc["ych_st_all"],
                 nk=64, scale=1.0, w_ready=op_w_ready)
    ar.reset(mpers)
    ar.war = war0 + [p.tok("s_pe"), p.tok("s_act"), p.tok("s_dve"),
                     p.tok("s_pool")] + p.stream_all("xst")
    return p.stream_all("xst")



def emit_l1ffn(p, ctx, *, x_ap, xout_ap, w0_cache_ap, a_d, b_d, b0_d, mask_d,
               n_oc=8, n_tt=8):
    ar = ctx["ar"]
    m0 = ar.mark()
    ps = ctx["psum"]
    xnT = ar.alloc([128, NKD * TOK], BF16)
    k_ready, ln_last = emit_ln_transpose(p, ctx, x_ap=x_ap, xnT=xnT,
                                         x_ready=(), n_tt=n_tt, nkd=NKD)
    AT = ar.alloc([128, NKD * 64], BF16)
    BTl = [ar.alloc([16, 4096], BF16) for _ in range(4)]
    hTl = [ar.alloc([16, TOK], BF16) for _ in range(4)]
    anat = ar.alloc([64, 4096], F32)
    bnat = ar.alloc([128, 512], F32)
    b0r = ar.alloc([1, 4096], F32)
    ones1 = ar.alloc([1, 128], F32)
    w0buf = ar.alloc([128, NKD * 512], BF16)
    htmp = ar.alloc([128, 64], BF16)
    mk = [ar.alloc([128, 512], BF16) for _ in range(4)]
    xb = [ar.alloc([128, 512], F32) for _ in range(2)]
    ac = [ar.alloc([128, 512], F32) for _ in range(3)]
    ob = [ar.alloc([128, 512], F32) for _ in range(2)]

    t_on = p.emit("dve", "memset", ones1, 1.0, waits=ar.war, inc="s_dve")
    l_a = p.dma("sp", anat, a_d, "misc", waits=ar.war)
    l_b0 = p.dma("sp", b0r, b0_d, "misc", waits=ar.war)
    ta = None
    for k in range(NKD):
        bank = 6 + (k % 2)
        pw = psum_wait(ctx, bank)
        tr = p.emit("pe", "transpose", ps[bank][:, :64],
                    anat[:, k * 128:(k + 1) * 128], ctx["ident_f"][0:64, 0:64],
                    waits=[l_a] + ([pw] if pw else []), inc="s_pe")
        ta = p.emit("dve", "tensor_copy", waits=[tr], inc="s_dve",
                    out=AT[:, k * 64:(k + 1) * 64], in_=ps[bank][:, :64])
        psum_claim(ctx, bank, ta)
    tb = None
    for l in range(4):
        l_bn = p.dma("sp", bnat, b_d[l].rearrange("(a p) r -> p a r", p=128),
                     "misc", waits=[tb] if tb else ar.war)
        for a in range(32):
            bank = 6 + (a % 2)
            pw = psum_wait(ctx, bank)
            tr = p.emit("pe", "transpose", ps[bank][0:16, :128],
                        bnat[:, a * 16:(a + 1) * 16], ctx["ident_f"],
                        waits=[l_bn] + ([pw] if pw else []), inc="s_pe")
            tb = p.emit("dve", "tensor_copy", waits=[tr], inc="s_dve",
                        out=BTl[l][:, a * 128:(a + 1) * 128], in_=ps[bank][0:16, :128])
            psum_claim(ctx, bank, tb)
    h_toks = []
    for t in range(n_tt):
        pw = psum_wait(ctx, 4)
        mm = None
        for k in range(NKD):
            mm = p.emit("pe", "matmul", ps[4][:, :64],
                        xnT[:, k * TOK + t * 128: k * TOK + (t + 1) * 128],
                        AT[:, k * 64:(k + 1) * 64],
                        waits=[ta, ln_last] + ([pw] if pw and k == 0 else []),
                        inc=("s_pe" if k == NKD - 1 else None),
                        start=(k == 0), stop=(k == NKD - 1))
        ev = p.emit("dve", "tensor_copy", waits=[mm] + (h_toks[-1:] if h_toks else []),
                    inc="s_dve", out=htmp, in_=ps[4][:, :64])
        psum_claim(ctx, 4, ev)
        ht = None
        for l in range(4):
            bank = 6 + (l % 2)
            pw = psum_wait(ctx, bank)
            tr = p.emit("pe", "transpose", ps[bank].bitcast(BF16)[0:16, :128],
                        htmp[:, l * 16:(l + 1) * 16], ctx["ident_bf"],
                        waits=[ev] + ([pw] if pw else []), inc="s_pe")
            ht = p.emit("dve", "tensor_copy", waits=[tr], inc="s_dve",
                        out=hTl[l][:, t * 128:(t + 1) * 128],
                        in_=ps[bank].bitcast(BF16)[0:16, :128])
            psum_claim(ctx, bank, ht)
        h_toks.append(ht)
    h_all = h_toks[-1]
    info = {}
    w0_rd_prev = None
    for oc in range(n_oc):
        w0_oc = (w0_cache_ap(oc) if callable(w0_cache_ap)
                 else w0_cache_ap[oc].rearrange("k p m -> p k m"))
        if not isinstance(w0_oc, (list, tuple)):
            w0_oc = [w0_oc]
        ld_w = []
        off = 0
        for seg in w0_oc:
            wdt = 1
            for s in seg.shape[1:]:
                wdt *= s
            ld_w.append(p.dma("sp", w0buf[:, off:off + wdt], seg, "w",
                              waits=([w0_rd_prev] if w0_rd_prev
                                     else list(ar.war))))
            off += wdt
        for t in range(n_tt):
            it = oc * n_tt + t
            par = it % 2
            xw = [info[it - 2]["xrd"]] if it - 2 in info else list(ar.war)
            ld_x = p.dma("sp", xb[par],
                         x_ap[t * 128:(t + 1) * 128, oc * 512:(oc + 1) * 512],
                         "x", waits=xw)
            mw = [info[it - 1]["mks"]] if it - 1 in info else []
            ld_m = [p.dma("pool", mk[l],
                          mask_d[l, t * 128:(t + 1) * 128, oc * 512:(oc + 1) * 512],
                          "hin", waits=mw) for l in range(4)]
            bb = 4 + par
            pwb = psum_wait(ctx, bb)
            mm = None
            for k in range(NKD):
                kw = []
                if k == 0:
                    kw = [ln_last] + ([pwb] if pwb else [])
                if k % 4 == 0 and t == 0:
                    kw = kw + [ld_w[min(k // 4, len(ld_w) - 1)]]
                mm = p.emit("pe", "matmul", ps[bb][:, :512],
                            xnT[:, k * TOK + t * 128: k * TOK + (t + 1) * 128],
                            w0buf[:, k * 512:(k + 1) * 512],
                            waits=kw, inc=None, start=(k == 0), stop=False)
            base_mm = p.emit("pe", "matmul", ps[bb][:, :512], ones1,
                             b0r[0:1, oc * 512:(oc + 1) * 512],
                             waits=[l_b0, t_on], inc="s_pe", start=False, stop=True)
            if t == n_tt - 1:
                w0_rd_prev = base_mm
            ymms = []
            for l in range(4):
                pwl = psum_wait(ctx, l)
                ym = p.emit("pe", "matmul", ps[l][:, :512],
                            hTl[l][:, t * 128:(t + 1) * 128],
                            BTl[l][:, oc * 512:(oc + 1) * 512],
                            waits=[h_all, tb] + ([pwl] if pwl else []),
                            inc=("s_pe" if l == 3 else None),
                            start=True, stop=True)
                ymms.append(ym)
            acc_w = [info[it - 1]["acc_rd"]] if it - 1 in info else []
            a0 = p.emit("dve", "tensor_tensor", waits=[ymms[-1]] + ld_m + acc_w,
                        inc="s_dve", out=ac[0], in0=ps[0][:, :512], in1=mk[0],
                        op=ALU.mult)
            a1 = p.emit("dve", "tensor_tensor", waits=[ymms[-1]] + ld_m,
                        inc="s_dve", out=ac[1], in0=ps[1][:, :512], in1=mk[1],
                        op=ALU.mult)
            s0 = p.emit("dve", "tensor_add", waits=[a0, a1], inc="s_dve",
                        out=ac[0], in0=ac[0], in1=ac[1])
            a2 = p.emit("dve", "tensor_tensor", waits=[ymms[-1]] + ld_m + [s0],
                        inc="s_dve", out=ac[1], in0=ps[2][:, :512], in1=mk[2],
                        op=ALU.mult)
            a3 = p.emit("dve", "tensor_tensor", waits=[ymms[-1]] + ld_m,
                        inc="s_dve", out=ac[2], in0=ps[3][:, :512], in1=mk[3],
                        op=ALU.mult)
            s1 = p.emit("dve", "tensor_add", waits=[a2, a3], inc="s_dve",
                        out=ac[1], in0=ac[1], in1=ac[2])
            s2 = p.emit("dve", "tensor_add", waits=[s0, s1], inc="s_dve",
                        out=ac[0], in0=ac[0], in1=ac[1])
            for l in range(4):
                psum_claim(ctx, l, s2)
            ob_w = [info[it - 2]["st"]] if it - 2 in info else []
            e1 = p.emit("dve", "scalar_tensor_tensor",
                        waits=[base_mm, ld_x, s2] + ob_w, inc="s_dve",
                        out=ob[par], in0=ps[bb][:, :512], scalar=0.5, in1=xb[par],
                        op0=ALU.mult, op1=ALU.add)
            psum_claim(ctx, bb, e1)
            e2 = p.emit("dve", "tensor_add", waits=[e1], inc="s_dve",
                        out=ob[par], in0=ob[par], in1=ac[0])
            st = p.dma("pool", xout_ap[t * 128:(t + 1) * 128, oc * 512:(oc + 1) * 512],
                       ob[par], "xst", waits=[e2])
            info[it] = {"st": st, "mks": a3, "acc_rd": e2, "xrd": e1}
    phase_war(p, ar, stores=("xst",))
    ar.reset(m0)
    return p.stream_all("xst")



def _l1_build_T(p, ctx, *, src_ap, dst, nrow_t, ncol_t, ld_buf, bf_buf,
                stream="misc", war=(), name=""):
    ar = ctx["ar"]
    ps = ctx["psum"]
    toks = []
    cvt = None
    ldb_bf = bf_buf[:, :ncol_t * 128]
    for rb in range(nrow_t):
        ld = p.dma("sp", ld_buf, src_ap[rb * 128:(rb + 1) * 128, :], stream,
                   waits=list(war) + ([cvt] if cvt else []) + toks[-1:])
        cvt = p.emit("dve", "tensor_copy", waits=[ld] + toks[-1:], inc="s_dve",
                     out=ldb_bf, in_=ld_buf)
        for k in range(ncol_t):
            bank = 6 + (k % 2)
            pw = psum_wait(ctx, bank)
            tr = p.emit("pe", "transpose", ps[bank].bitcast(BF16)[:, :128],
                        ldb_bf[:, k * 128:(k + 1) * 128], ctx["ident_bf"],
                        waits=[cvt] + ([pw] if pw else []), inc="s_pe")
            ev = p.emit("dve", "tensor_copy", waits=[tr], inc="s_dve",
                        out=dst[:, (k * nrow_t + rb) * 128:(k * nrow_t + rb + 1) * 128],
                        in_=ps[bank].bitcast(BF16)[:, :128])
            psum_claim(ctx, bank, ev)
            toks.append(ev)
    return toks[-1]


def _l1_ln(p, ctx, *, h_s, out_s, w_ap, b_ap, silu, sq_s, ones_col,
           war=(), extra_rd=()):
    ps = ctx["psum"]
    st = ctx["l1_stat"]
    sq_t = None
    for j in range(4):
        sq_t = p.emit("act", "activation", sq_s[:, j * 1024:(j + 1) * 1024],
                      h_s[:, j * 1024:(j + 1) * 1024], AF.Square,
                      waits=list(war) + (list(extra_rd) if j == 0 else []),
                      inc="s_act")
    last = None
    for half in range(2):
        hw_ = list(war)
        for (src_s, dst_off, wtok) in ((h_s, half * 512, None),
                                       (sq_s, 1024 + half * 512, sq_t)):
            pw = psum_wait(ctx, 4)
            mm = None
            for j in range(4):
                mm = p.emit("pe", "matmul", ps[4][0:1, 0:512], ones_col,
                            src_s[:, j * 1024 + half * 512: j * 1024 + (half + 1) * 512],
                            waits=hw_ + ([wtok] if wtok else []) +
                            ([pw] if pw and j == 0 else []),
                            inc=("s_pe" if j == 3 else None),
                            start=(j == 0), stop=(j == 3))
            ev = p.emit("dve", "tensor_copy", waits=[mm], inc="s_dve",
                        out=st[0:1, dst_off:dst_off + 512], in_=ps[4][0:1, 0:512])
            psum_claim(ctx, 4, ev)
        o = half * 512
        t1 = p.emit("dve", "tensor_scalar", waits=[ev], inc="s_dve",
                    out=st[0:1, o:o + 512], in0=st[0:1, o:o + 512],
                    scalar1=1.0 / 512, scalar2=None, op0=ALU.mult)
        t2 = p.emit("dve", "tensor_scalar", waits=[ev], inc="s_dve",
                    out=st[0:1, 1024 + o:1024 + o + 512],
                    in0=st[0:1, 1024 + o:1024 + o + 512],
                    scalar1=1.0 / 512, scalar2=EPS, op0=ALU.mult, op1=ALU.add)
        t3 = p.emit("dve", "tensor_mul", waits=[t1], inc="s_dve",
                    out=st[0:1, 2048:2560], in0=st[0:1, o:o + 512],
                    in1=st[0:1, o:o + 512])
        t4 = p.emit("dve", "tensor_sub", waits=[t2, t3], inc="s_dve",
                    out=st[0:1, 1024 + o:1024 + o + 512],
                    in0=st[0:1, 1024 + o:1024 + o + 512], in1=st[0:1, 2048:2560])
        t5 = p.emit("act", "activation", st[0:1, 1024 + o:1024 + o + 512],
                    st[0:1, 1024 + o:1024 + o + 512], AF.Ln, waits=[t4], inc="s_act")
        t6 = p.emit("act", "activation", st[0:1, 1024 + o:1024 + o + 512],
                    st[0:1, 1024 + o:1024 + o + 512], AF.Exp, scale=-0.5,
                    waits=[t5], inc="s_act")
        pw5 = psum_wait(ctx, 5)
        mb = p.emit("pe", "matmul", ps[5][:, 0:512], ctx["l1_ones1"],
                    st[0:1, o:o + 512], waits=[t1] + ([pw5] if pw5 else []),
                    inc=None, start=True, stop=True)
        pw6 = psum_wait(ctx, 6)
        rb_ = p.emit("pe", "matmul", ps[6][:, 0:512], ctx["l1_ones1"],
                     st[0:1, 1024 + o:1024 + o + 512],
                     waits=[t6] + ([pw6] if pw6 else []), inc="s_pe",
                     start=True, stop=True)
        for j in range(4):
            sl = slice(j * 1024 + half * 512, j * 1024 + (half + 1) * 512)
            tj1 = p.emit("dve", "tensor_sub", waits=[rb_] +
                         (list(extra_rd) if j == 0 and half == 0 else []),
                         inc="s_dve",
                         out=out_s[:, sl], in0=h_s[:, sl], in1=ps[5][:, 0:512])
            tj2 = p.emit("dve", "tensor_tensor", waits=[tj1], inc="s_dve",
                         out=out_s[:, sl], in0=out_s[:, sl],
                         in1=ps[6][:, 0:512], op=ALU.mult)
            last = p.emit("act", "activation", out_s[:, sl], out_s[:, sl],
                          AF.Silu if silu else AF.Identity,
                          scale=w_ap[j], bias=b_ap[j], waits=[tj2], inc="s_act")
        psum_claim(ctx, 5, last)
        psum_claim(ctx, 6, last)
    return last


def _l1_sconv(p, ctx, mc1, *, y_s, h_s, lidx, y_tok):
    ar = ctx["ar"]
    ps = ctx["psum"]
    Call = mc1["Call"]; Sall = mc1["Sall"]; Rall = mc1["Rall"]
    sc = mc1["sc_scr"]
    u_re = mc1["u_re"]; u_im = mc1["u_im"]
    d_re = mc1["d_re"]; d_im = mc1["d_im"]
    rrb = mc1["rrb"]
    carr = mc1["carry"]
    toks = []
    scan_toks = {}
    tab_toks = {}
    for j in range(4):
        cidx = lidx * 4 + j
        C = Call[:, j * 1025:(j + 1) * 1025]
        S = Sall[:, j * 1025:(j + 1) * 1025]
        R = Rall[:, j * 1025:(j + 1) * 1025]
        pr = mc1["phzr"][:, cidx:cidx + 1]
        pi = mc1["phzi"][:, cidx:cidx + 1]
        war_t = [y_tok]
        t1 = p.emit("dve", "tensor_tensor", waits=[mc1["par_tok"]] + war_t,
                    inc="s_dve", out=sc[:, 0:1], in0=pr, in1=pr, op=ALU.mult)
        t2 = p.emit("dve", "scalar_tensor_tensor", waits=[t1], inc="s_dve",
                    out=sc[:, 0:1], in0=pi, scalar=pi, in1=sc[:, 0:1],
                    op0=ALU.mult, op1=ALU.add)
        t2 = p.emit("dve", "tensor_scalar", waits=[t2], inc="s_dve",
                    out=sc[:, 0:1], in0=sc[:, 0:1], scalar1=1e-37, scalar2=None,
                    op0=ALU.add)
        t3 = p.emit("act", "activation", sc[:, 1:2], sc[:, 0:1], AF.Ln,
                    waits=[t2], inc="s_act")
        t4 = p.emit("act", "activation", sc[:, 1:2], sc[:, 1:2], AF.Exp,
                    scale=0.5, waits=[t3], inc="s_act")
        t5 = p.emit("act", "activation", sc[:, 2:3], sc[:, 1:2], AF.Tanh,
                    waits=[t4], inc="s_act")
        t6 = p.emit("dve", "reciprocal", sc[:, 3:4], sc[:, 1:2], waits=[t4],
                    inc="s_dve")
        t7 = p.emit("dve", "tensor_tensor", waits=[t6], inc="s_dve",
                    out=sc[:, 4:5], in0=pr, in1=sc[:, 3:4], op=ALU.mult)
        t8 = p.emit("dve", "tensor_tensor", waits=[t6], inc="s_dve",
                    out=sc[:, 5:6], in0=pi, in1=sc[:, 3:4], op=ALU.mult)
        tc = p.emit("dve", "memset", C[:, 0:1], 1.0, waits=war_t, inc="s_dve")
        ts_ = p.emit("dve", "memset", S[:, 0:1], 0.0, waits=war_t, inc="s_dve")
        tr_ = p.emit("dve", "memset", R[:, 0:1], 1.0, waits=war_t, inc="s_dve")
        base = 1
        ctok, stok, rtok = tc, ts_, tr_
        cb, sb, rb = sc[:, 4:5], sc[:, 5:6], sc[:, 2:3]
        first = True
        while base < 1025:
            n = min(base, 1025 - base)
            if first:
                ctok = p.emit("dve", "tensor_copy", waits=[t7, tc], inc="s_dve",
                              out=C[:, 1:2], in_=sc[:, 4:5])
                stok = p.emit("dve", "tensor_copy", waits=[t8, ts_], inc="s_dve",
                              out=S[:, 1:2], in_=sc[:, 5:6])
                rtok = p.emit("dve", "tensor_copy", waits=[t5, tr_], inc="s_dve",
                              out=R[:, 1:2], in_=sc[:, 2:3])
                first = False
            else:
                b2 = base // 2
                mm1_ = p.emit("dve", "tensor_tensor", waits=[ctok, stok], inc="s_dve",
                              out=sc[:, 17:18], in0=C[:, b2:b2 + 1],
                              in1=C[:, b2:b2 + 1], op=ALU.mult)
                mm2_ = p.emit("dve", "scalar_tensor_tensor", waits=[mm1_], inc="s_dve",
                              out=sc[:, 17:18], in0=S[:, b2:b2 + 1],
                              scalar=S[:, b2:b2 + 1], in1=sc[:, 17:18],
                              op0=ALU.mult, op1=ALU.subtract)
                mm2_ = p.emit("dve", "tensor_scalar", waits=[mm2_], inc="s_dve",
                              out=sc[:, 17:18], in0=sc[:, 17:18], scalar1=-1.0,
                              scalar2=None, op0=ALU.mult)
                mm3_ = p.emit("dve", "tensor_tensor", waits=[ctok, stok], inc="s_dve",
                              out=sc[:, 18:19], in0=S[:, b2:b2 + 1],
                              in1=C[:, b2:b2 + 1], op=ALU.mult)
                mm3_ = p.emit("dve", "tensor_scalar", waits=[mm3_], inc="s_dve",
                              out=sc[:, 18:19], in0=sc[:, 18:19], scalar1=2.0,
                              scalar2=None, op0=ALU.mult)
                mm4_ = p.emit("dve", "tensor_tensor", waits=[rtok], inc="s_dve",
                              out=sc[:, 19:20], in0=R[:, b2:b2 + 1],
                              in1=R[:, b2:b2 + 1], op=ALU.mult)
                cbase = sc[:, 17:18]
                sbase = sc[:, 18:19]
                rbase = sc[:, 19:20]
                m1 = p.emit("dve", "tensor_scalar", waits=[mm2_, mm3_], inc="s_dve",
                            out=mc1["tmp1"][:, 0:n],
                            in0=S[:, 0:n], scalar1=sbase, scalar2=None, op0=ALU.mult)
                ctok2 = p.emit("dve", "scalar_tensor_tensor", waits=[m1], inc="s_dve",
                               out=C[:, base:base + n], in0=C[:, 0:n], scalar=cbase,
                               in1=mc1["tmp1"][:, 0:n], op0=ALU.mult, op1=ALU.subtract)
                m2 = p.emit("dve", "tensor_scalar", waits=[ctok, stok, ctok2],
                            inc="s_dve",
                            out=mc1["tmp1"][:, 0:n], in0=S[:, 0:n], scalar1=cbase,
                            scalar2=None, op0=ALU.mult)
                stok2 = p.emit("dve", "scalar_tensor_tensor", waits=[m2, ctok2],
                               inc="s_dve",
                               out=S[:, base:base + n], in0=C[:, 0:n], scalar=sbase,
                               in1=mc1["tmp1"][:, 0:n], op0=ALU.mult, op1=ALU.add)
                rtok = p.emit("dve", "tensor_scalar", waits=[rtok, mm4_], inc="s_dve",
                              out=R[:, base:base + n], in0=R[:, 0:n], scalar1=rbase,
                              scalar2=None, op0=ALU.mult)
                ctok, stok = ctok2, stok2
            base *= 2
        tabs = (ctok, stok, rtok)
        tab_toks[j] = stok
        uw = [y_tok, ctok, stok]
        t_ur = p.emit("dve", "tensor_tensor", waits=uw + ([scan_toks[j - 1][2]] if j - 1 in scan_toks else []),
                      inc="s_dve",
                      out=u_re, in0=y_s[:, j * 1024:(j + 1) * 1024],
                      in1=C[:, 0:1024], op=ALU.mult)
        t_ui = p.emit("dve", "tensor_tensor", waits=uw, inc="s_dve",
                      out=u_im, in0=y_s[:, j * 1024:(j + 1) * 1024],
                      in1=S[:, 0:1024], op=ALU.mult)
        lr_ = mc1["lcr"][:, cidx:cidx + 1]
        li_ = mc1["lci"][:, cidx:cidx + 1]
        i1 = p.emit("dve", "tensor_tensor", waits=[t7, mc1["par_tok"]], inc="s_dve",
                    out=sc[:, 6:7], in0=lr_, in1=sc[:, 4:5], op=ALU.mult)
        i2 = p.emit("dve", "scalar_tensor_tensor", waits=[i1, t8], inc="s_dve",
                    out=sc[:, 6:7], in0=li_, scalar=sc[:, 5:6], in1=sc[:, 6:7],
                    op0=ALU.mult, op1=ALU.subtract)
        i2 = p.emit("dve", "tensor_scalar", waits=[i2], inc="s_dve",
                    out=sc[:, 6:7], in0=sc[:, 6:7], scalar1=-1.0, scalar2=None,
                    op0=ALU.mult)
        i3 = p.emit("dve", "tensor_tensor", waits=[t8], inc="s_dve",
                    out=sc[:, 7:8], in0=lr_, in1=sc[:, 5:6], op=ALU.mult)
        i4 = p.emit("dve", "scalar_tensor_tensor", waits=[i3, t7], inc="s_dve",
                    out=sc[:, 7:8], in0=li_, scalar=sc[:, 4:5], in1=sc[:, 7:8],
                    op0=ALU.mult, op1=ALU.add)
        i4 = p.emit("dve", "tensor_scalar", waits=[i4], inc="s_dve",
                    out=sc[:, 7:8], in0=sc[:, 7:8], scalar1=-1.0, scalar2=None,
                    op0=ALU.mult)
        rrb_t = p.emit("dve", "tensor_scalar", waits=[t5, mc1["ones_tok"]] +
                       ([scan_toks[j - 1][2]] if j - 1 in scan_toks else []),
                       inc="s_dve",
                       out=rrb, in0=mc1["ones_t"], scalar1=sc[:, 2:3],
                       scalar2=None, op0=ALU.mult)
        s_re = p.emit("dve", "tensor_tensor_scan", waits=[t_ur, rrb_t, i2],
                      inc="s_dve", out=d_re[:, j * 1024:(j + 1) * 1024],
                      data0=rrb, data1=u_re, initial=sc[:, 6:7],
                      op0=ALU.mult, op1=ALU.add)
        s_im = p.emit("dve", "tensor_tensor_scan", waits=[t_ui, rrb_t, i4],
                      inc="s_dve", out=d_im[:, j * 1024:(j + 1) * 1024],
                      data0=rrb, data1=u_im, initial=sc[:, 7:8],
                      op0=ALU.mult, op1=ALU.add)
        c1 = p.emit("dve", "tensor_tensor", waits=[s_re, ctok], inc="s_dve",
                    out=carr[:, 2 * j:2 * j + 1], in0=C[:, 1023:1024],
                    in1=d_re[:, j * 1024 + 1023:j * 1024 + 1024], op=ALU.mult)
        c2 = p.emit("dve", "scalar_tensor_tensor", waits=[c1, s_im, stok], inc="s_dve",
                    out=carr[:, 2 * j:2 * j + 1],
                    in0=d_im[:, j * 1024 + 1023:j * 1024 + 1024],
                    scalar=S[:, 1023:1024], in1=carr[:, 2 * j:2 * j + 1],
                    op0=ALU.mult, op1=ALU.add)
        c3 = p.emit("dve", "tensor_tensor", waits=[s_re, stok], inc="s_dve",
                    out=carr[:, 2 * j + 1:2 * j + 2], in0=S[:, 1023:1024],
                    in1=d_re[:, j * 1024 + 1023:j * 1024 + 1024], op=ALU.mult)
        c4 = p.emit("dve", "scalar_tensor_tensor", waits=[c3, s_im, ctok], inc="s_dve",
                    out=carr[:, 2 * j + 1:2 * j + 2],
                    in0=d_im[:, j * 1024 + 1023:j * 1024 + 1024],
                    scalar=C[:, 1023:1024], in1=carr[:, 2 * j + 1:2 * j + 2],
                    op0=ALU.mult, op1=ALU.subtract)
        c4b = p.emit("dve", "tensor_scalar", waits=[c4], inc="s_dve",
                     out=carr[:, 2 * j + 1:2 * j + 2],
                     in0=carr[:, 2 * j + 1:2 * j + 2], scalar1=-1.0, scalar2=None,
                     op0=ALU.mult)
        scan_toks[j] = (s_re, s_im, c4b, tabs, (i2, i4))
    bt = p.dma("pool", mc1["carr_b_d"], carr[:, 0:8], "pb",
               waits=[scan_toks[j][2] for j in range(4)])
    ag = p.raw("pool", lambda g: g.collective_compute(
        "AllGather", ALU.bypass, replica_groups=[list(range(mc1["ncores"]))],
        ins=[mc1["carr_b_d"].opt()], outs=[mc1["carr_a_d"].opt()]),
        waits=[bt], inc="s_cc")
    allc = mc1["allc"]
    ld_all = p.dma("pool", allc,
                   mc1["carr_a_d"].rearrange("(r p) c -> p r c", p=128),
                   "pb", waits=[ag] + ([mc1["allc_war"]]
                                       if mc1.get("allc_war") else []))
    t_rdc = p.emit("dve", "tensor_scalar", waits=[ld_all, mc1["par_tok"]],
                   inc="s_dve", out=carr[:, 8:16], in0=allc[:, 0:8],
                   scalar1=mc1["rksel"][:, 0:1], scalar2=None, op0=ALU.mult)
    for rr in range(1, mc1["ncores"]):
        t_rdc = p.emit("dve", "scalar_tensor_tensor", waits=[t_rdc], inc="s_dve",
                       out=carr[:, 8:16], in0=allc[:, 8 * rr:8 * rr + 8],
                       scalar=mc1["rksel"][:, rr:rr + 1], in1=carr[:, 8:16],
                       op0=ALU.mult, op1=ALU.add)
    mc1["allc_war"] = t_rdc

    h_last = None
    for j in range(4):
        cidx = lidx * 4 + j
        (s_re, s_im, _c, tabs, inits) = scan_toks[j]
        ctok, stok, rtok = tabs
        pr = mc1["phzr"][:, cidx:cidx + 1]
        pi = mc1["phzi"][:, cidx:cidx + 1]
        t1 = p.emit("dve", "tensor_tensor", waits=[mc1["par_tok"], t_rdc], inc="s_dve",
                    out=sc[:, 8:9], in0=pr, in1=pr, op=ALU.mult)
        t2 = p.emit("dve", "scalar_tensor_tensor", waits=[t1], inc="s_dve",
                    out=sc[:, 8:9], in0=pi, scalar=pi, in1=sc[:, 8:9],
                    op0=ALU.mult, op1=ALU.add)
        t2 = p.emit("dve", "tensor_scalar", waits=[t2], inc="s_dve",
                    out=sc[:, 8:9], in0=sc[:, 8:9], scalar1=1e-37, scalar2=None,
                    op0=ALU.add)
        t3 = p.emit("act", "activation", sc[:, 9:10], sc[:, 8:9], AF.Ln,
                    waits=[t2], inc="s_act")
        t4 = p.emit("act", "activation", sc[:, 9:10], sc[:, 9:10], AF.Exp,
                    scale=0.5, waits=[t3], inc="s_act")
        t6 = p.emit("dve", "reciprocal", sc[:, 10:11], sc[:, 9:10], waits=[t4],
                    inc="s_dve")
        t7 = p.emit("dve", "tensor_tensor", waits=[t6], inc="s_dve",
                    out=sc[:, 11:12], in0=pr, in1=sc[:, 10:11], op=ALU.mult)
        t8 = p.emit("dve", "tensor_tensor", waits=[t6], inc="s_dve",
                    out=sc[:, 12:13], in0=pi, in1=sc[:, 10:11], op=ALU.mult)
        pre = carr[:, 8 + 2 * j:9 + 2 * j]
        pim = carr[:, 9 + 2 * j:10 + 2 * j]
        j1 = p.emit("dve", "tensor_tensor", waits=[t7, t_rdc], inc="s_dve",
                    out=sc[:, 13:14], in0=pre, in1=sc[:, 11:12], op=ALU.mult)
        j2 = p.emit("dve", "scalar_tensor_tensor", waits=[j1, t8], inc="s_dve",
                    out=sc[:, 13:14], in0=pim, scalar=sc[:, 12:13], in1=sc[:, 13:14],
                    op0=ALU.mult, op1=ALU.subtract)
        j2 = p.emit("dve", "tensor_scalar", waits=[j2], inc="s_dve",
                    out=sc[:, 13:14], in0=sc[:, 13:14], scalar1=-1.0, scalar2=None,
                    op0=ALU.mult)
        j3 = p.emit("dve", "tensor_tensor", waits=[t8], inc="s_dve",
                    out=sc[:, 14:15], in0=pre, in1=sc[:, 12:13], op=ALU.mult)
        j4 = p.emit("dve", "scalar_tensor_tensor", waits=[j3, t7], inc="s_dve",
                    out=sc[:, 14:15], in0=pim, scalar=sc[:, 11:12], in1=sc[:, 14:15],
                    op0=ALU.mult, op1=ALU.add)
        j4 = p.emit("dve", "tensor_scalar", waits=[j4], inc="s_dve",
                    out=sc[:, 14:15], in0=sc[:, 14:15], scalar1=-1.0, scalar2=None,
                    op0=ALU.mult)
        lr_ = mc1["lcr"][:, cidx:cidx + 1]
        li_ = mc1["lci"][:, cidx:cidx + 1]
        k1 = p.emit("dve", "tensor_tensor", waits=[t7], inc="s_dve",
                    out=sc[:, 15:16], in0=lr_, in1=sc[:, 11:12], op=ALU.mult)
        k2 = p.emit("dve", "scalar_tensor_tensor", waits=[k1, t8], inc="s_dve",
                    out=sc[:, 15:16], in0=li_, scalar=sc[:, 12:13], in1=sc[:, 15:16],
                    op0=ALU.mult, op1=ALU.subtract)
        k2 = p.emit("dve", "tensor_scalar", waits=[k2], inc="s_dve",
                    out=sc[:, 15:16], in0=sc[:, 15:16], scalar1=-1.0, scalar2=None,
                    op0=ALU.mult)
        k3 = p.emit("dve", "tensor_tensor", waits=[t8], inc="s_dve",
                    out=sc[:, 16:17], in0=lr_, in1=sc[:, 12:13], op=ALU.mult)
        k4 = p.emit("dve", "scalar_tensor_tensor", waits=[k3, t7], inc="s_dve",
                    out=sc[:, 16:17], in0=li_, scalar=sc[:, 11:12], in1=sc[:, 16:17],
                    op0=ALU.mult, op1=ALU.add)
        k4 = p.emit("dve", "tensor_scalar", waits=[k4], inc="s_dve",
                    out=sc[:, 16:17], in0=sc[:, 16:17], scalar1=-1.0, scalar2=None,
                    op0=ALU.mult)
        dr = p.emit("dve", "tensor_sub", waits=[j2, k2], inc="s_dve",
                    out=sc[:, 13:14], in0=sc[:, 13:14], in1=sc[:, 15:16])
        dr = p.emit("dve", "tensor_scalar", waits=[dr, mc1["flag_tok"]], inc="s_dve",
                    out=sc[:, 13:14], in0=sc[:, 13:14], scalar1=mc1["flag"][:, 0:1],
                    scalar2=None, op0=ALU.mult)
        di = p.emit("dve", "tensor_sub", waits=[j4, k4], inc="s_dve",
                    out=sc[:, 14:15], in0=sc[:, 14:15], in1=sc[:, 16:17])
        di = p.emit("dve", "tensor_scalar", waits=[di, mc1["flag_tok"]], inc="s_dve",
                    out=sc[:, 14:15], in0=sc[:, 14:15], scalar1=mc1["flag"][:, 0:1],
                    scalar2=None, op0=ALU.mult)
        Rj = mc1["Rall"][:, j * 1025:(j + 1) * 1025]
        f1 = p.emit("dve", "scalar_tensor_tensor", waits=[dr, scan_toks[j][0]],
                    inc="s_dve",
                    out=d_re[:, j * 1024:(j + 1) * 1024],
                    in0=Rj[:, 1:1025], scalar=sc[:, 13:14],
                    in1=d_re[:, j * 1024:(j + 1) * 1024],
                    op0=ALU.mult, op1=ALU.add)
        f2 = p.emit("dve", "scalar_tensor_tensor", waits=[di, scan_toks[j][1]],
                    inc="s_dve",
                    out=d_im[:, j * 1024:(j + 1) * 1024],
                    in0=Rj[:, 1:1025], scalar=sc[:, 14:15],
                    in1=d_im[:, j * 1024:(j + 1) * 1024],
                    op0=ALU.mult, op1=ALU.add)
        Cj = mc1["Call"][:, j * 1025:(j + 1) * 1025]
        Sj = mc1["Sall"][:, j * 1025:(j + 1) * 1025]
        g1 = p.emit("dve", "tensor_tensor", waits=[f1] + ([h_last] if h_last else []),
                    inc="s_dve",
                    out=u_re, in0=Cj[:, 0:1024],
                    in1=d_re[:, j * 1024:(j + 1) * 1024], op=ALU.mult)
        g2 = p.emit("dve", "tensor_tensor", waits=[f2], inc="s_dve",
                    out=u_im, in0=Sj[:, 0:1024],
                    in1=d_im[:, j * 1024:(j + 1) * 1024], op=ALU.mult)
        g3 = p.emit("dve", "tensor_add", waits=[g1, g2], inc="s_dve",
                    out=u_re, in0=u_re, in1=u_im)
        h_last = p.emit("dve", "scalar_tensor_tensor", waits=[g3, mc1["par_tok"]],
                        inc="s_dve",
                        out=h_s[:, j * 1024:(j + 1) * 1024], in0=u_re,
                        scalar=mc1["scl"][:, cidx:cidx + 1],
                        in1=h_s[:, j * 1024:(j + 1) * 1024],
                        op0=ALU.mult, op1=ALU.add)
    return h_last


def emit_l1attn(p, ctx, nc, *, x_ap, xout_ap, dins, dd1, ncores=8, dbg_ap=None):
    ar = ctx["ar"]
    m0 = ar.mark()
    ps = ctx["psum"]
    mc1 = dict(dd1)
    mc1["ncores"] = ncores
    h_s = ar.alloc([128, 4 * TOK], F32)
    y_s = ar.alloc([128, 4 * TOK], F32)
    sq_s = ar.alloc([128, 4 * TOK], F32)
    ybuf = ar.alloc([128, 4 * TOK], BF16)
    tmp1 = ar.alloc([128, 512], F32)
    u_re = ar.alloc([128, TOK], F32)
    u_im = ar.alloc([128, TOK], F32)
    ones_col = ar.alloc([128, 1], F32)
    ones1 = ar.alloc([1, 128], BF16)
    ones1f = ar.alloc([1, 128], F32)
    sc_scr = ar.alloc([128, 32], F32)
    carr = ar.alloc([128, 16], F32)
    stat = ar.alloc([1, 2560], F32)
    ldw = ar.alloc([128, 4096], F32)
    smalls = {}
    for k in ("bin_d", "l1w_d", "l1b_d", "l2w_d", "l2b_d", "scl_d", "phzr_d",
              "phzi_d", "lcr_d", "lci_d", "b1_d", "b2_d", "lnlw_d", "lnlb_d",
              "flag_d", "rksel_d"):
        shp = {"bin_d": 4, "b1_d": 64, "b2_d": 16, "lnlw_d": 4, "lnlb_d": 4,
               "flag_d": 1, "rksel_d": 8}.get(k, 16)
        smalls[k] = ar.alloc([128, shp], F32)
    allc = ar.alloc([128, 8 * ncores], F32)
    boutr = ar.alloc([1, 4096], BF16)
    pers_mark = ar.mark()

    par_toks = []
    for k in smalls:
        par_toks.append(p.dma("sp", smalls[k], dins[k], "misc", waits=ar.war))
    lb0 = p.dma("pool", boutr, dins["bout_d"], "pb", waits=ar.war)
    t_oc = p.emit("dve", "memset", ones_col, 1.0, waits=ar.war, inc="s_dve")
    t_o1 = p.emit("dve", "memset", ones1, 1.0, waits=ar.war, inc="s_dve")
    t_o1f = p.emit("dve", "memset", ones1f, 1.0, waits=ar.war, inc="s_dve")
    par_tok = par_toks[-1]
    mc1.update(dict(flag=smalls["flag_d"],
                    phzr=smalls["phzr_d"], phzi=smalls["phzi_d"],
                    lcr=smalls["lcr_d"], lci=smalls["lci_d"],
                    scl=smalls["scl_d"], rksel=smalls["rksel_d"],
                    sc_scr=sc_scr, tmp1=tmp1,
                    u_re=u_re, u_im=u_im,
                    carry=carr, allc=allc))
    par_all = p.stream_all("misc")
    t_pb = p.emit("dve", "engine_nop", waits=par_all, inc="s_dve")
    mc1["par_tok"] = t_pb
    mc1["flag_tok"] = t_pb
    ctx["l1_stat"] = stat
    ctx["l1_ones1"] = ones1f

    xnm = pers_mark
    xnT = ar.alloc([128, NKD * TOK], BF16)
    k_ready, ln_last = emit_ln_transpose(p, ctx, x_ap=x_ap, xnT=xnT,
                                         x_ready=(), n_tt=8, nkd=NKD, light=True)
    winT = ar.alloc([128, NKD * 512], BF16)
    wcvt = ar.alloc([128, 2048], BF16)
    wt0 = _l1_build_T(p, ctx, src_ap=dins["win_d"][:, 0:2048],
                      dst=winT[:, 0:16 * 512], nrow_t=4,
                      ncol_t=16, ld_buf=ldw[:, :2048], bf_buf=wcvt, war=ar.war)
    wt = _l1_build_T(p, ctx, src_ap=dins["win_d"][:, 2048:4096],
                     dst=winT[:, 16 * 512:32 * 512], nrow_t=4,
                     ncol_t=16, ld_buf=ldw[:, :2048], bf_buf=wcvt,
                     war=ar.war + [wt0])
    ev_h = None
    for ob in range(4):
        for th in range(2):
            bank = ob % 2
            pw = psum_wait(ctx, bank)
            mm = None
            for k in range(NKD):
                mm = p.emit("pe", "matmul", ps[bank][:, :512],
                            winT[:, (k * 4 + ob) * 128:(k * 4 + ob + 1) * 128],
                            xnT[:, k * TOK + th * 512: k * TOK + (th + 1) * 512],
                            waits=([wt, ln_last] + ([pw] if pw else []))
                            if k == 0 else (),
                            inc=("s_pe" if k == NKD - 1 else None),
                            start=(k == 0), stop=(k == NKD - 1))
            ev_h = p.emit("act", "activation",
                          h_s[:, ob * TOK + th * 512: ob * TOK + (th + 1) * 512],
                          ps[bank][:, :512], AF.Identity,
                          scale=1.0, bias=smalls["bin_d"][:, ob:ob + 1],
                          waits=[mm] + par_all, inc="s_act")
            psum_claim(ctx, bank, ev_h)
    ar.reset(xnm)
    h_tok = ev_h
    if dbg_ap is not None:
        dst_ = p.dma("pool", dbg_ap[0], h_s, "pb", waits=[h_tok])
        h_tok = p.emit("dve", "engine_nop", waits=[h_tok, dst_], inc="s_dve")

    for lidx in range(4):
        l1w = [smalls["l1w_d"][:, lidx * 4 + j:lidx * 4 + j + 1] for j in range(4)]
        l1b = [smalls["l1b_d"][:, lidx * 4 + j:lidx * 4 + j + 1] for j in range(4)]
        t_y = _l1_ln(p, ctx, h_s=h_s, out_s=y_s, w_ap=l1w, b_ap=l1b, silu=True,
                     sq_s=sq_s, ones_col=ones_col, war=[h_tok, t_oc] + par_all)
        if dbg_ap is not None and lidx == 0 and dbg_ap.shape[0] > 5:
            dst_ = p.dma("pool", dbg_ap[5], y_s, "pb", waits=[t_y])
            t_y = p.emit("dve", "engine_nop", waits=[t_y, dst_], inc="s_dve")
        mS = ar.mark()
        mc1["Call"] = ar.alloc([128, 4 * 1025], F32)
        mc1["Sall"] = ar.alloc([128, 4 * 1025], F32)
        mc1["Rall"] = ar.alloc([128, 4 * 1025], F32)
        mc1["d_re"] = ar.alloc([128, 4 * TOK], F32)
        mc1["d_im"] = ar.alloc([128, 4 * TOK], F32)
        mc1["rrb"] = ar.alloc([128, TOK], F32)
        mc1["ones_t"] = ar.alloc([128, TOK], F32)
        t_on = p.emit("dve", "memset", mc1["ones_t"], 1.0,
                      waits=ar.war + [t_y], inc="s_dve")
        mc1["ones_tok"] = t_on
        h_tok = _l1_sconv(p, ctx, mc1, y_s=y_s, h_s=h_s, lidx=lidx, y_tok=t_y)
        if dbg_ap is not None and lidx == 0 and dbg_ap.shape[0] > 5:
            d1_ = p.dma("pool", dbg_ap[6], mc1["d_re"], "pb", waits=[h_tok])
            d2_ = p.dma("pool", dbg_ap[7], mc1["Call"][:, :4096], "pb", waits=[h_tok, d1_])
            d3_ = p.dma("pool", dbg_ap[8], mc1["Rall"][:, :4096], "pb", waits=[h_tok, d2_])
            d4_ = p.dma("pool", dbg_ap[9], h_s, "pb", waits=[h_tok, d3_])
            h_tok = p.emit("dve", "engine_nop", waits=[h_tok, d4_], inc="s_dve")
        phase_war(p, ar)
        ar.reset(mS)
        y1_s = ar.alloc([128, 16 * TOK], BF16)
        w1T = ar.alloc([128, 4 * 2048], BF16)
        w2T = ar.alloc([128, 16 * 512], BF16)
        wcvt2 = ar.alloc([128, 2048], BF16)
        w1w = list(ar.war)
        wt1 = _l1_build_T(p, ctx, src_ap=dins["w1_d"][lidx], dst=w1T, nrow_t=16,
                          ncol_t=4, ld_buf=ldw[:, :512], bf_buf=wcvt2, war=w1w)
        wt2 = _l1_build_T(p, ctx, src_ap=dins["w2_d"][lidx], dst=w2T, nrow_t=4,
                          ncol_t=16, ld_buf=ldw[:, :2048], bf_buf=wcvt2,
                          war=w1w + [wt1])
        l2w = [smalls["l2w_d"][:, lidx * 4 + j:lidx * 4 + j + 1] for j in range(4)]
        l2b = [smalls["l2b_d"][:, lidx * 4 + j:lidx * 4 + j + 1] for j in range(4)]
        t_y2 = _l1_ln(p, ctx, h_s=h_s, out_s=y_s, w_ap=l2w, b_ap=l2b, silu=False,
                      sq_s=sq_s, ones_col=ones_col, war=[h_tok])
        yb = ybuf
        t_yb = p.emit("dve", "tensor_copy", waits=[t_y2], inc="s_dve",
                      out=yb, in_=y_s)
        mm1l = None
        for Mb in range(16):
            for th in range(2):
                bank = Mb % 2
                pw = psum_wait(ctx, bank)
                mm = None
                for k in range(4):
                    mm = p.emit("pe", "matmul", ps[bank][:, :512],
                                w1T[:, (k * 16 + Mb) * 128:(k * 16 + Mb + 1) * 128],
                                yb[:, k * TOK + th * 512: k * TOK + (th + 1) * 512],
                                waits=([wt1, t_yb] + ([pw] if pw else []))
                                if k == 0 else (),
                                inc=("s_pe" if k == 3 else None),
                                start=(k == 0), stop=(k == 3))
                mm1l = p.emit("act", "activation",
                              y1_s[:, Mb * TOK + th * 512: Mb * TOK + (th + 1) * 512],
                              ps[bank][:, :512], AF.Silu,
                              bias=smalls["b1_d"][:, lidx * 16 + Mb:lidx * 16 + Mb + 1],
                              waits=[mm], inc="s_act")
                psum_claim(ctx, bank, mm1l)
        for ob in range(4):
            for th in range(2):
                bank = 2 + (ob % 2)
                pw = psum_wait(ctx, bank)
                mm = None
                for k in range(16):
                    mm = p.emit("pe", "matmul", ps[bank][:, :512],
                                w2T[:, (k * 4 + ob) * 128:(k * 4 + ob + 1) * 128],
                                y1_s[:, k * TOK + th * 512: k * TOK + (th + 1) * 512],
                                waits=([wt2, mm1l] + ([pw] if pw else []))
                                if k == 0 else (),
                                inc=("s_pe" if k == 15 else None),
                                start=(k == 0), stop=(k == 15))
                e1 = p.emit("act", "activation", u_re[:, th * 512:(th + 1) * 512],
                            ps[bank][:, :512], AF.Identity,
                            bias=smalls["b2_d"][:, lidx * 4 + ob:lidx * 4 + ob + 1],
                            waits=[mm, h_tok], inc="s_act")
                psum_claim(ctx, bank, e1)
                h_tok = p.emit("dve", "tensor_add", waits=[e1, h_tok], inc="s_dve",
                               out=h_s[:, ob * TOK + th * 512: ob * TOK + (th + 1) * 512],
                               in0=h_s[:, ob * TOK + th * 512: ob * TOK + (th + 1) * 512],
                               in1=u_re[:, th * 512:(th + 1) * 512])
        if dbg_ap is not None:
            dst_ = p.dma("pool", dbg_ap[1 + lidx], h_s, "pb", waits=[h_tok])
            h_tok = p.emit("dve", "engine_nop", waits=[h_tok, dst_], inc="s_dve")
        phase_war(p, ar)
        ar.reset(pers_mark)
    lnw = [smalls["lnlw_d"][:, j:j + 1] for j in range(4)]
    lnb = [smalls["lnlb_d"][:, j:j + 1] for j in range(4)]
    t_y = _l1_ln(p, ctx, h_s=h_s, out_s=y_s, w_ap=lnw, b_ap=lnb, silu=False,
                 sq_s=sq_s, ones_col=ones_col, war=[h_tok])
    yb = ybuf
    t_yb = p.emit("dve", "tensor_copy", waits=[t_y], inc="s_dve", out=yb, in_=y_s)
    woutT = ar.alloc([128, 4 * 4096], BF16)
    wcvt3 = ar.alloc([128, 512], BF16)
    wto = _l1_build_T(p, ctx, src_ap=dins["wout_d"], dst=woutT, nrow_t=32,
                      ncol_t=4, ld_buf=ldw[:, :512], bf_buf=wcvt3,
                      war=[("s_pe", p.counts["s_pe"]), ("s_dve", p.counts["s_dve"])])
    xb = [ar.alloc([128, 512], F32) for _ in range(2)]
    ob_ = [ar.alloc([128, 512], F32) for _ in range(2)]
    info = {}
    for t in range(8):
        for oc in range(8):
            it = t * 8 + oc
            pr = it % 2
            xw = [info[it - 2]["e"]] if it - 2 in info else list(ar.war)
            ld_x = p.dma("sp", xb[pr],
                         x_ap[t * 128:(t + 1) * 128, oc * 512:(oc + 1) * 512],
                         "x", waits=xw)
            bank = 4 + pr
            pw = psum_wait(ctx, bank)
            mm = None
            for k in range(4):
                mm = p.emit("pe", "matmul", ps[bank][:, :512],
                            yb[:, k * TOK + t * 128: k * TOK + (t + 1) * 128],
                            woutT[:, k * 4096 + oc * 512: k * 4096 + (oc + 1) * 512],
                            waits=([wto, t_yb] + ([pw] if pw else []))
                            if k == 0 else (),
                            inc=None, start=(k == 0), stop=False)
            mm = p.emit("pe", "matmul", ps[bank][:, :512], ones1,
                        boutr[0:1, oc * 512:(oc + 1) * 512],
                        waits=[lb0, t_o1], inc="s_pe", start=False, stop=True)
            stw = [info[it - 2]["st"]] if it - 2 in info else []
            e = p.emit("dve", "tensor_add", waits=[mm, ld_x] + stw, inc="s_dve",
                       out=ob_[pr], in0=ps[bank][:, :512], in1=xb[pr])
            psum_claim(ctx, bank, e)
            st = p.dma("pool", xout_ap[t * 128:(t + 1) * 128, oc * 512:(oc + 1) * 512],
                       ob_[pr], "xst", waits=[e])
            info[it] = {"e": e, "st": st}
    phase_war(p, ar, stores=("xst",))
    ar.reset(m0)
    return p.stream_all("xst")



def emit_tcache_lhsT(p, ctx, *, src_ap, dst_ap, n_mb, nkd=NKD, war=(),
                     ld_eng="sp", ld_stream="misc", bufs=None):
    ar = ctx["ar"]
    ps = ctx["psum"]
    m0 = ar.mark()
    if bufs is None:
        ldb = [ar.alloc([128, nkd * 128], F32) for _ in range(2)]
        ebuf = [ar.alloc([128, 128], BF16) for _ in range(4)]
    else:
        ldb, ebuf = bufs
    tr_last = {}
    st_of = {}
    idx = 0
    for mb in range(n_mb):
        par = mb % 2
        ld = p.dma(ld_eng, ldb[par], src_ap[mb * 128:(mb + 1) * 128, :],
                   ld_stream,
                   waits=list(war) + ([tr_last[mb - 2]]
                                      if mb - 2 in tr_last else []))
        for k in range(nkd):
            bank = 6 + (k % 2)
            pw = psum_wait(ctx, bank)
            tr = p.emit("pe", "transpose", ps[bank][:, :128],
                        ldb[par][:, k * 128:(k + 1) * 128], ctx["ident_f"],
                        waits=[ld] + ([pw] if pw else []), inc="s_pe")
            if k == nkd - 1:
                tr_last[mb] = tr
            eb = ebuf[idx % 4]
            evs = p.emit("dve", "tensor_copy",
                         waits=[tr] + ([st_of[idx - 4]]
                                       if idx - 4 in st_of else []),
                         inc="s_dve", out=eb, in_=ps[bank][:, :128])
            psum_claim(ctx, bank, evs)
            st_of[idx] = p.dma("pool",
                               dst_ap[mb][:, k * 128:(k + 1) * 128],
                               eb, "cst", waits=[evs])
            idx += 1
    phase_war(p, ar, stores=("cst",))
    ar.reset(m0)
    return p.stream_all("cst")


def emit_tcache_rhs(p, ctx, *, src_ap, dst_ap, n_kc, n_ob, n_oc, gnw_ap=None,
                    war=(), ld_eng="sp", ld_stream="misc", bufs=None):
    ar = ctx["ar"]
    ps = ctx["psum"]
    m0 = ar.mark()
    W = n_kc * 128
    if bufs is None:
        ldb = [ar.alloc([128, W], F32) for _ in range(2)]
        ebuf = [ar.alloc([128, 128], BF16) for _ in range(4)]
    else:
        ldb, ebuf = bufs
    trl = {}
    st_of = {}
    idx = 0
    for ob in range(n_ob):
        par = ob % 2
        ld = p.dma(ld_eng, ldb[par], src_ap[ob * 128:(ob + 1) * 128, :],
                   ld_stream,
                   waits=list(war) + ([trl[ob - 2]] if ob - 2 in trl else []))
        for kc in range(n_kc):
            bank = 6 + (kc % 2)
            pw = psum_wait(ctx, bank)
            tr = p.emit("pe", "transpose", ps[bank][:, :128],
                        ldb[par][:, kc * 128:(kc + 1) * 128], ctx["ident_f"],
                        waits=[ld] + ([pw] if pw else []), inc="s_pe")
            if kc == n_kc - 1:
                trl[ob] = tr
            eb = ebuf[idx % 4]
            evw = [tr] + ([st_of[idx - 4]] if idx - 4 in st_of else [])
            if gnw_ap is not None:
                ev = p.emit("dve", "tensor_scalar", waits=evw + [gnw_ap[1]],
                            inc="s_dve", out=eb,
                            in0=ps[bank][:, :128],
                            scalar1=gnw_ap[0][kc][:, 0:1], scalar2=None,
                            op0=ALU.mult)
            else:
                ev = p.emit("dve", "tensor_copy", waits=evw, inc="s_dve",
                            out=eb, in_=ps[bank][:, :128])
            psum_claim(ctx, bank, ev)
            st_of[idx] = p.dma("pool",
                               dst_ap[ob // 4, kc][:, (ob % 4) * 128:
                                                   (ob % 4 + 1) * 128],
                               eb, "cst", waits=[ev])
            idx += 1
    phase_war(p, ar, stores=("cst",))
    ar.reset(m0)
    return p.stream_all("cst")


def emit_allgather(p, *, in_ap, out_ap, ncores, waits=()):
    ag = p.raw("pool", lambda g: g.collective_compute(
        "AllGather", ALU.bypass, replica_groups=[list(range(ncores))],
        ins=[in_ap.opt()], outs=[out_ap.opt()]),
        waits=list(waits), inc="s_cc")
    return ag


NC = 8


def _setup(nc, p):
    ident_d = nc.dram_tensor("ident_d", [128, 128], BF16, kind="ExternalInput").ap()
    identf_d = nc.dram_tensor("identf_d", [128, 128], F32, kind="ExternalInput").ap()
    ident_s = nc.alloc_sbuf_tensor("ident_s", [128, 128], BF16)
    identf_s = nc.alloc_sbuf_tensor("identf_s", [128, 128], F32)
    ctx = make_ctx(nc, p)
    ctx["ident_bf"] = ident_s.ap()
    ctx["ident_f"] = identf_s.ap()
    i1 = p.dma("sp", ident_s.ap(), ident_d, "misc")
    i2 = p.dma("sp", identf_s.ap(), identf_d, "misc")
    ctx["ar"].war = [i1, i2]
    return ctx


def build_launch_ffn(ncores=NC):
    nc = bass.Bass("TRN2", target_bir_lowering=False, debug=False)
    p = Prog(nc)
    ctx = _setup(nc, p)
    ar = ctx["ar"]
    E = lambda n, s, d: nc.dram_tensor(n, s, d, kind="ExternalInput").ap()
    x_in = E("x", [TOK, D], F32)
    qw_gu = E("qw_gu", [512, 2816], U32)
    qz_gu = E("qz_gu", [32, 352], U32)
    sc_gu = E("sc_gu", [32, 2816], F32)
    gidx_gu = E("gidx_gu", [4096], I32)
    qw_d = E("qw_d", [176, 4096], U32)
    qz_d = E("qz_d", [86, 512], U32)
    sc_d = E("sc_d", [86, 4096], F32)
    gidx_d = E("gidx_d", [1408], I32)
    mask_in = E("mask", [128, 1], U32)
    pow2_in = E("pow2", [128, 1], F32)
    iota32 = E("iota32", [32, 1], F32)
    iota86 = E("iota86", [86, 1], F32)
    x_out = nc.dram_tensor("x_out", [TOK, D], F32, kind="ExternalOutput").ap()

    wgu_slice = nc.dram_tensor("wgu_sl", [11, 128, 8192], BF16).ap()
    wgu_all = nc.dram_tensor("wgu_all", [ncores * 11 * 128, 8192], BF16).ap()
    wd_slice = nc.dram_tensor("wd_sl", [8, 11, 128, 512], BF16).ap()
    wd_all = nc.dram_tensor("wd_all", [ncores * 8 * 11 * 128, 512], BF16).ap()
    h_spill = nc.dram_tensor("h_spill", [86, 128, TOK], BF16).ap()

    m0 = ar.mark()
    mask_s = ar.alloc([128, 1], U32)
    pow2_s = ar.alloc([128, 1], F32)
    iota32_s = ar.alloc([32, 1], F32)
    iota86_s = ar.alloc([86, 1], F32)
    oneh = ar.alloc([32, NKD * 128], BF16)
    oneh_d = ar.alloc([86, 11 * 128], BF16)
    rep32 = ar.alloc([32, 128], I32)
    rep86 = ar.alloc([86, 128], I32)
    t1 = p.dma("sp", mask_s, mask_in, "misc")
    t1b = p.dma("sp", pow2_s, pow2_in, "misc")
    t2 = p.dma("sp", iota32_s, iota32, "misc")
    t3 = p.dma("sp", iota86_s, iota86, "misc")
    war0 = list(ar.war)
    ar.war = war0 + [t1, t1b, t2, t3]
    ctx["oneh_tok"] = emit_onehot_tables(p, ctx, gidx_ap=gidx_gu, iota_ap=iota32_s,
                                         oneh=oneh, n_groups=32, nkd=NKD, rep=rep32)
    ctx["oneh_d_tok"] = emit_onehot_tables(p, ctx, gidx_ap=gidx_d, iota_ap=iota86_s,
                                           oneh=oneh_d, n_groups=86, nkd=11, rep=rep86)
    ar.war = ar.war + [ctx["oneh_tok"], ctx["oneh_d_tok"]]
    emit_dequant_gu(p, ctx, qw_ap=qw_gu, qz_ap=qz_gu, sc_ap=sc_gu, oneh=oneh,
                    mask_ap=mask_s, pow2_ap=pow2_s, out_slice_ap=wgu_slice)
    gu_sts = p.stream_all("cst")
    ag1 = emit_allgather(p, in_ap=wgu_slice.rearrange("a p m -> (a p) m"),
                         out_ap=wgu_all, ncores=ncores, waits=gu_sts)
    emit_dequant_wd(p, ctx, qw_ap=qw_d, sc_full_ap=sc_d, qz_full_ap=qz_d,
                    oneh_d=oneh_d, mask_ap=mask_s, pow2_ap=pow2_s,
                    out_slice_ap=wd_slice)
    wd_sts = p.stream_all("cst")
    ag2 = emit_allgather(p, in_ap=wd_slice.rearrange("a b p m -> (a b p) m"),
                         out_ap=wd_all, ncores=ncores, waits=wd_sts)
    ar.reset(m0)
    ar.war = war0 + [p.tok("s_pe"), p.tok("s_act"), p.tok("s_dve"), ag1, ag2]

    xnT = ar.alloc([128, NKD * TOK], BF16)
    k_ready, ln_last = emit_ln_transpose(p, ctx, x_ap=x_in, xnT=xnT, x_ready=())
    wgu_v = wgu_all.rearrange("(a p) m -> a p m", p=128)
    h_toks = emit_ffn_mm1(p, ctx, wcache_ap=wgu_v, xnT=xnT, h_out_ap=h_spill,
                          x_ready_toks=[ln_last, ag1], n_pairs=86)
    wd_v = wd_all.rearrange("(r o k p) m -> r o k p m", o=8, k=11, p=128)
    st_toks = emit_ffn_mm2(p, ctx,
                           wd_cache_ap=lambda oc, k: wd_v[k // 11, oc, k % 11],
                           h_ap=h_spill, x_ap=x_in, xout_ap=x_out,
                           h_ready_toks=h_toks + [ag2], nk=86, w_ready=[ag2])
    p.barrier("pool", st_toks)
    p.replay(alloc_sems(nc, prog_sems(p)))
    return nc


def build_launch_mamba(ncores=NC):
    nc = bass.Bass("TRN2", target_bir_lowering=False, debug=False)
    p = Prog(nc)
    ctx = _setup(nc, p)
    ar = ctx["ar"]
    E = lambda n, s, d: nc.dram_tensor(n, s, d, kind="ExternalInput").ap()
    x_in = E("x", [TOK, D], F32)
    ip_rows = E("ip_rows", [19 * 128, 4096], F32)
    op_cols = E("op_cols", [8192, 512], F32)
    gnw_sl = E("gnw_sl", [128, 8], F32)
    dins = {k: E(k, s, dt) for k, s, dt in (
        ("norm_w_g", [128, NKD], F32), ("convw_d", [128, 320], F32),
        ("convb_d", [128, 80], F32), ("dt_bias_d", [128, 1], F32),
        ("A_log_d", [128, 1], F32), ("dp_row_d", [1, 8192], F32),
        ("gnw_d", [128, 64], F32), ("flag_d", [128, 1], F32),
        ("utri_d", [128, 128], BF16))}
    x_out = nc.dram_tensor("x_out", [TOK, D], F32, kind="ExternalOutput").ap()

    ipc_slice = nc.dram_tensor("ipc_sl", [19, 128, 4096], BF16).ap()
    ipc_all = nc.dram_tensor("ipc_all", [ncores * 19 * 128, 4096], BF16).ap()
    opc_slice = nc.dram_tensor("opc_sl", [8, 8, 128, 512], BF16).ap()
    opc_all = nc.dram_tensor("opc_all", [ncores * 8 * 8 * 128, 512], BF16).ap()

    m0 = ar.mark()
    gnw_s = ar.alloc([128, 8], F32)
    g1 = p.dma("sp", gnw_s, gnw_sl, "misc", waits=ar.war)
    war0 = list(ar.war)
    cldb = [ar.alloc([128, 4096], F32) for _ in range(2)]
    ccvb = [ar.alloc([128, 4096], BF16) for _ in range(2)]
    chunks = cache_chunks(ip_rows=ip_rows, ipc_slice=ipc_slice,
                          op_cols=op_cols, opc_slice=opc_slice)
    hook = make_cache_hook(p, chunks=chunks, cldb=cldb, ccvb=ccvb,
                           gnw_s=gnw_s, g1=g1, war=ar.war)
    for n in range(len(chunks) + 2):
        hook(n)
    sts = p.stream_all("cs")
    ag1 = emit_allgather(p, in_ap=ipc_slice.rearrange("a p m -> (a p) m"),
                         out_ap=ipc_all, ncores=ncores, waits=sts)
    ag2 = emit_allgather(p, in_ap=opc_slice.rearrange("a b p m -> (a b p) m"),
                         out_ap=opc_all, ncores=ncores, waits=sts)
    ar.war = ar.war + sts
    ar.reset(m0)
    ar.war = war0 + [p.tok("s_pe"), p.tok("s_act"), p.tok("s_dve"), ag1, ag2]

    dd = mamba_dram(nc, ncores=ncores)
    ipc_v = ipc_all.rearrange("(a p) m -> a p m", p=128)
    opc_v = opc_all.rearrange("(r o k p) m -> r o k p m", o=8, k=8, p=128)
    st = emit_mamba_sublayer(p, ctx, nc, x_ap=x_in, xout_ap=x_out,
                             inproj_cache_ap=ipc_v,
                             outproj_cache_ap=lambda oc, k: opc_v[k // 8, oc, k % 8],
                             dins=dins, dd=dd, ncores=ncores, op_w_ready=[ag2])
    p.barrier("pool", st)
    p.replay(alloc_sems(nc, prog_sems(p)))
    return nc


def build_launch_l1ffn(ncores=NC):
    nc = bass.Bass("TRN2", target_bir_lowering=False, debug=False)
    p = Prog(nc)
    ctx = _setup(nc, p)
    ar = ctx["ar"]
    E = lambda n, s, d: nc.dram_tensor(n, s, d, kind="ExternalInput").ap()
    x_in = E("x", [TOK, D], F32)
    w0_cols = E("w0_cols", [4096, 512], F32)
    a_d = E("a_d", [64, 4096], F32)
    b_d = E("b_d", [4, 4096, 16], F32)
    b0_d = E("b0_d", [1, 4096], F32)
    mask_d = E("mask_d", [4, TOK, 4096], BF16)
    x_out = nc.dram_tensor("x_out", [TOK, D], F32, kind="ExternalOutput").ap()
    w0_slice = nc.dram_tensor("w0_sl", [8, 4, 128, 512], BF16).ap()
    w0_all = nc.dram_tensor("w0_all", [ncores * 8 * 4 * 128, 512], BF16).ap()
    m0 = ar.mark()
    war0 = list(ar.war)
    cldb = [ar.alloc([128, 4096], F32) for _ in range(2)]
    ccvb = [ar.alloc([128, 4096], BF16) for _ in range(2)]
    chunks = cache_chunks(w0_cols=w0_cols, w0_slice=w0_slice)
    hook = make_cache_hook(p, chunks=chunks, cldb=cldb, ccvb=ccvb,
                           war=ar.war)
    for n in range(len(chunks) + 2):
        hook(n)
    sts = p.stream_all("cs")
    ag = emit_allgather(p, in_ap=w0_slice.rearrange("a b p m -> (a b p) m"),
                        out_ap=w0_all, ncores=ncores, waits=sts)
    ar.reset(m0)
    ar.war = war0 + [p.tok("s_pe"), p.tok("s_act"), p.tok("s_dve"), ag] + sts
    w0_v = w0_all.rearrange("(r o k p) m -> r o k p m", o=8, k=4, p=128)
    st = emit_l1ffn(p, ctx, x_ap=x_in, xout_ap=x_out,
                    w0_cache_ap=lambda oc: [
                        w0_v[r, oc].rearrange("k p m -> p k m")
                        for r in range(ncores)],
                    a_d=a_d, b_d=b_d, b0_d=b0_d, mask_d=mask_d)
    p.barrier("pool", st)
    p.replay(alloc_sems(nc, prog_sems(p)))
    return nc


def build_launch_l1attn(ncores=NC):
    nc = bass.Bass("TRN2", target_bir_lowering=False, debug=False)
    p = Prog(nc)
    ctx = _setup(nc, p)
    E = lambda n, s, d: nc.dram_tensor(n, s, d, kind="ExternalInput").ap()
    x_in = E("x", [TOK, D], F32)
    specs = dict(win_d=([512, 4096], F32), bin_d=([128, 4], F32),
                 l1w_d=([128, 16], F32), l1b_d=([128, 16], F32),
                 l2w_d=([128, 16], F32), l2b_d=([128, 16], F32),
                 scl_d=([128, 16], F32), phzr_d=([128, 16], F32),
                 phzi_d=([128, 16], F32), lcr_d=([128, 16], F32),
                 lci_d=([128, 16], F32), w1_d=([4, 2048, 512], F32),
                 b1_d=([128, 64], F32), w2_d=([4, 512, 2048], F32),
                 b2_d=([128, 16], F32), lnlw_d=([128, 4], F32),
                 lnlb_d=([128, 4], F32), wout_d=([4096, 512], F32), rksel_d=([128, 8], F32),
                 bout_d=([1, 4096], F32), flag_d=([128, 1], F32))
    dins = {k: E(k, s, dt) for k, (s, dt) in specs.items()}
    x_out = nc.dram_tensor("x_out", [TOK, D], F32, kind="ExternalOutput").ap()
    dd1 = dict(carr_b_d=nc.dram_tensor("carr_b", [128, 8], F32).ap(),
               carr_a_d=nc.dram_tensor("carr_a", [ncores * 128, 8], F32).ap())
    st = emit_l1attn(p, ctx, nc, x_ap=x_in, xout_ap=x_out, dins=dins, dd1=dd1,
                     ncores=ncores)
    p.barrier("pool", st)
    p.replay(alloc_sems(nc, prog_sems(p)))
    return nc


def emit_ffn1_ksharded(p, ctx, nc, *, x_ap, xout_ap, wgu_slice, wd_slice,
                       xn_bounce, xn_all, h_spill, part_d, rs_d, ncores=8,
                       hook=None, dq_rdy=(), dqd_rdy=(), post_ag_cb=None):
    ar = ctx["ar"]
    ps = ctx["psum"]
    m0 = ar.mark()
    mln = ar.mark()
    xnT = ar.alloc([128, NKD * TOK], BF16)
    _, ln_last = emit_ln_transpose(p, ctx, x_ap=x_ap, xnT=xnT, x_ready=())
    spx = p.dma("pool", xn_bounce.rearrange("(k p) t -> p k t", p=128),
                xnT, "pb", waits=[ln_last])
    agx = p.raw("pool", lambda g: g.collective_compute(
        "AllGather", ALU.bypass, replica_groups=[list(range(ncores))],
        ins=[xn_bounce.opt()], outs=[xn_all.opt()]), waits=[spx], inc="s_cc")
    if post_ag_cb is not None:
        post_ag_cb()
    xa_v = xn_all.rearrange("(r k p) t -> r k p t", k=NKD, p=128)

    ar.reset(mln)
    m_mm1 = ar.mark()
    xts = [ar.alloc([128, NKD * 512], BF16) for _ in range(3)]
    wbufs = [ar.alloc([128, 2 * NKD * 128], BF16) for _ in range(2)]
    silu_t = [ar.alloc([128, 512], F32) for _ in range(2)]
    h_t = [ar.alloc([128, 512], BF16) for _ in range(2)]
    pe_done = {}
    hst = {}
    xl = {}
    hooked = [0]
    gi = 0
    widx = 0

    def load_xt(tt):
        par = tt % 3
        r, h = tt // 2, tt % 2
        src3 = xn_all.rearrange("(r k p) t -> r k p t", k=NKD, p=128)[
            r, :, :, h * 512:(h + 1) * 512].rearrange("k p t -> p k t")
        w = [agx] + ([("s_pe", xl[tt - 3])] if tt - 3 in xl else list(ar.war))
        return p.dma("sp", xts[par], src3, "x", waits=w)

    xtok = {0: load_xt(0), 1: load_xt(1), 2: load_xt(2)}
    for tg in range(8):
        for j in range(11):
            par = widx % 2
            ww = list(dq_rdy)
            if widx < 2:
                ww += ar.war
            else:
                ww.append(("s_pe", pe_done[widx - 2]))
            wtok = p.dma("sp", wbufs[par], wgu_slice[j], "w", waits=ww)
            for ti in range(2):
                tt = tg * 2 + ti
                tpar = tt % 2
                banks = {}
                for gu in range(2):
                    bank = gi % 8
                    gi += 1
                    banks[gu] = bank
                    pw = psum_wait(ctx, bank)
                    for k in range(NKD):
                        p.emit("pe", "matmul", ps[bank][:, :512],
                               wbufs[par][:, (gu * NKD + k) * 128:
                                          (gu * NKD + k + 1) * 128],
                               xts[tt % 3][:, k * 512:(k + 1) * 512],
                               waits=([wtok, xtok[tt]] +
                                      ([pw] if pw else []))
                               if k == 0 else (),
                               inc=("s_pe" if k == NKD - 1 else None),
                               start=(k == 0), stop=(k == NKD - 1))
                idx = j * 16 + tt
                ta = p.emit("act", "activation", silu_t[tpar],
                            ps[banks[0]][:, :512], AF.Silu,
                            waits=[("s_pe", p.counts["s_pe"])] +
                            ([("s_dve", hst[(widx - 1, ti)][1])]
                             if (widx - 1, ti) in hst else []),
                            inc="s_act")
                td = p.emit("dve", "tensor_mul",
                            waits=[ta] + ([hst[(widx - 1, ti)][0]]
                                          if (widx - 1, ti) in hst else []),
                            inc="s_dve", out=h_t[tpar], in0=silu_t[tpar],
                            in1=ps[banks[1]][:, :512])
                psum_claim(ctx, banks[0], ta)
                psum_claim(ctx, banks[1], td)
                st = p.dma("sp", h_spill[j][:, tt * 512:(tt + 1) * 512],
                           h_t[tpar], "hs", waits=[td])
                hst[(widx, ti)] = (st, td[1])
            pe_done[widx] = p.counts["s_pe"]
            widx += 1
            if hook is not None:
                hook(hooked[0])
                hooked[0] += 1
        if tg < 7:
            for nt in (2 * tg + 3, 2 * tg + 4):
                if nt < 16 and nt not in xtok:
                    xl[nt - 3] = pe_done[widx - 1]
                    xtok[nt] = load_xt(nt)
    if hook is not None:
        for _ in range(4):
            hook(hooked[0])
            hooked[0] += 1
    h_rdy = p.stream_all("hs")
    phase_war(p, ar, stores=("hs",))
    ar.reset(m_mm1)

    wb = [ar.alloc([128, 512], BF16) for _ in range(3)]
    hb = [ar.alloc([128, 1024], BF16) for _ in range(3)]
    ob = [ar.alloc([128, 512], F32) for _ in range(8)]
    h_v2 = h_spill.rearrange("j p (tb u) -> j p tb u", u=1024)
    pe_hist = {}
    ev_hist = {}
    it = 0
    for oc in range(8):
        for tbg in range(8):
            tk = None
            for j in range(11):
                kb = it % 3
                wwar = list(dqd_rdy)
                if it < 3:
                    wwar += ar.war
                else:
                    wwar.append(("s_pe", pe_hist[it - 3]))
                wtok = p.dma("sp", wb[kb], wd_slice[oc, j], "w", waits=wwar)
                htok = p.dma("act", hb[kb],
                             h_v2[j, :, tbg], "ha", waits=h_rdy + wwar)
                for t8 in range(8):
                    pe_waits = [wtok, htok]
                    if j == 0:
                        pw = psum_wait(ctx, t8)
                        if pw is not None:
                            pe_waits.append(pw)
                    tk = p.emit("pe", "matmul", ps[t8][:, :512],
                                hb[kb][:, t8 * 128:(t8 + 1) * 128], wb[kb],
                                waits=pe_waits,
                                inc=("s_pe" if t8 == 7 else None),
                                start=(j == 0), stop=(j == 10))
                pe_hist[it] = tk[1]
                it += 1
            for t8 in range(8):
                gt = tbg * 8 + t8
                evw = [tk] + ([ev_hist[(oc - 1, t8)]] if oc >= 1 else [])
                ev = p.emit("dve", "tensor_copy", waits=evw, inc="s_dve",
                            out=ob[t8], in_=ps[t8][:, :512])
                psum_claim(ctx, t8, ev)
                ev_hist[(oc, t8)] = p.dma(
                    "sp", part_d[gt * 128:(gt + 1) * 128,
                                 oc * 512:(oc + 1) * 512],
                    ob[t8], "xst", waits=[ev])
    part_sts = p.stream_all("xst")
    rs = p.raw("pool", lambda g: g.collective_compute(
        "ReduceScatter", ALU.add, replica_groups=[list(range(ncores))],
        ins=[part_d.opt()], outs=[rs_d.opt()]), waits=part_sts, inc="s_cc")

    xb = [ar.alloc([128, 1024], F32) for _ in range(2)]
    rb = [ar.alloc([128, 1024], F32) for _ in range(2)]
    fob = [ar.alloc([128, 1024], F32) for _ in range(2)]
    fin = {}
    for ft in range(32):
        t, c = ft // 4, ft % 4
        fpar = ft % 2
        fw = [fin[ft - 2][0]] if ft - 2 in fin else []
        lx = p.dma("sp", xb[fpar],
                   x_ap[t * 128:(t + 1) * 128, c * 1024:(c + 1) * 1024], "x",
                   waits=fw)
        lr = p.dma("act", rb[fpar],
                   rs_d[t * 128:(t + 1) * 128, c * 1024:(c + 1) * 1024], "ha",
                   waits=[rs] + fw)
        sw = [fin[ft - 2][1]] if ft - 2 in fin else []
        e = p.emit("dve", "scalar_tensor_tensor", waits=[lx, lr] + sw,
                   inc="s_dve",
                   out=fob[fpar], in0=rb[fpar], scalar=0.5, in1=xb[fpar],
                   op0=ALU.mult, op1=ALU.add)
        st = p.dma("sp",
                   xout_ap[t * 128:(t + 1) * 128, c * 1024:(c + 1) * 1024],
                   fob[fpar], "xst", waits=[e])
        fin[ft] = (e, st)
    phase_war(p, ar, stores=("xst", "hs"))
    ar.reset(m0)
    return p.stream_all("xst")


def make_cache_hook(p, *, chunks, cldb, ccvb, gnw_s=None, g1=None, war=()):
    cw = {"ld": {}, "cv": {}, "st": {}}

    def hook(n):
        m = n - 2
        if 0 <= m < len(chunks):
            kind, csrc, cdst, ckc = chunks[m]
            wdt = 4096 if kind == "ip" else 1024
            par = m % 2
            cvw = [cw["ld"][m]] + ([cw["st"][m - 2]] if m - 2 in cw["st"]
                                   else [])
            if kind == "op":
                cv = None
                for h in range(2):
                    cv = p.emit("dve", "tensor_scalar",
                                waits=cvw + [g1], inc="s_dve",
                                out=ccvb[par][:, h * 512:(h + 1) * 512],
                                in0=cldb[par][:, h * 512:(h + 1) * 512],
                                scalar1=gnw_s[:, ckc + h:ckc + h + 1],
                                scalar2=None, op0=ALU.mult)
            else:
                cv = None
                for h in range(0, wdt, 1024):
                    cv = p.emit("dve", "tensor_copy",
                                waits=(cvw if h == 0 else [cv]), inc="s_dve",
                                out=ccvb[par][:, h:h + 1024],
                                in_=cldb[par][:, h:h + 1024])
            cw["cv"][m] = cv
            if kind == "ip":
                cw["st"][m] = p.dma("act", cdst, ccvb[par][:, :wdt], "cs",
                                    waits=[cv])
            else:
                cw["st"][m] = p.dma("act",
                                    cdst.rearrange("k c m -> c k m"),
                                    ccvb[par][:, :1024], "cs", waits=[cv])
        if n < len(chunks):
            kind, csrc, cdst, ckc = chunks[n]
            wdt = 4096 if kind == "ip" else 1024
            par = n % 2
            cw["ld"][n] = p.dma("act", cldb[par][:, :wdt], csrc, "cl",
                                waits=list(war) +
                                ([cw["cv"][n - 2]] if n - 2 in cw["cv"]
                                 else []))
    return hook


def cache_chunks(*, ip_rows=None, ipc_slice=None, op_cols=None,
                 opc_slice=None, w0_cols=None, w0_slice=None):
    chunks = []
    if ip_rows is not None:
        for mb in range(19):
            chunks.append(("ip", ip_rows[mb * 128:(mb + 1) * 128, :],
                           ipc_slice[mb], None))
    if op_cols is not None:
        for oc in range(8):
            for kc in range(0, 8, 2):
                chunks.append(("op",
                               op_cols[(oc * 8 + kc) * 128:
                                       (oc * 8 + kc + 2) * 128, :]
                               .rearrange("(k c) m -> c k m", k=2),
                               opc_slice[oc, kc:kc + 2], kc))
    if w0_cols is not None:
        for oc in range(8):
            for kc in range(0, 4, 2):
                chunks.append(("w0",
                               w0_cols[(oc * 4 + kc) * 128:
                                       (oc * 4 + kc + 2) * 128, :]
                               .rearrange("(k c) m -> c k m", k=2),
                               w0_slice[oc, kc:kc + 2], None))
    return chunks


def build_fused(ncores=NC):
    nc = bass.Bass("TRN2", target_bir_lowering=False, debug=False)
    p = Prog(nc)
    ctx = _setup(nc, p)
    ar = ctx["ar"]
    E = lambda n, s, d: nc.dram_tensor(n, s, d, kind="ExternalInput").ap()
    x_in = E("x", [TOK, D], F32)
    qw_gu = E("qw_gu", [512, 2816], U32)
    qz_gu = E("qz_gu", [32, 352], U32)
    sc_gu = E("sc_gu", [32, 2816], F32)
    gidx_gu = E("gidx_gu", [4096], I32)
    qw_d = E("qw_d", [176, 4096], U32)
    qz_d = E("qz_d", [86, 512], U32)
    sc_d = E("sc_d", [86, 4096], F32)
    gidx_d = E("gidx_d", [1408], I32)
    mask_in = E("mask", [128, 1], U32)
    pow2_in = E("pow2", [128, 1], F32)
    iota32 = E("iota32", [32, 1], F32)
    iota86 = E("iota86", [86, 1], F32)
    ip_rows = E("ip_rows", [19 * 128, 4096], F32)
    op_cols = E("op_cols", [8192, 512], F32)
    gnw_sl = E("gnw_sl", [128, 8], F32)
    mdins = {k: E(k, s, dt) for k, s, dt in (
        ("norm_w_g", [128, NKD], F32), ("convw_d", [128, 320], F32),
        ("convb_d", [128, 80], F32), ("dt_bias_d", [128, 1], F32),
        ("A_log_d", [128, 1], F32), ("dp_row_d", [1, 8192], F32),
        ("gnw_d", [128, 64], F32), ("flag_d", [128, 1], F32),
        ("utri_d", [128, 128], BF16))}
    w0_cols = E("w0_cols", [4096, 512], F32)
    a_d = E("a_d", [64, 4096], F32)
    b_d = E("b_d", [4, 4096, 16], F32)
    b0_d = E("b0_d", [1, 4096], F32)
    mask_d = E("mask_d", [4, TOK, 4096], BF16)
    aspecs = dict(win_d=([512, 4096], F32), bin_d=([128, 4], F32),
                  l1w_d=([128, 16], F32), l1b_d=([128, 16], F32),
                  l2w_d=([128, 16], F32), l2b_d=([128, 16], F32),
                  scl_d=([128, 16], F32), phzr_d=([128, 16], F32),
                  phzi_d=([128, 16], F32), lcr_d=([128, 16], F32),
                  lci_d=([128, 16], F32), w1_d=([4, 2048, 512], F32),
                  b1_d=([128, 64], F32), w2_d=([4, 512, 2048], F32),
                  b2_d=([128, 16], F32), lnlw_d=([128, 4], F32),
                  lnlb_d=([128, 4], F32), wout_d=([4096, 512], F32), rksel_d=([128, 8], F32),
                  bout_d=([1, 4096], F32))
    adins = {k: E(k, s, dt) for k, (s, dt) in aspecs.items()}
    adins["flag_d"] = mdins["flag_d"]
    x_out = nc.dram_tensor("x_out", [TOK, D], F32, kind="ExternalOutput").ap()

    wgu_slice = nc.dram_tensor("wgu_sl", [11, 128, 8192], BF16).ap()
    wgu_all = nc.dram_tensor("wgu_all", [11 * ncores * 128, 8192], BF16).ap()
    wd_slice = nc.dram_tensor("wd_sl", [8, 11, 128, 512], BF16).ap()
    wd_all = nc.dram_tensor("wd_all", [ncores * 8 * 11 * 128, 512], BF16).ap()
    h_spill = nc.dram_tensor("h_spill", [86, 128, TOK], BF16).ap()
    ipc_slice = nc.dram_tensor("ipc_sl", [19, 128, 4096], BF16).ap()
    ipc_all = nc.dram_tensor("ipc_all", [ncores * 19 * 128, 4096], BF16).ap()
    opc_slice = nc.dram_tensor("opc_sl", [8, 8, 128, 512], BF16).ap()
    opc_all = nc.dram_tensor("opc_all", [ncores * 8 * 8 * 128, 512], BF16).ap()
    w0_slice = nc.dram_tensor("w0_sl", [8, 4, 128, 512], BF16).ap()
    w0_all = nc.dram_tensor("w0_all", [ncores * 8 * 4 * 128, 512], BF16).ap()
    xnb_d = nc.dram_tensor("xnb", [NKD * 128, TOK], BF16).ap()
    xna_d = nc.dram_tensor("xna", [ncores * NKD * 128, TOK], BF16).ap()
    hk_d = nc.dram_tensor("hk", [11, 128, 8192], BF16).ap()
    part_d = nc.dram_tensor("part", [8192, 4096], F32).ap()
    rs_d = nc.dram_tensor("rsout", [1024, 4096], F32).ap()
    xA = nc.dram_tensor("xA", [TOK, D], F32).ap()
    xB = nc.dram_tensor("xB", [TOK, D], F32).ap()
    dd = mamba_dram(nc, ncores=ncores)
    dd1 = dict(carr_b_d=nc.dram_tensor("carr_b", [128, 8], F32).ap(),
               carr_a_d=nc.dram_tensor("carr_a", [ncores * 128, 8], F32).ap())

    wgu_v2 = wgu_all.rearrange("(j r p) m -> j r p m", r=ncores, p=128)
    wd_v = wd_all.rearrange("(r o k p) m -> r o k p m", o=8, k=11, p=128)
    ipc_v = ipc_all.rearrange("(a p) m -> a p m", p=128)
    opc_v = opc_all.rearrange("(r o k p) m -> r o k p m", o=8, k=8, p=128)
    w0_v = w0_all.rearrange("(r o k p) m -> r o k p m", o=8, k=4, p=128)

    def gu_pairs(ag_toks=None):
        out = []
        for j in range(11):
            for r in range(ncores):
                gp = 11 * r + j
                if gp < 86:
                    out.append((gp, wgu_v2[j, r],
                                (ag_toks[j],) if ag_toks else ()))
        return out

    m0 = ar.mark()
    mask_s = ar.alloc([128, 1], U32)
    pow2_s = ar.alloc([128, 1], F32)
    iota32_s = ar.alloc([32, 1], F32)
    iota86_s = ar.alloc([86, 1], F32)
    oneh = ar.alloc([32, NKD * 128], BF16)
    oneh_d = ar.alloc([86, 11 * 128], BF16)
    rep32 = ar.alloc([32, 128], I32)
    rep86 = ar.alloc([86, 128], I32)
    t1 = p.dma("sp", mask_s, mask_in, "misc")
    t1b = p.dma("sp", pow2_s, pow2_in, "misc")
    t2 = p.dma("sp", iota32_s, iota32, "misc")
    t3 = p.dma("sp", iota86_s, iota86, "misc")
    war0 = list(ar.war)
    ar.war = war0 + [t1, t1b, t2, t3]
    ctx["oneh_tok"] = emit_onehot_tables(p, ctx, gidx_ap=gidx_gu, iota_ap=iota32_s,
                                         oneh=oneh, n_groups=32, nkd=NKD, rep=rep32)
    ctx["oneh_d_tok"] = emit_onehot_tables(p, ctx, gidx_ap=gidx_d, iota_ap=iota86_s,
                                           oneh=oneh_d, n_groups=86, nkd=11, rep=rep86)
    ar.war = ar.war + [ctx["oneh_tok"], ctx["oneh_d_tok"]]
    emit_dequant_gu(p, ctx, qw_ap=qw_gu, qz_ap=qz_gu, sc_ap=sc_gu, oneh=oneh,
                    mask_ap=mask_s, pow2_ap=pow2_s, out_slice_ap=wgu_slice)
    gu_sts = p.stream_all("cst")
    emit_dequant_wd(p, ctx, qw_ap=qw_d, sc_full_ap=sc_d, qz_full_ap=qz_d,
                    oneh_d=oneh_d, mask_ap=mask_s, pow2_ap=pow2_s,
                    out_slice_ap=wd_slice)
    wd_sts = p.stream_all("cst")
    ar.reset(m0)
    ar.war = ar.war + wd_sts
    cldb = [ar.alloc([128, 4096], F32) for _ in range(2)]
    ccvb = [ar.alloc([128, 4096], BF16) for _ in range(2)]
    gnw_s = ar.alloc([128, 8], F32)
    g1 = p.dma("pool", gnw_s, gnw_sl, "pb", waits=ar.war)
    war_early = list(ar.war)
    chunks = cache_chunks(ip_rows=ip_rows, ipc_slice=ipc_slice,
                          op_cols=op_cols, opc_slice=opc_slice,
                          w0_cols=w0_cols, w0_slice=w0_slice)
    cache_hook = make_cache_hook(p, chunks=chunks, cldb=cldb, ccvb=ccvb,
                                 gnw_s=gnw_s, g1=g1, war=war_early)
    ncch = len(chunks)
    cs_final = [("cs0", 16 * ((ncch + 1) // 2)), ("cs1", 16 * (ncch // 2))]
    agd = {}

    def post_ag():
        agd["gu"] = [emit_allgather(p, in_ap=wgu_slice[j],
                                    out_ap=wgu_all[j * ncores * 128:
                                                   (j + 1) * ncores * 128, :],
                                    ncores=ncores, waits=gu_sts)
                     for j in range(11)]
        agd["wd"] = emit_allgather(
            p, in_ap=wd_slice.rearrange("a b p m -> (a b p) m"),
            out_ap=wd_all, ncores=ncores, waits=wd_sts)
        agd["ip"] = emit_allgather(
            p, in_ap=ipc_slice.rearrange("a p m -> (a p) m"),
            out_ap=ipc_all, ncores=ncores, waits=cs_final)
        agd["op"] = emit_allgather(
            p, in_ap=opc_slice.rearrange("a b p m -> (a b p) m"),
            out_ap=opc_all, ncores=ncores, waits=cs_final)
        agd["w0"] = emit_allgather(
            p, in_ap=w0_slice.rearrange("a b p m -> (a b p) m"),
            out_ap=w0_all, ncores=ncores, waits=cs_final)

    emit_ffn1_ksharded(p, ctx, nc, x_ap=x_in, xout_ap=xA,
                       wgu_slice=wgu_slice, wd_slice=wd_slice,
                       xn_bounce=xnb_d, xn_all=xna_d, h_spill=hk_d,
                       part_d=part_d, rs_d=rs_d, ncores=ncores,
                       hook=cache_hook, dq_rdy=gu_sts, dqd_rdy=wd_sts,
                       post_ag_cb=post_ag)
    ag3, ag4, ag5 = agd["ip"], agd["op"], agd["w0"]
    ar.war = ar.war + p.stream_all("cs")
    ar.reset(m0)

    def ffn_sub(xi, xo):
        m = ar.mark()
        xnT = ar.alloc([128, NKD * TOK], BF16)
        _, ln_last = emit_ln_transpose(p, ctx, x_ap=xi, xnT=xnT, x_ready=())
        h_toks = emit_ffn_mm1(p, ctx, xnT=xnT, h_out_ap=h_spill,
                              x_ready_toks=[ln_last], pairs=gu_pairs())
        emit_ffn_mm2(p, ctx,
                     wd_cache_ap=lambda oc, k: wd_v[k // 11, oc, k % 11],
                     h_ap=h_spill, x_ap=xi, xout_ap=xo,
                     h_ready_toks=h_toks, nk=86)
        ar.reset(m)

    def l1ffn_sub(xi, xo):
        emit_l1ffn(p, ctx, x_ap=xi, xout_ap=xo,
                   w0_cache_ap=lambda oc: [
                       w0_v[r, oc].rearrange("k p m -> p k m")
                       for r in range(ncores)],
                   a_d=a_d, b_d=b_d, b0_d=b0_d, mask_d=mask_d)

    ar.war = ar.war + [ag3, ag4]
    emit_mamba_sublayer(p, ctx, nc, x_ap=xA, xout_ap=xB,
                        inproj_cache_ap=ipc_v,
                        outproj_cache_ap=lambda oc, k: opc_v[k // 8, oc, k % 8],
                        dins=mdins, dd=dd, ncores=ncores, op_w_ready=[ag4])
    ffn_sub(xB, xA)
    ar.war = ar.war + [ag5]
    l1ffn_sub(xA, xB)
    emit_l1attn(p, ctx, nc, x_ap=xB, xout_ap=xA,
                dins=adins, dd1=dd1, ncores=ncores)
    ar.war = ar.war + [ag5]
    l1ffn_sub(xA, x_out)
    p.barrier("pool", p.stream_all("xst"))
    p.replay(alloc_sems(nc, prog_sems(p)))
    return nc


_PROGS = {}


def _progs():
    if not _PROGS:
        _PROGS["fused"] = build_fused()
    return _PROGS


def _consts():
    import ml_dtypes
    c = {}
    c["ident_d"] = np.eye(128, dtype=np.float32).astype(ml_dtypes.bfloat16)
    c["identf_d"] = np.eye(128, dtype=np.float32)
    c["mask"] = (np.uint32(0xF) << (4 * (np.arange(128, dtype=np.uint32) % 8)))[:, None]
    c["pow2"] = (2.0 ** (-4.0 * (np.arange(128) % 8)))[:, None].astype(np.float32)
    c["iota32"] = np.arange(32, dtype=np.float32)[:, None]
    c["iota86"] = np.arange(86, dtype=np.float32)[:, None]
    c["utri_d"] = np.triu(np.ones((128, 128), np.float32)).astype(ml_dtypes.bfloat16)
    return c


def _s16(a):
    return np.ascontiguousarray(a.reshape(4, 4, 128).transpose(2, 0, 1)
                                .reshape(128, 16)).astype(np.float32)


def _prep_maps(Z):
    import ml_dtypes
    import jax
    C = _consts()
    with jax.default_device(jax.devices("cpu")[0]):
        u = np.asarray(jax.random.uniform(jax.random.key(42), (4, 2048, 4096)))

    ffn_maps, mam_maps, l1f_maps, l1a_maps = [], [], [], []
    qweight_gu = Z["L0_ffn_qweight_gu"]; qzeros_gu = Z["L0_ffn_qzeros_gu"]
    scales_gu = Z["L0_ffn_scales_gu"].astype(np.float32)
    qweight_d = Z["L0_ffn_qweight_down"]; qzeros_d = Z["L0_ffn_qzeros_down"]
    scales_d = Z["L0_ffn_scales_down"].astype(np.float32)
    g_idx_gu = Z["L0_ffn_g_idx_gu"].astype(np.int32)
    g_idx_d = Z["L0_ffn_g_idx_down"].astype(np.int32)
    W_ip = Z["L0_attn_in_proj_w"].astype(np.float32)
    W_op = Z["L0_attn_out_proj_w"].astype(np.float32)
    gnorm_w = Z["L0_attn_gnorm_w"].astype(np.float32)
    W0 = Z["L1_ffn_W0"].astype(np.float32)

    mam_base = dict(
        norm_w_g=np.ascontiguousarray(Z["L0_attn_norm_w"].reshape(NKD, 128).T)
        .astype(np.float32),
        convw_d=np.ascontiguousarray(Z["L0_attn_conv_w"].reshape(80, 128, 4)
                                     .transpose(1, 0, 2).reshape(128, 320))
        .astype(np.float32),
        convb_d=np.ascontiguousarray(Z["L0_attn_conv_b"].reshape(80, 128).T)
        .astype(np.float32),
        dt_bias_d=Z["L0_attn_dt_bias"][:, None].astype(np.float32),
        A_log_d=Z["L0_attn_A_log"][:, None].astype(np.float32),
        dp_row_d=np.repeat(Z["L0_attn_Dp"], 64)[None, :].astype(np.float32),
        gnw_d=np.ascontiguousarray(gnorm_w.reshape(64, 128).T).astype(np.float32),
        utri_d=C["utri_d"], ident_d=C["ident_d"], identf_d=C["identf_d"])
    g = lambda k: Z["L1_attn_" + k]
    l1a_base = dict(
        win_d=g("Win").astype(np.float32),
        bin_d=np.ascontiguousarray(g("bin").reshape(4, 128).T).astype(np.float32),
        l1w_d=_s16(g("ln1_w")), l1b_d=_s16(g("ln1_b")),
        l2w_d=_s16(g("ln2_w")), l2b_d=_s16(g("ln2_b")),
        scl_d=_s16(g("sc_lin")), phzr_d=_s16(g("phz_re")), phzi_d=_s16(g("phz_im")),
        lcr_d=_s16(g("lc_re")), lci_d=_s16(g("lc_im")),
        w1_d=g("W1").astype(np.float32),
        b1_d=np.ascontiguousarray(g("b1").reshape(4, 16, 128).transpose(2, 0, 1)
                                  .reshape(128, 64)).astype(np.float32),
        w2_d=g("W2").astype(np.float32), b2_d=_s16(g("b2")),
        lnlw_d=np.ascontiguousarray(g("lnl_w").reshape(4, 128).T).astype(np.float32),
        lnlb_d=np.ascontiguousarray(g("lnl_b").reshape(4, 128).T).astype(np.float32),
        wout_d=g("Wout").astype(np.float32),
        bout_d=g("bout")[None, :].astype(np.float32),
        ident_d=C["ident_d"], identf_d=C["identf_d"])
    l1f_base = dict(a_d=Z["L1_ffn_A"].reshape(64, 4096).astype(np.float32),
                    b_d=Z["L1_ffn_B"].astype(np.float32),
                    b0_d=Z["L1_ffn_b0"][None, :].astype(np.float32),
                    ident_d=C["ident_d"], identf_d=C["identf_d"])

    for r in range(NC):
        qw = np.zeros((512, 2816), np.uint32)
        sc = np.zeros((32, 2816), np.float32)
        qz = np.zeros((32, 352), np.uint32)
        g0, g1 = 1408 * r, min(1408 * (r + 1), 11008)
        n = g1 - g0
        qw[:, :n] = qweight_gu[:, g0:g1]
        qw[:, 1408:1408 + n] = qweight_gu[:, 11008 + g0:11008 + g1]
        sc[:, :n] = scales_gu[:, g0:g1]
        sc[:, 1408:1408 + n] = scales_gu[:, 11008 + g0:11008 + g1]
        nz = n // 8
        qz[:, :nz] = qzeros_gu[:, g0 // 8:g1 // 8]
        qz[:, 176:176 + nz] = qzeros_gu[:, 1376 + g0 // 8:1376 + g1 // 8]
        qwd = np.zeros((176, 4096), np.uint32)
        gid = np.zeros((1408,), np.int32)
        qwd[:(g1 - g0) // 8] = qweight_d[g0 // 8:g1 // 8]
        gid[:g1 - g0] = g_idx_d[g0:g1]
        ffn_maps.append(dict(qw_gu=qw, qz_gu=qz, sc_gu=sc, gidx_gu=g_idx_gu,
                             qw_d=qwd, qz_d=qzeros_d, sc_d=scales_d, gidx_d=gid,
                             mask=C["mask"], pow2=C["pow2"],
                             iota32=C["iota32"], iota86=C["iota86"],
                             ident_d=C["ident_d"], identf_d=C["identf_d"]))
        ipr = np.zeros((19 * 128, 4096), np.float32)
        r0, r1 = 19 * 128 * r, min(19 * 128 * (r + 1), 18560)
        ipr[:r1 - r0] = W_ip[r0:r1]
        m = dict(mam_base)
        m["ip_rows"] = np.ascontiguousarray(
            ipr.reshape(19, 128, 32, 128).transpose(0, 3, 2, 1)
            .reshape(19 * 128, 4096))
        m["op_cols"] = np.ascontiguousarray(
            W_op[:, 1024 * r:1024 * (r + 1)].reshape(8, 512, 8, 128)
            .transpose(0, 2, 3, 1).reshape(8192, 512))
        m["gnw_sl"] = np.ascontiguousarray(
            gnorm_w[1024 * r:1024 * (r + 1)].reshape(8, 128).T)
        m["flag_d"] = np.full((128, 1), float(r % 2), np.float32)
        mam_maps.append(m)
        mf = dict(l1f_base)
        mf["w0_cols"] = np.ascontiguousarray(
            W0[:, 512 * r:512 * (r + 1)].reshape(8, 512, 4, 128)
            .transpose(0, 2, 3, 1).reshape(4096, 512))
        useg = u[r // 2, (r % 2) * 1024:(r % 2 + 1) * 1024]
        m4 = np.zeros((4, TOK, 4096), np.float32)
        for l in range(4):
            m4[l] = 4.0 * ((useg >= 0.25 * l) & (useg < 0.25 * (l + 1)))
        mf["mask_d"] = m4.astype(ml_dtypes.bfloat16)
        l1f_maps.append(mf)
        ma = dict(l1a_base)
        ma["flag_d"] = np.full((128, 1), float(r % 2), np.float32)
        rk = np.zeros((128, 8), np.float32)
        rk[:, (r + NC - 1) % NC] = 1.0
        ma["rksel_d"] = rk
        l1a_maps.append(ma)
    return ffn_maps, mam_maps, l1f_maps, l1a_maps


def kernel(**Z):
    P = _progs()
    ffn_maps, mam_maps, l1f_maps, l1a_maps = _prep_maps(Z)
    x = np.ascontiguousarray(Z["L0_attn_x"]).astype(np.float32)
    maps = []
    for r in range(NC):
        fm = {**ffn_maps[r], **mam_maps[r], **l1f_maps[r], **l1a_maps[r]}
        fm["x"] = np.ascontiguousarray(
            x[r // 2, (r % 2) * 1024:(r % 2 + 1) * 1024])
        maps.append(fm)
    res = run_bass_kernel_spmd(P["fused"], maps, core_ids=list(range(NC)))
    out = np.zeros((4, 2048, 4096), np.float32)
    for c in range(NC):
        out[c // 2, (c % 2) * 1024:(c % 2 + 1) * 1024] = \
            np.asarray(res.results[c]["x_out"])
    return out


# version 15
# speedup vs baseline: 1.1939x; 1.1939x over previous
import math
import numpy as np
import concourse.bass as bass
import concourse.mybir as mybir
from concourse.bass_utils import run_bass_kernel_spmd

F32 = mybir.dt.float32
BF16 = mybir.dt.bfloat16
U32 = mybir.dt.uint32
I32 = mybir.dt.int32
U8 = mybir.dt.uint8
AF = mybir.ActivationFunctionType
ALU = mybir.AluOpType
AX = mybir.AxisListType

ENG = ("sp", "pe", "act", "dve", "pool")

D = 4096
TOK = 1024
NKD = 32
EPS = 1e-5


class _Rec:
    __slots__ = ("method", "args", "kwargs", "waits", "incs")

    def __init__(self, method, args, kwargs, waits, incs):
        self.method = method
        self.args = args
        self.kwargs = kwargs
        self.waits = waits
        self.incs = incs


class Prog:

    def __init__(self, nc):
        self.nc = nc
        self.q = {e: [] for e in ENG}
        self.counts = {}
        self._last_wait = {e: {} for e in ENG}
        self.streams = {}
        self.stream_idx = {}
        self.stream_last = {}

    def _norm_waits(self, eng, waits):
        out = []
        for t in waits:
            if t is None:
                continue
            sem, val = t
            if self._last_wait[eng].get(sem, -1) >= val:
                continue
            self._last_wait[eng][sem] = val
            out.append((sem, val))
        return out

    def emit(self, eng, method, *args, waits=(), inc=None, inc_n=1, **kwargs):
        w = self._norm_waits(eng, waits)
        token = None
        incs = []
        if inc is not None:
            self.counts[inc] = self.counts.get(inc, 0) + inc_n
            token = (inc, self.counts[inc])
            incs = [(inc, inc_n)]
        self.q[eng].append(_Rec(method, args, kwargs, w, incs))
        return token

    def stream(self, name, rot=2):
        self.streams[name] = rot
        self.stream_idx.setdefault(name, 0)
        for i in range(rot):
            self.stream_last.setdefault(f"{name}{i}", None)

    def dma(self, eng, out, in_, stream, waits=(), **kwargs):
        i = self.stream_idx[stream]
        self.stream_idx[stream] = i + 1
        sem = f"{stream}{i % self.streams[stream]}"
        w = list(waits)
        prev = self.stream_last[sem]
        if prev is not None:
            w.append(prev)
        t = self.emit(eng, "dma_start", waits=w, inc=sem, inc_n=16,
                      out=out, in_=in_, **kwargs)
        self.stream_last[sem] = t
        return t

    def stream_all(self, name):
        return [self.tok(f"{name}{i}") for i in range(self.streams[name])]

    def tok(self, sem):
        return (sem, self.counts.get(sem, 0))

    def barrier(self, eng, waits):
        return self.emit(eng, "engine_nop", waits=waits)

    def raw(self, eng, fn, waits=(), inc=None, inc_n=1):
        w = self._norm_waits(eng, waits)
        token = None
        incs = []
        if inc is not None:
            self.counts[inc] = self.counts.get(inc, 0) + inc_n
            token = (inc, self.counts[inc])
            incs = [(inc, inc_n)]
        self.q[eng].append(_Rec(None, (fn,), {}, w, incs))
        return token

    def replay(self, sem_handles):
        nc = self.nc
        with nc.Block() as block:
            def mk(eng_name):
                def body(engobj):
                    for rec in self.q[eng_name]:
                        for (s, v) in rec.waits:
                            engobj.wait_ge(sem_handles[s], v)
                        if rec.method is None:
                            inst = rec.args[0](engobj)
                        else:
                            inst = getattr(engobj, rec.method)(*rec.args, **rec.kwargs)
                        for (s, v) in rec.incs:
                            inst.then_inc(sem_handles[s], v)
                return body
            block.sync(mk("sp"))
            block.tensor(mk("pe"))
            block.scalar(mk("act"))
            block.vector(mk("dve"))
            block.gpsimd(mk("pool"))


def alloc_sems(nc, names):
    return {n: nc.alloc_semaphore(n) for n in names}


ENGINE_SEMS = ["s_pe", "s_act", "s_dve", "s_pool", "s_sp", "s_cc"]


def prog_sems(p):
    names = list(ENGINE_SEMS)
    for name, rot in p.streams.items():
        names += [f"{name}{i}" for i in range(rot)]
    return names

DT_SIZE = {F32: 4, BF16: 2, U32: 4, I32: 4, U8: 1}


class SBufArena:

    def __init__(self, nc, nbytes, name="arena"):
        self.t = nc.alloc_sbuf_tensor(name, [128, nbytes], U8)
        self.nbytes = nbytes
        self.off = 0
        self.war = []

    def alloc(self, shape, dtype, align=64):
        assert len(shape) == 2 and shape[0] <= 128
        sz = shape[1] * DT_SIZE[dtype]
        off = (self.off + align - 1) // align * align
        assert off + sz <= self.nbytes, f"SBUF arena overflow: {off}+{sz} > {self.nbytes}"
        self.off = off + sz
        ap = self.t.ap()[0:shape[0], off:off + sz].bitcast(dtype)
        return ap

    def alloc128(self, shape, dtype, align=64):
        return self.alloc(shape, dtype, align)

    def mark(self):
        return self.off

    def reset(self, mark=0):
        self.off = mark


def phase_war(p, ar, stores=()):
    toks = [p.tok("s_pe"), p.tok("s_act"), p.tok("s_dve"), p.tok("s_pool")]
    for s in stores:
        toks += p.stream_all(s)
    ar.war = [t for t in toks if t[1] > 0]


def psum_wait(ctx, bank):
    return ctx["psum_last"].get(bank)


def psum_claim(ctx, bank, token):
    ctx["psum_last"][bank] = token



def emit_ln_transpose(p, ctx, *, x_ap, xnT, rms_only=False, mamba_rms=False,
                      gamma=None, x_ready=(), n_tt=8, nkd=NKD, dname="x",
                      xnT_war=(), light=False):
    ar = ctx["ar"]
    m0 = ar.mark()
    dd = nkd * 128
    tt = n_tt * 128
    nxb = 1 if light else 2
    xb = [ar.alloc([128, dd], F32) for _ in range(nxb)]
    xn = [ar.alloc([128, dd], BF16) for _ in range(2)]
    stats = [ar.alloc([128, 8], F32) for _ in range(2)]
    ps = ctx["psum"]
    ident = ctx["ident_bf"]

    sq_tok = {}
    ss_tok = {}
    apply_tok = {}
    tr_tok = {}
    ev_tok = {}
    last = None
    for t in range(n_tt):
        par = t % 2
        xpar = par % nxb
        ld_waits = list(x_ready)
        if t < 2:
            ld_waits += ar.war
        else:
            ld_waits += [sq_tok[t - 2], apply_tok[t - 2]]
        if light and t >= 1:
            ld_waits += [sq_tok[t - 1], apply_tok[t - 1]]
        ld = p.dma("sp", xb[xpar][:, :], x_ap[t * 128:(t + 1) * 128, :], dname,
                   waits=ld_waits)
        s_ = stats[par]
        xn_war = [tr_tok[t - 2]] if t >= 2 else []
        sq = xn[par]
        sq_waits = [ld] + ([ss_tok[t - 1]] if t >= 1 else []) + xn_war
        t_sq = p.emit("act", "activation", sq[:, :], xb[xpar][:, :], AF.Square,
                      waits=sq_waits, inc="s_act")
        sq_tok[t] = t_sq
        stat_war = [apply_tok[t - 2]] if t >= 2 else []
        t_ss = p.emit("dve", "tensor_reduce", s_[:, 1:2], sq[:, :], AX.X, ALU.add,
                      waits=[t_sq] + stat_war, inc="s_dve")
        ss_tok[t] = t_ss
        if rms_only:
            t_pre = p.emit("dve", "tensor_scalar", waits=[t_ss], inc="s_dve",
                           out=s_[:, 6:7], in0=s_[:, 1:2], scalar1=1.0 / dd,
                           scalar2=EPS, op0=ALU.mult, op1=ALU.add)
            t_v = p.emit("act", "activation", s_[:, 4:5], s_[:, 6:7], AF.Ln,
                         waits=[t_pre], inc="s_act")
            t_r = p.emit("act", "activation", s_[:, 5:6], s_[:, 4:5], AF.Exp,
                         scale=-0.5, waits=[t_v], inc="s_act")
            t_ap = p.emit("dve", "tensor_scalar", waits=[t_r, ld] + xn_war,
                          inc="s_dve",
                          out=xn[par][:, :], in0=xb[xpar][:, :],
                          scalar1=s_[:, 5:6], scalar2=None, op0=ALU.mult)
        else:
            t_s = p.emit("dve", "tensor_reduce", s_[:, 0:1], xb[xpar][:, :], AX.X,
                         ALU.add, waits=[ld] + stat_war, inc="s_dve")
            t_m = p.emit("dve", "tensor_scalar", waits=[t_s], inc="s_dve",
                         out=s_[:, 2:3], in0=s_[:, 0:1], scalar1=1.0 / dd,
                         scalar2=None, op0=ALU.mult)
            t_nm = p.emit("dve", "tensor_scalar", waits=[t_s], inc="s_dve",
                          out=s_[:, 3:4], in0=s_[:, 0:1], scalar1=-1.0 / dd,
                          scalar2=None, op0=ALU.mult)
            t_e2 = p.emit("dve", "tensor_scalar", waits=[t_ss], inc="s_dve",
                          out=s_[:, 6:7], in0=s_[:, 1:2], scalar1=1.0 / dd,
                          scalar2=EPS, op0=ALU.mult, op1=ALU.add)
            t_v0 = p.emit("dve", "scalar_tensor_tensor", waits=[t_m, t_e2],
                          inc="s_dve",
                          out=s_[:, 4:5], in0=s_[:, 2:3], scalar=s_[:, 2:3],
                          in1=s_[:, 6:7], op0=ALU.mult, op1=ALU.subtract)
            t_v = p.emit("act", "activation", s_[:, 2:3], s_[:, 4:5], AF.Ln,
                         scale=-1.0, waits=[t_v0], inc="s_act")
            t_r = p.emit("act", "activation", s_[:, 5:6], s_[:, 2:3], AF.Exp,
                         scale=-0.5, waits=[t_v], inc="s_act")
            if mamba_rms:
                t_u = p.emit("dve", "tensor_scalar", waits=[t_v0], inc="s_dve",
                             out=s_[:, 6:7], in0=s_[:, 4:5], scalar1=-1.0,
                             scalar2=None, op0=ALU.mult)
                t_ru = p.emit("dve", "reciprocal", s_[:, 7:8], s_[:, 6:7],
                              waits=[t_u], inc="s_dve")
                t_w = p.emit("dve", "tensor_scalar", waits=[t_u], inc="s_dve",
                             out=s_[:, 6:7], in0=s_[:, 6:7], scalar1=-EPS,
                             scalar2=None, op0=ALU.add)
                t_w2 = p.emit("dve", "tensor_scalar", waits=[t_w, t_ru], inc="s_dve",
                              out=s_[:, 6:7], in0=s_[:, 6:7], scalar1=s_[:, 7:8],
                              scalar2=EPS, op0=ALU.mult, op1=ALU.add)
                t_lw = p.emit("act", "activation", s_[:, 6:7], s_[:, 6:7], AF.Ln,
                              waits=[t_w2], inc="s_act")
                t_ew = p.emit("act", "activation", s_[:, 6:7], s_[:, 6:7], AF.Exp,
                              scale=-0.5, waits=[t_lw], inc="s_act")
                t_r = p.emit("dve", "tensor_mul", waits=[t_r, t_ew], inc="s_dve",
                             out=s_[:, 5:6], in0=s_[:, 5:6], in1=s_[:, 6:7])
            t_ap = p.emit("dve", "tensor_scalar", waits=[t_r, ld] + xn_war,
                          inc="s_dve",
                          out=xn[par][:, :], in0=xb[xpar][:, :],
                          scalar1=s_[:, 3:4], scalar2=s_[:, 5:6],
                          op0=ALU.add, op1=ALU.mult)
        apply_tok[t] = t_ap
        for kg in range(nkd // 4):
            bank = 4 + (kg % 4)
            pb = ps[bank].bitcast(BF16)
            pw = psum_wait(ctx, bank)
            tr = None
            for j in range(4):
                k = kg * 4 + j
                tr = p.emit("pe", "transpose", pb[:, j * 128:(j + 1) * 128],
                            xn[par][:, k * 128:(k + 1) * 128], ident,
                            waits=[t_ap] + ([pw] if pw and j == 0 else []),
                            inc=("s_pe" if j == 3 else None))
            if gamma is not None:
                ev = None
                for j in range(4):
                    k = kg * 4 + j
                    dstk = xnT[:, k * tt + t * 128: k * tt + (t + 1) * 128]
                    ev = p.emit("act", "activation", dstk,
                                pb[:, j * 128:(j + 1) * 128], AF.Copy,
                                scale=gamma[:, k:k + 1],
                                waits=[tr] + list(xnT_war), inc="s_act")
            else:
                dst = xnT.rearrange("p (k c) -> p k c", c=tt)[:, kg * 4:kg * 4 + 4,
                                                             t * 128:(t + 1) * 128]
                ev = p.emit("dve", "tensor_copy", waits=[tr] + list(xnT_war),
                            inc="s_dve", out=dst, in_=pb[:, :512])
            psum_claim(ctx, bank, ev)
            last = ev
        tr_tok[t] = last
    ar.reset(m0)
    phase_war(p, ar)
    return list(tr_tok.values()), last


def emit_ffn_mm1(p, ctx, *, wcache_ap=None, xnT, h_out_ap, x_ready_toks,
                 n_pairs=86, nk=NKD, tok=TOK, pairs=None, hook=None):
    ar = ctx["ar"]
    m0 = ar.mark()
    wbufs = [ar.alloc([128, 2 * nk * 128], BF16) for _ in range(2)]
    silu_tmps = [ar.alloc([128, tok], F32) for _ in range(2)]
    h_tiles = [ar.alloc([128, tok], BF16) for _ in range(2)]
    ps = ctx["psum"]

    if pairs is None:
        pairs = [(m, wcache_ap[m], ()) for m in range(n_pairs)]
    pe_done, act_done, dve_done, st_tok = {}, {}, {}, {}
    for idx, (m, wc_ap, exw) in enumerate(pairs):
        par = idx % 2
        waits = list(exw)
        if idx < 2:
            waits += ar.war
        else:
            waits.append(("s_pe", pe_done[idx - 2]))
        wtok = p.dma("sp", wbufs[par][:, :], wc_ap, "w", waits=waits)

        pe_waits = [wtok] + list(x_ready_toks)
        if idx >= 2:
            pe_waits.append(("s_act", act_done[idx - 2]))
            pe_waits.append(("s_dve", dve_done[idx - 2]))
        groups = []
        for gu in range(2):
            for h in range(2):
                bank = par * 4 + gu * 2 + h
                psb = ps[bank]
                w0 = list(pe_waits)
                pw = psum_wait(ctx, bank)
                if pw is not None:
                    w0.append(pw)
                for k in range(nk):
                    w_tile = wbufs[par][:, (gu * nk + k) * 128:(gu * nk + k + 1) * 128]
                    x_tile = xnT[:, k * tok + h * 512: k * tok + h * 512 + 512]
                    t = p.emit("pe", "matmul", psb[:, :512], w_tile, x_tile,
                               waits=w0 if k == 0 else (),
                               inc=("s_pe" if k == nk - 1 else None),
                               start=(k == 0), stop=(k == nk - 1))
                groups.append(t)
        pe_done[idx] = groups[-1][1]
        act_waits_war = [("s_dve", dve_done[idx - 2])] if idx >= 2 else []
        dve_waits_war = [st_tok[idx - 2]] if idx >= 2 else []
        for h in range(2):
            gate_bank = par * 4 + h
            up_bank = par * 4 + 2 + h
            ta = p.emit("act", "activation", silu_tmps[par][:, h * 512:(h + 1) * 512],
                        ps[gate_bank][:, :512], AF.Silu,
                        waits=[groups[h]] + act_waits_war, inc="s_act")
            td = p.emit("dve", "tensor_mul", waits=[ta, groups[2 + h]] + dve_waits_war,
                        inc="s_dve",
                        out=h_tiles[par][:, h * 512:(h + 1) * 512],
                        in0=silu_tmps[par][:, h * 512:(h + 1) * 512],
                        in1=ps[up_bank][:, :512])
            psum_claim(ctx, gate_bank, ta)
            psum_claim(ctx, up_bank, td)
        act_done[idx] = ta[1]
        dve_done[idx] = td[1]
        st_tok[idx] = p.dma("pool", h_out_ap[m], h_tiles[par][:, :], "hout",
                            waits=[td])
        if hook is not None:
            hook(idx)
    if hook is not None:
        for extra in range(len(pairs), len(pairs) + 4):
            hook(extra)
    phase_war(p, ar, stores=("hout",))
    ar.reset(m0)
    return p.stream_all("hout")


def emit_ffn_mm2(p, ctx, *, wd_cache_ap, h_ap, x_ap, xout_ap, h_ready_toks,
                 nk=86, n_oc=8, n_tt=8, scale=0.5, w_ready=()):
    ar = ctx["ar"]
    m0 = ar.mark()
    wb = [ar.alloc([128, 512], BF16) for _ in range(3)]
    hb = [ar.alloc([128, TOK], BF16) for _ in range(3)]
    xb = [ar.alloc([128, 512], F32) for _ in range(2 * n_tt)]
    ob = [ar.alloc([128, 512], F32) for _ in range(2 * n_tt)]
    ps = ctx["psum"]

    hrdy = list(h_ready_toks)
    ev_done = {}
    st_done = {}
    pe_hist = {}
    for oc in range(n_oc):
        ocp = oc % 2
        xtoks = []
        for t in range(n_tt):
            xw = []
            if oc < 2:
                xw += ar.war
            if (oc - 2, t) in st_done:
                xw.append(st_done[(oc - 2, t)])
            xtoks.append(p.dma("sp", xb[ocp * n_tt + t][:, :],
                               x_ap[t * 128:(t + 1) * 128, oc * 512:(oc + 1) * 512],
                               "x", waits=xw))
        tk = None
        for k in range(nk):
            kb = (oc * nk + k) % 3
            wwar = []
            if (oc * nk + k) < 3:
                wwar += ar.war + list(w_ready)
            else:
                wwar.append(("s_pe", pe_hist[(oc * nk + k) - 3]))
            wd_t = (wd_cache_ap(oc, k) if callable(wd_cache_ap)
                    else wd_cache_ap[oc, k])
            wtok = p.dma("sp", wb[kb][:, :], wd_t, "w", waits=wwar)
            htok = p.dma("act", hb[kb][:, :], h_ap[k], "ha",
                         waits=hrdy + wwar)
            for t in range(n_tt):
                pe_waits = [wtok, htok]
                if k == 0:
                    pw = psum_wait(ctx, t)
                    if pw is not None:
                        pe_waits.append(pw)
                tk = p.emit("pe", "matmul", ps[t][:, :512],
                            hb[kb][:, t * 128:(t + 1) * 128], wb[kb][:, :],
                            waits=pe_waits,
                            inc=("s_pe" if t == n_tt - 1 else None),
                            start=(k == 0), stop=(k == nk - 1))
            pe_hist[oc * nk + k] = tk[1]
        for t in range(n_tt):
            ev = p.emit("dve", "scalar_tensor_tensor",
                        waits=[tk, xtoks[t]], inc="s_dve",
                        out=ob[ocp * n_tt + t][:, :], in0=ps[t][:, :512],
                        scalar=scale, in1=xb[ocp * n_tt + t][:, :],
                        op0=ALU.mult, op1=ALU.add)
            ev_done[(oc, t)] = ev
            psum_claim(ctx, t, ev)
            st = p.dma("pool", xout_ap[t * 128:(t + 1) * 128, oc * 512:(oc + 1) * 512],
                       ob[ocp * n_tt + t][:, :], "xst", waits=[ev])
            st_done[(oc, t)] = st
    phase_war(p, ar, stores=("xst",))
    ar.reset(m0)
    return p.stream_all("xst")


def make_ctx(nc, p, arena_bytes=None):
    if arena_bytes is None:
        arena_bytes = ((nc.sbuf_bytes_remaining - 2048) // 64) * 64
    ctx = {}
    ctx["ar"] = SBufArena(nc, arena_bytes)
    ctx["psum"] = [nc.alloc_psum_tensor(f"psb{i}", [128, 512], F32).ap()
                   for i in range(8)]
    ctx["psum_last"] = {}
    for name, rot in (("x", 2), ("w", 3), ("hout", 2),
                      ("xst", 2), ("misc", 2), ("cst", 2), ("yst", 2),
                      ("ha", 3),
                      ("hin", 3), ("pb", 2), ("cx", 2),
                      ("cl", 2), ("cs", 2)):
        p.stream(name, rot)
    return ctx



def emit_onehot_tables(p, ctx, *, gidx_ap, iota_ap, oneh, n_groups, nkd, rep,
                       stream="misc"):
    t = None
    for k in range(nkd):
        ld = p.dma("sp", rep, gidx_ap[k * 128:(k + 1) * 128].unsqueeze(0)
                   .broadcast_to([n_groups, 128]), stream,
                   waits=[t] if t is not None else ctx["ar"].war)
        t = p.emit("dve", "tensor_scalar", waits=[ld], inc="s_dve",
                   out=oneh[:, k * 128:(k + 1) * 128], in0=rep,
                   scalar1=iota_ap, scalar2=None, op0=ALU.is_equal)
    return t


def emit_dequant_gu(p, ctx, *, qw_ap, qz_ap, sc_ap, oneh, mask_ap, pow2_ap,
                    out_slice_ap, n_pairs=11, nkd=NKD):
    ar = ctx["ar"]
    m0 = ar.mark()
    CW = n_pairs * 128
    qz_s = ar.alloc([32, 2 * CW // 8], U32)
    zq_s = ar.alloc([32, 2 * CW], F32)
    sc_s = ar.alloc([32, 2 * CW], F32)
    off_s = ar.alloc([32, 2 * CW], F32)
    scb_s = ar.alloc([32, 2 * CW], BF16)
    offb_s = ar.alloc([32, 2 * CW], BF16)
    qw_b = [ar.alloc([128, CW], U32) for _ in range(2)]
    msk_b = [ar.alloc([128, CW], U32) for _ in range(2)]
    nib_b = [b.bitcast(F32) for b in msk_b]
    w_b = [b.bitcast(F32) for b in qw_b]
    wo_b = [ar.alloc([128, CW], BF16) for _ in range(2)]
    ps = ctx["psum"]

    ldz = p.dma("sp", qz_s, qz_ap, "misc", waits=ar.war)
    lds = p.dma("sp", sc_s, sc_ap, "misc", waits=ar.war)
    zq_u = zq_s.bitcast(U32)
    tz = None
    for j in range(8):
        tz = p.emit("dve", "tensor_scalar", waits=[ldz], inc="s_dve",
                    out=zq_u[:, j::8], in0=qz_s,
                    scalar1=4 * j, scalar2=15,
                    op0=ALU.logical_shift_right, op1=ALU.bitwise_and)
    tz1 = p.emit("dve", "tensor_scalar", waits=[tz], inc="s_dve",
                 out=zq_s, in0=zq_u, scalar1=1.0, scalar2=None, op0=ALU.add)
    toff = p.emit("dve", "tensor_mul", waits=[tz1, lds], inc="s_dve",
                  out=off_s, in0=zq_s, in1=sc_s)
    tscb = p.emit("dve", "tensor_copy", waits=[lds], inc="s_dve",
                  out=scb_s, in_=sc_s)
    toffb = p.emit("dve", "tensor_copy", waits=[toff], inc="s_dve",
                   out=offb_s, in_=off_s)

    st_toks = {}
    prev = {}
    i = 0
    for gu in range(2):
        for k in range(nkd):
            par = i % 2
            w_ld = list(ar.war) if i < 2 else [prev[i - 2]["sub"]]
            ldq = p.dma("sp", qw_b[par],
                        qw_ap[16 * k:16 * (k + 1), gu * CW:(gu + 1) * CW]
                        .unsqueeze(1).broadcast_to([16, 8, CW]),
                        "w", waits=w_ld)
            t_msk = p.emit("dve", "tensor_scalar",
                           waits=[ldq] + ([prev[i - 2]["nib"]] if i >= 2 else []),
                           inc="s_dve",
                           out=msk_b[par], in0=qw_b[par],
                           scalar1=mask_ap, scalar2=None, op0=ALU.bitwise_and)
            t_nib = p.emit("dve", "tensor_scalar",
                           waits=[t_msk] + ([prev[i - 2]["mul"]] if i >= 2 else []),
                           inc="s_dve",
                           out=nib_b[par], in0=msk_b[par],
                           scalar1=pow2_ap, scalar2=None, op0=ALU.mult)
            t_mul = None
            t_sub = None
            for c3 in range((CW + 511) // 512):
                ncol = min(512, CW - c3 * 512)
                scp = ps[c3]
                offp = ps[3 + c3]
                mm_w = [ctx["oneh_tok"], tscb]
                pw = psum_wait(ctx, c3)
                if pw is not None:
                    mm_w.append(pw)
                m1 = p.emit("pe", "matmul", scp[:, :ncol], oneh[:, k * 128:(k + 1) * 128],
                            scb_s[:, gu * CW + c3 * 512: gu * CW + c3 * 512 + ncol],
                            waits=mm_w, inc="s_pe", start=True, stop=True)
                pw = psum_wait(ctx, 3 + c3)
                m2 = p.emit("pe", "matmul", offp[:, :ncol], oneh[:, k * 128:(k + 1) * 128],
                            offb_s[:, gu * CW + c3 * 512: gu * CW + c3 * 512 + ncol],
                            waits=[toffb] + ([pw] if pw is not None else []),
                            inc="s_pe", start=True, stop=True)
                t_mul = p.emit("dve", "tensor_mul", waits=[t_nib, m1], inc="s_dve",
                               out=w_b[par][:, c3 * 512:c3 * 512 + ncol],
                               in0=nib_b[par][:, c3 * 512:c3 * 512 + ncol],
                               in1=scp[:, :ncol])
                psum_claim(ctx, c3, t_mul)
                wsub = [t_mul, m2] + ([prev[i - 2]["st"]] if i >= 2 else [])
                t_sub = p.emit("dve", "tensor_sub", waits=wsub, inc="s_dve",
                               out=wo_b[par][:, c3 * 512:c3 * 512 + ncol],
                               in0=w_b[par][:, c3 * 512:c3 * 512 + ncol],
                               in1=offp[:, :ncol])
                psum_claim(ctx, 3 + c3, t_sub)
            st = p.dma("pool",
                       out_slice_ap[:, :, (gu * nkd + k) * 128:(gu * nkd + k + 1) * 128]
                       .rearrange("a p m -> p a m"),
                       wo_b[par], "cst", waits=[t_sub])
            prev[i] = {"msk": t_msk, "nib": t_nib, "mul": t_mul, "st": st,
                       "sub": t_sub}
            st_toks[i] = st
            i += 1
    phase_war(p, ar, stores=("cst",))
    ar.reset(m0)
    return p.stream_all("cst")


def emit_dequant_wd(p, ctx, *, qw_ap, sc_full_ap, qz_full_ap, oneh_d, mask_ap,
                    pow2_ap, out_slice_ap, n_kl=11, n_groups=86, n_oc=8):
    ar = ctx["ar"]
    m0 = ar.mark()
    W = 4096
    qz_s = ar.alloc([n_groups, W // 8], U32)
    zq_s = ar.alloc([n_groups, W], F32)
    sc_s = ar.alloc([n_groups, W], F32)
    off_s = ar.alloc([n_groups, W], F32)
    scb_s = ar.alloc([n_groups, W], BF16)
    offb_s = ar.alloc([n_groups, W], BF16)
    qw_b = [ar.alloc([128, W], U32) for _ in range(2)]
    msk_b = [ar.alloc([128, W], U32) for _ in range(2)]
    nib_b = [b.bitcast(F32) for b in msk_b]
    w_b = [b.bitcast(F32) for b in qw_b]
    wo_b = [ar.alloc([128, W], BF16) for _ in range(2)]
    ps = ctx["psum"]

    ldz = p.dma("sp", qz_s, qz_full_ap, "misc", waits=ar.war)
    lds = p.dma("sp", sc_s, sc_full_ap, "misc", waits=ar.war)
    zq_u = zq_s.bitcast(U32)
    tz = None
    for j in range(8):
        tz = p.emit("dve", "tensor_scalar", waits=[ldz], inc="s_dve",
                    out=zq_u[:, j::8], in0=qz_s,
                    scalar1=4 * j, scalar2=15,
                    op0=ALU.logical_shift_right, op1=ALU.bitwise_and)
    tz1 = p.emit("dve", "tensor_scalar", waits=[tz], inc="s_dve",
                 out=zq_s, in0=zq_u, scalar1=1.0, scalar2=None, op0=ALU.add)
    toff = p.emit("dve", "tensor_mul", waits=[tz1, lds], inc="s_dve",
                  out=off_s, in0=zq_s, in1=sc_s)
    tscb = p.emit("dve", "tensor_copy", waits=[lds], inc="s_dve",
                  out=scb_s, in_=sc_s)
    toffb = p.emit("dve", "tensor_copy", waits=[toff], inc="s_dve",
                   out=offb_s, in_=off_s)

    prev = {}
    for kl in range(n_kl):
        par = kl % 2
        w_ld = list(ar.war) if kl < 2 else [prev[kl - 2]["sub"]]
        ldq = p.dma("sp", qw_b[par],
                    qw_ap[16 * kl:16 * (kl + 1), :]
                    .unsqueeze(1).broadcast_to([16, 8, W]),
                    "w", waits=w_ld)
        t_msk = p.emit("dve", "tensor_scalar",
                       waits=[ldq] + ([prev[kl - 2]["nib"]] if kl >= 2 else []),
                       inc="s_dve",
                       out=msk_b[par], in0=qw_b[par],
                       scalar1=mask_ap, scalar2=None, op0=ALU.bitwise_and)
        t_nib = p.emit("dve", "tensor_scalar",
                       waits=[t_msk] + ([prev[kl - 2]["mul"]] if kl >= 2 else []),
                       inc="s_dve",
                       out=nib_b[par], in0=msk_b[par],
                       scalar1=pow2_ap, scalar2=None, op0=ALU.mult)
        t_mul = None
        t_sub = None
        st = None
        for c in range(n_oc):
            scp = ps[c % 3]
            offp = ps[3 + c % 3]
            mm_w = [ctx["oneh_d_tok"], tscb]
            pw = psum_wait(ctx, c % 3)
            if pw is not None:
                mm_w.append(pw)
            m1 = p.emit("pe", "matmul", scp[:, :512], oneh_d[:, kl * 128:(kl + 1) * 128],
                        scb_s[:, c * 512:(c + 1) * 512],
                        waits=mm_w, inc="s_pe", start=True, stop=True)
            pw = psum_wait(ctx, 3 + c % 3)
            m2 = p.emit("pe", "matmul", offp[:, :512], oneh_d[:, kl * 128:(kl + 1) * 128],
                        offb_s[:, c * 512:(c + 1) * 512],
                        waits=[toffb] + ([pw] if pw is not None else []),
                        inc="s_pe", start=True, stop=True)
            t_mul = p.emit("dve", "tensor_mul", waits=[t_nib, m1], inc="s_dve",
                           out=w_b[par][:, c * 512:(c + 1) * 512],
                           in0=nib_b[par][:, c * 512:(c + 1) * 512],
                           in1=scp[:, :512])
            psum_claim(ctx, c % 3, t_mul)
            wsub = [t_mul, m2] + (prev[kl - 2]["st_all"] if kl >= 2 else [])
            t_sub = p.emit("dve", "tensor_sub", waits=wsub, inc="s_dve",
                           out=wo_b[par][:, c * 512:(c + 1) * 512],
                           in0=w_b[par][:, c * 512:(c + 1) * 512],
                           in1=offp[:, :512])
            psum_claim(ctx, 3 + c % 3, t_sub)
            st = p.dma("pool", out_slice_ap[c, kl],
                       wo_b[par][:, c * 512:(c + 1) * 512], "cst", waits=[t_sub])
        prev[kl] = {"msk": t_msk, "nib": t_nib, "mul": t_mul, "sub": t_sub,
                    "st_all": p.stream_all("cst")}
    phase_war(p, ar, stores=("cst",))
    ar.reset(m0)
    return p.stream_all("cst")


H_M, P_M, G_M, N_M = 128, 64, 8, 128
D_INNER = H_M * P_M
CONV_DIM = D_INNER + 2 * G_M * N_M
NMB = 145
NMB_PAD = 152
NCHUNK = 8


def emit_inproj(p, ctx, mc, *, cache_ap, xnT, x_ready_toks, hook=None):
    ar = ctx["ar"]
    m0 = ar.mark()
    wbufs = [ar.alloc([128, NKD * 128], BF16) for _ in range(2)]
    ev_f = [ar.alloc([128, TOK], F32) for _ in range(2)]
    ev_b = [ar.alloc([128, TOK], BF16) for _ in range(2)]
    stage = ar.alloc([128, 8 * TOK], BF16)
    ps = ctx["psum"]

    order = list(range(64, 144)) + [144] + list(range(0, 64))
    pe_done, ev_done = {}, {}
    evb_reader = {}
    evf_reader = {}
    last_flush = []
    ng = 0
    for idx, mb in enumerate(order):
        par = idx % 2
        waits = list(ar.war) if idx < 2 else [("s_pe", pe_done[idx - 2])]
        wtok = p.dma("sp", wbufs[par][:, :], cache_ap[mb], "w", waits=waits)
        pe_waits = [wtok] + list(x_ready_toks)
        if idx >= 2:
            pe_waits.append(ev_done[idx - 2])
        groups = []
        for h in range(2):
            bank = par * 2 + h
            psb = ps[bank]
            w0 = list(pe_waits)
            pw = psum_wait(ctx, bank)
            if pw is not None:
                w0.append(pw)
            for k in range(NKD):
                t = p.emit("pe", "matmul", psb[:, :512],
                           wbufs[par][:, k * 128:(k + 1) * 128],
                           xnT[:, k * TOK + h * 512: k * TOK + h * 512 + 512],
                           waits=w0 if k == 0 else (),
                           inc=("s_pe" if k == NKD - 1 else None),
                           start=(k == 0), stop=(k == NKD - 1))
            groups.append(t)
        pe_done[idx] = groups[-1][1]
        war_b = [evb_reader[idx - 2]] if idx - 2 in evb_reader else []
        war_f = [evf_reader[idx - 2]] if idx - 2 in evf_reader else []

        if 64 <= mb < 144:
            ct = mb - 64
            evs = []
            for h in range(2):
                bank = par * 2 + h
                ev = p.emit("act", "activation", ev_b[par][:, h * 512:(h + 1) * 512],
                            ps[bank][:, :512], AF.Copy,
                            waits=[groups[h]] + war_b, inc="s_act")
                psum_claim(ctx, bank, ev)
                evs.append(ev)
            ev_done[idx] = evs[-1]
            hb = p.emit("dve", "tensor_copy", waits=[evs[1]], inc="s_dve",
                        out=mc["halo_out"][:, ct * 3:(ct + 1) * 3],
                        in_=ev_b[par][:, 1021:1024])
            mc.setdefault("halo_toks", []).append(hb)
            st = p.dma("pool", mc["hbc_spill"][ct], ev_b[par][:, :], "hout",
                       waits=[evs[1], hb])
            evb_reader[idx] = st
        elif mb == 144:
            evs = []
            for h in range(2):
                bank = par * 2 + h
                ev = p.emit("act", "activation", mc["dt_raw"][:, h * 512:(h + 1) * 512],
                            ps[bank][:, :512], AF.Copy,
                            waits=[groups[h]], inc="s_act")
                psum_claim(ctx, bank, ev)
                evs.append(ev)
            mc["dt_raw_tok"] = evs[-1]
            ev_done[idx] = evs[-1]
            if idx - 2 in evb_reader:
                evb_reader[idx] = evb_reader[idx - 2]
            if idx - 2 in evf_reader:
                evf_reader[idx] = evf_reader[idx - 2]
        else:
            evs = []
            for h in range(2):
                bank = par * 2 + h
                ev = p.emit("act", "activation", ev_f[par][:, h * 512:(h + 1) * 512],
                            ps[bank][:, :512], AF.Silu,
                            waits=[groups[h]] + war_f, inc="s_act")
                psum_claim(ctx, bank, ev)
                evs.append(ev)
            ev_done[idx] = evs[-1]
            tg = p.emit("dve", "tensor_copy",
                        waits=evs + war_b, inc="s_dve",
                        out=ev_b[par][:, :], in_=ev_f[par][:, :])
            evf_reader[idx] = tg
            col = (ng % 8) * 128
            trs = {}
            for i in range(NCHUNK):
                bank = 4 + (i % 4)
                pb = ps[bank].bitcast(BF16)
                pw = psum_wait(ctx, bank)
                tr = p.emit("pe", "transpose",
                            pb[:, (i // 4) * 128:(i // 4) * 128 + 128],
                            ev_b[par][:, i * 128:(i + 1) * 128], mc["ident_bf"],
                            waits=[tg] + ([pw] if pw else []),
                            inc=("s_pe" if i % 4 == 3 else None))
                if i % 4 == 3:
                    trs[i // 4] = tr
            evb_reader[idx] = trs[1]
            evt = None
            for half in range(2):
                for i in range(4):
                    chunk = half * 4 + i
                    bank = 4 + (chunk % 4)
                    pb = ps[bank].bitcast(BF16)
                    flw = list(last_flush)
                    evt = p.emit("dve", "tensor_copy", waits=[trs[half]] + flw,
                                 inc="s_dve",
                                 out=stage[:, chunk * TOK + col: chunk * TOK + col + 128],
                                 in_=pb[:, half * 128: half * 128 + 128])
                    psum_claim(ctx, bank, evt)
            ng += 1
            if ng % 8 == 0:
                base = (ng // 8 - 1) * 1024
                for i in range(NCHUNK):
                    p.dma("pool",
                          mc["sgate_tok_spill"][i][:, base:base + 1024],
                          stage[:, i * TOK:(i + 1) * TOK], "hout",
                          waits=[evt])
                last_flush = p.stream_all("hout")
        if hook is not None:
            hook(idx)
    phase_war(p, ar, stores=("hout",))
    ar.reset(m0)
    return ("s_pe", pe_done[len(order) - 1])


def make_conv_stepper(p, ctx, mc):
    ar = ctx["ar"]
    ps = ctx["psum"]
    st = {"ct": 0, "prev": {}, "last_flush": [], "hb_rdy": None,
          "hflag": None}
    xcat = [ar.alloc([128, 1027 + 13], F32) for _ in range(2)]
    ones_t = ar.alloc([128, 128], F32)
    mc["ones_t"] = ones_t
    t_ot = p.emit("dve", "memset", ones_t, 1.0, waits=ar.war, inc="s_dve")
    taps = [ar.alloc([128, TOK], F32) for _ in range(2)]
    hs_b = [ar.alloc([128, TOK], BF16) for _ in range(2)]
    stage = ar.alloc([128, 8 * TOK], BF16)
    halo_in = ar.alloc([128, 240], F32)
    cw_ld = p.dma("sp", mc["convw"], mc["convw_d"], "misc", waits=ar.war)
    cb_ld = p.dma("sp", mc["convb"], mc["convb_d"], "misc", waits=ar.war)
    t_ca = p.emit("act", "activation", mc["cA"], mc["A_log"][:, 0:1], AF.Exp,
                  waits=ar.war, inc="s_act")
    st["t_ca"] = p.emit("dve", "tensor_scalar", waits=[t_ca], inc="s_dve",
                        out=mc["cA"], in0=mc["cA"], scalar1=-1.0, scalar2=None,
                        op0=ALU.mult)

    def emit_halo():
        b1 = p.dma("pool", mc["halo_bounce_d"], mc["halo_out"], "pb",
                   waits=list(mc["halo_toks"]))
        ag = p.raw("pool", lambda g: g.collective_compute(
            "AllGather", ALU.bypass,
            replica_groups=[list(range(mc["ncores"]))],
            ins=[mc["halo_bounce_d"].opt()], outs=[mc["halo_all_d"].opt()]),
            waits=[b1], inc="s_cc")
        ncr = mc["ncores"]

        def rd(g):
            prev = (g.partition_id() + (ncr - 1)) % ncr
            return g.dma_start(out=halo_in,
                               in_=mc["halo_all_d"][bass.ts(prev, 128), :])
        h2 = p.raw("pool", rd, waits=[ag], inc="hin0", inc_n=16)
        st["hflag"] = p.emit("dve", "tensor_scalar", waits=[h2], inc="s_dve",
                             out=halo_in, in0=halo_in,
                             scalar1=mc["flag"][:, 0:1],
                             scalar2=None, op0=ALU.mult)
        st["hb_rdy"] = p.stream_all("hout")

    def emit_dt():
        dtr = mc["dt_raw"]
        t_a = p.emit("act", "activation", mc["sp_a"], dtr, AF.Abs,
                     bias=mc["dt_bias"][:, 0:1], waits=[mc["dt_raw_tok"]],
                     inc="s_act")
        t_e = p.emit("act", "activation", mc["sp_a"], mc["sp_a"], AF.Exp,
                     scale=-1.0, waits=[t_a], inc="s_act")
        t_l = p.emit("act", "activation", mc["sp_a"], mc["sp_a"], AF.Ln,
                     bias=1.0, waits=[t_e], inc="s_act")
        t_m = p.emit("dve", "tensor_scalar", waits=[mc["dt_raw_tok"]],
                     inc="s_dve", out=mc["sp_m"], in0=dtr,
                     scalar1=mc["dt_bias"][:, 0:1],
                     scalar2=0.0, op0=ALU.add, op1=ALU.max)
        t_dt = p.emit("dve", "tensor_add", waits=[t_l, t_m], inc="s_dve",
                      out=mc["dt"], in0=mc["sp_a"], in1=mc["sp_m"])
        mc["dt_tok"] = t_dt
        t_af = p.emit("dve", "tensor_scalar", waits=[t_dt, st["t_ca"]],
                      inc="s_dve",
                      out=mc["A_full"], in0=mc["dt"],
                      scalar1=mc["cA"][:, 0:1],
                      scalar2=None, op0=ALU.mult)
        t_cs = None
        for i in range(NCHUNK):
            t_cs = p.emit("dve", "tensor_tensor_scan", waits=[t_af, t_ot],
                          inc="s_dve",
                          out=mc["A_cs"][:, i * 128:(i + 1) * 128],
                          data0=ones_t[:, :128],
                          data1=mc["A_full"][:, i * 128:(i + 1) * 128],
                          initial=0.0, op0=ALU.mult, op1=ALU.add)
        mc["A_cs_tok"] = t_cs

    def emit_ct():
        ct = st["ct"]
        if ct >= 80:
            return
        st["ct"] = ct + 1
        prev = st["prev"]
        par = ct % 2
        eng = "dve"
        w_ld = [prev[ct - 2]["tap3"]] if ct - 2 in prev else list(ar.war)
        ld = p.dma("pool", xcat[par][:, 3:1027], mc["hbc_spill"][ct], "cx",
                   waits=w_ld + st["hb_rdy"])
        hl = p.emit(eng, "tensor_copy", waits=[st["hflag"], ld] +
                    ([prev[ct - 2]["tap3"]] if ct - 2 in prev else []),
                    inc="s_" + eng,
                    out=xcat[par][:, 0:3], in_=halo_in[:, ct * 3:(ct + 1) * 3])
        c4 = ct * 4
        t0 = p.emit(eng, "tensor_scalar", waits=[ld, cw_ld, hl],
                    inc="s_" + eng,
                    out=taps[par], in0=xcat[par][:, 0:1024],
                    scalar1=mc["convw"][:, c4:c4 + 1], scalar2=None,
                    op0=ALU.mult)
        acc = t0
        for k in range(1, 4):
            acc = p.emit(eng, "scalar_tensor_tensor", waits=[acc, hl],
                         inc="s_" + eng,
                         out=taps[par], in0=xcat[par][:, k:k + 1024],
                         scalar=mc["convw"][:, c4 + k:c4 + k + 1],
                         in1=taps[par], op0=ALU.mult, op1=ALU.add)
        prev[ct] = {"tap3": acc}
        if ct >= 64:
            dst = (mc["BT"] if ct < 72 else mc["CT"])
            gi = (ct - 64) % 8
            t_si = p.emit("act", "activation", dst[:, gi * TOK:(gi + 1) * TOK],
                          taps[par], AF.Silu, bias=mc["convb"][:, ct:ct + 1],
                          waits=[acc, cb_ld], inc="s_act")
            mc.setdefault("bc_toks", []).append(t_si)
            prev[ct]["tap3"] = t_si
        else:
            war_h = ([prev[ct - 2]["hsb"]]
                     if (ct - 2 in prev and "hsb" in prev[ct - 2]) else [])
            t_si = p.emit("act", "activation", hs_b[par], taps[par], AF.Silu,
                          bias=mc["convb"][:, ct:ct + 1],
                          waits=[acc, cb_ld] + war_h, inc="s_act")
            prev[ct]["tap3"] = t_si
            col = (ct % 8) * 128
            trs = {}
            for i in range(NCHUNK):
                bank = 4 + (i % 4)
                pb = ps[bank].bitcast(BF16)
                pw = psum_wait(ctx, bank)
                tr = p.emit("pe", "transpose",
                            pb[:, (i // 4) * 128:(i // 4) * 128 + 128],
                            hs_b[par][:, i * 128:(i + 1) * 128],
                            mc["ident_bf"],
                            waits=[t_si] + ([pw] if pw else []),
                            inc=("s_pe" if i % 4 == 3 else None))
                if i % 4 == 3:
                    trs[i // 4] = tr
            prev[ct]["hsb"] = trs[1]
            evt = None
            for half in range(2):
                for i in range(4):
                    chunk = half * 4 + i
                    bank = 4 + (chunk % 4)
                    pb = ps[bank].bitcast(BF16)
                    evt = p.emit("dve", "tensor_copy",
                                 waits=[trs[half]] + list(st["last_flush"]),
                                 inc="s_dve",
                                 out=stage[:, chunk * TOK + col:
                                           chunk * TOK + col + 128],
                                 in_=pb[:, half * 128:half * 128 + 128])
                    psum_claim(ctx, bank, evt)
            if ct % 8 == 7:
                base = (ct // 8) * 1024
                for i in range(NCHUNK):
                    p.dma("pool", mc["hid_tok_spill"][i][:, base:base + 1024],
                          stage[:, i * TOK:(i + 1) * TOK], "hout", waits=[evt])
                st["last_flush"] = p.stream_all("hout")

    def on_idx(idx):
        if idx == 79:
            emit_halo()
        elif idx == 80:
            emit_dt()
        elif idx >= 81:
            emit_ct()
            emit_ct()

    def finish():
        while st["ct"] < 80:
            emit_ct()
        mc["hid_spill_toks"] = p.stream_all("hout")
    return on_idx, finish


def emit_mamba_scan(p, ctx, mc):
    ar = ctx["ar"]
    m0 = ar.mark()
    ps = ctx["psum"]
    R = mc["R"]
    Rb = ar.alloc([128, 8192], BF16)
    dp_b = ar.alloc([128, 8192], BF16)
    hid_b = ar.alloc([128, 8192], BF16)
    hc = ar.alloc([128, 8192], BF16)
    hten = hc
    ysc = ar.alloc([128, 8192], BF16)
    abuf = [ar.alloc([1, 2048], F32) for _ in range(2)]
    m1row = ar.alloc([1, 128], F32)
    dtT = ar.alloc([128, 128], BF16)
    acsT = ar.alloc([128, 128], F32)
    EcsT = ar.alloc([128, 128], F32)
    dec_d = ar.alloc([128, 128], F32)
    dec_t = ar.alloc([128, 128], BF16)
    el_b = ar.alloc([128, 128], F32)
    el_row = ar.alloc([1, 128], F32)
    B_tok = ar.alloc([128, 1024], BF16)
    Mexp = ar.alloc([128, 512], BF16)
    Mq = ar.alloc([128, 512], BF16)
    Mq2 = ar.alloc([128, 512], BF16)
    cbt = [ar.alloc([128, 128], BF16) for _ in range(2)]
    tq = ar.alloc([128, 256], F32)
    ones1 = ar.alloc([1, 128], F32)

    t_on = p.emit("dve", "memset", ones1, 1.0, waits=ar.war, inc="s_dve")
    t_r0 = p.emit("dve", "memset", R, 0.0, waits=ar.war, inc="s_dve")
    msLR = [p.emit("dve", "memset", m1row, -1.0, waits=ar.war, inc="s_dve")]
    dp_row = ar.alloc([1, 512], F32)
    t_dp = None
    mprev = {}
    for c in range(16):
        half = 0
        dl = p.dma("sp", dp_row[0:1, half * 512:(half + 1) * 512],
                   mc["dp_row_d"][0:1, c * 512:(c + 1) * 512], "misc",
                   waits=(ar.war if c < 1 else [mprev[c - 1]]))
        pw = psum_wait(ctx, half)
        mmw = [dl, t_on] + ([pw] if pw else [])
        m = p.emit("pe", "matmul", ps[half][:, :512], ones1,
                   dp_row[0:1, half * 512:(half + 1) * 512],
                   waits=mmw, inc="s_pe", start=True, stop=True)
        mprev[c] = m
        t_dp = p.emit("dve", "tensor_copy", waits=[m], inc="s_dve",
                      out=dp_b[:, c * 512:(c + 1) * 512], in_=ps[half][:, :512])
        psum_claim(ctx, half, t_dp)
    mc["dp_tok"] = t_dp

    info = {}
    for i in range(NCHUNK):
        hw = list(mc["hid_spill_toks"]) if i == 0 else [info[i - 1]["hid_rd"]]
        ld_h = p.dma("sp", hid_b, mc["hid_tok_spill"][i], "w", waits=hw)
        b_acs = p.dma("pool", mc["acs_d"], mc["A_cs"][:, i * 128:(i + 1) * 128],
                      "hin", waits=[mc["A_cs_tok"]] + p.stream_all("hin"))
        pw6 = psum_wait(ctx, 6)
        tr_dt = p.emit("pe", "transpose", ps[6][:, :128],
                       mc["dt"][:, i * 128:(i + 1) * 128], mc["ident_f"],
                       waits=[mc["dt_tok"]] + ([pw6] if pw6 else []), inc="s_pe")
        ev_dt = p.emit("dve", "tensor_copy",
                       waits=[tr_dt] + ([info[i - 1]["dtT_rd"]] if i >= 1 else []),
                       inc="s_dve", out=dtT, in_=ps[6][:, :128])
        tr_ac = p.emit("pe", "transpose", ps[6][:, 128:256],
                       mc["A_cs"][:, i * 128:(i + 1) * 128], mc["ident_f"],
                       waits=[mc["A_cs_tok"], ev_dt] +
                       (list(info[i - 1]["acsT_rd"]) if i >= 1 else []),
                       inc="s_pe")
        ev_ac = p.emit("dve", "tensor_copy", waits=[tr_ac], inc="s_dve",
                       out=acsT, in_=ps[6][:, 128:256])
        psum_claim(ctx, 6, ev_ac)
        r_al = p.dma("pool", el_row, acsT[127:128, :], "hin",
                     waits=[ev_ac] + ([info[i - 1]["el_rd"]] if i >= 1
                                      else []))
        t_el = p.emit("act", "activation", el_row, el_row, AF.Exp,
                      waits=[r_al], inc="s_act")
        m_al = p.emit("pe", "matmul", ps[6][:, 256:384], ones1, el_row,
                      waits=[t_el, t_on], inc="s_pe", start=True, stop=True)
        t_elb = p.emit("dve", "tensor_copy", waits=[m_al] +
                       ([info[i - 1]["elb_rd"]] if i >= 1 else []), inc="s_dve",
                       out=el_b, in_=ps[6][:, 256:384])
        r_al2 = p.dma("pool", el_row, acsT[127:128, :], "hin",
                      waits=[m_al])
        m_al2 = p.emit("pe", "matmul", ps[6][:, 384:512], ones1, el_row,
                       waits=[r_al2, t_on], inc="s_pe", start=True, stop=True)
        t_dd = p.emit("dve", "tensor_sub", waits=[m_al2, ev_ac] +
                      ([info[i - 1]["decd_rd"]] if i >= 1 else []), inc="s_dve",
                      out=dec_d, in0=ps[6][:, 384:512], in1=acsT)
        psum_claim(ctx, 6, t_dd)
        t_dect = p.emit("act", "activation", dec_t, dec_d, AF.Exp,
                        waits=[t_dd] + ([info[i - 1]["dect_rd"]] if i >= 1 else []),
                        inc="s_act")
        t_ecs = p.emit("act", "activation", EcsT, acsT, AF.Exp,
                       waits=[ev_ac] + ([info[i - 1]["ecs_rd"]] if i >= 1 else []),
                       inc="s_act")
        dtb = dtT.rearrange("p (h o) -> p h o", o=1).broadcast_to([128, 128, 64])
        t_hc = p.emit("dve", "tensor_tensor", waits=[ld_h, ev_dt] +
                      ([info[i - 1]["hc_rd"]] if i >= 1 else []), inc="s_dve",
                      out=hc, in0=hid_b, in1=dtb, op=ALU.mult)
        t_rb = p.emit("dve", "tensor_copy",
                      waits=[t_r0 if i == 0 else info[i - 1]["r_done"]],
                      inc="s_dve", out=Rb, in_=R)
        ev_bt = None
        for g in range(8):
            pw7 = psum_wait(ctx, 7)
            tr = p.emit("pe", "transpose",
                        ps[7].bitcast(BF16)[:, (g % 4) * 128:(g % 4) * 128 + 128],
                        mc["BT"][:, g * TOK + i * 128: g * TOK + i * 128 + 128],
                        mc["ident_bf"],
                        waits=mc["bc_toks"] + ([pw7] if pw7 else []), inc="s_pe")
            ev_bt = p.emit("dve", "tensor_copy", waits=[tr] +
                           ([info[i - 1]["btok_rd"]] if i >= 1 and g == 0 else []),
                           inc="s_dve",
                           out=B_tok[:, g * 128:(g + 1) * 128],
                           in_=ps[7].bitcast(BF16)[:, (g % 4) * 128:(g % 4) * 128 + 128])
            psum_claim(ctx, 7, ev_bt)
        t_dres = p.emit("dve", "tensor_tensor", waits=[ld_h, mc["dp_tok"]] +
                        ([info[i - 1]["ysc_st"]] if i >= 1 else []), inc="s_dve",
                        out=ysc, in0=hid_b, in1=dp_b, op=ALU.mult)

        r_gl = {}
        md_of = dict(mc.get("scan_md_hist", {}))

        def issue_buflr(qg):
            par = qg % 2
            key = (i, qg - 2) if qg >= 2 else (i - 1, qg + 6)
            w_extra = [md_of[key]] if key in md_of else list(msLR)
            r_gl[qg] = [
                p.dma("pool", abuf[par],
                      mc["acs_d"][16 * qg:16 * (qg + 1), :], "hin",
                      waits=[b_acs] + w_extra),
                t_on,
            ]
        issue_buflr(0)
        t_cb = {}
        my = None
        ysc_tt = t_dres
        mqs = [Mq, Mq2]
        tmq_of = {}
        for q in range(33):
            if q < 32:
                g = q // 4
                qg, qi = q // 4, q % 4
                par = qg % 2
                bdq = q % 2
                if qi == 0:
                    if qg + 1 < 8:
                        issue_buflr(qg + 1)
                    pw3 = psum_wait(ctx, 3)
                    m_cb = p.emit("pe", "matmul", ps[3][:, (qg % 2) * 128:
                                                        (qg % 2) * 128 + 128],
                                  mc["BT"][:, g * TOK + i * 128:
                                           g * TOK + i * 128 + 128],
                                  mc["CT"][:, g * TOK + i * 128:
                                           g * TOK + i * 128 + 128],
                                  waits=mc["bc_toks"] + ([pw3] if pw3 else []),
                                  inc="s_pe", start=True, stop=True)
                    t_cb[qg] = p.emit("dve", "tensor_tensor",
                                      waits=[m_cb, mc["utri_tok"]],
                                      inc="s_dve", out=cbt[qg % 2],
                                      in0=ps[3][:, (qg % 2) * 128:
                                                (qg % 2) * 128 + 128],
                                      in1=mc["utri"], op=ALU.mult)
                    psum_claim(ctx, 3, t_cb[qg])
                pw0 = psum_wait(ctx, bdq)
                md = None
                for j in range(4):
                    col = (qi * 4 + j) * 128
                    p.emit("pe", "matmul",
                           ps[bdq][:, j * 128:(j + 1) * 128],
                           ones1, abuf[par][0:1, col:col + 128],
                           waits=r_gl[qg] + ([pw0] if pw0 and j == 0
                                             else []),
                           inc=None, start=True, stop=False)
                    md = p.emit("pe", "matmul",
                                ps[bdq][:, j * 128:(j + 1) * 128],
                                abuf[par][0:1, col:col + 128], m1row,
                                waits=(), inc=("s_pe" if j == 3 else None),
                                start=False, stop=True)
                if qi == 3:
                    md_of[(i, qg)] = md
                t_cl = p.emit("dve", "tensor_scalar", waits=[md], inc="s_dve",
                              out=ps[bdq][:, :512], in0=ps[bdq][:, :512],
                              scalar1=0.0, scalar2=None, op0=ALU.min)
                t_me = p.emit("act", "activation", Mexp, ps[bdq][:, :512],
                              AF.Exp, waits=[t_cl], inc="s_act")
                psum_claim(ctx, bdq, t_me)
                cbb = cbt[qg % 2].unsqueeze(1).broadcast_to([128, 4, 128])
                tmq_of[q] = p.emit("dve", "tensor_tensor",
                                   waits=[t_me, t_cb[qg]], inc="s_dve",
                                   out=mqs[q % 2], in0=Mexp, in1=cbb,
                                   op=ALU.mult)
            if q >= 1:
                qq = q - 1
                gq = qq // 4
                pw1 = psum_wait(ctx, 2)
                for j in range(4):
                    h = 4 * qq + j
                    p.emit("pe", "matmul", ps[2][:, j * 64:(j + 1) * 64],
                           mqs[qq % 2][:, j * 128:(j + 1) * 128],
                           hc[:, h * 64:(h + 1) * 64],
                           waits=[tmq_of[qq], t_hc] +
                           ([pw1] if pw1 and j == 0 else []),
                           inc=None, start=True, stop=True)
                mo = p.emit("pe", "matmul", ps[2][:, 256:512],
                            mc["CT"][:, gq * TOK + i * 128:
                                     gq * TOK + i * 128 + 128],
                            Rb[:, qq * 256:(qq + 1) * 256],
                            waits=[t_rb] + mc["bc_toks"], inc="s_pe",
                            start=True, stop=True)
                my = mo
                ecsb = EcsT[:, 4 * qq:4 * qq + 4]\
                    .rearrange("p (h o) -> p h o", o=1)\
                    .broadcast_to([128, 4, 64])
                t_t1 = p.emit("dve", "tensor_tensor",
                              waits=[mo, t_ecs, ysc_tt] +
                              ([info[i - 1]["tq_rd"]] if i >= 1 and qq == 0
                               else []),
                              inc="s_dve",
                              out=tq, in0=ps[2][:, 256:512], in1=ecsb,
                              op=ALU.mult)
                t_t2 = p.emit("dve", "tensor_add", waits=[t_t1], inc="s_dve",
                              out=tq, in0=tq, in1=ps[2][:, 0:256])
                psum_claim(ctx, 2, t_t2)
                ysc_tt = p.emit("dve", "tensor_add", waits=[t_t2, t_dres],
                                inc="s_dve",
                                out=ysc[:, qq * 256:(qq + 1) * 256],
                                in0=ysc[:, qq * 256:(qq + 1) * 256], in1=tq)
        mc["scan_md_hist"] = {k: v for k, v in md_of.items() if k[0] == i}
        diff_last = my
        t_ht = p.emit("dve", "tensor_tensor",
                      waits=[t_hc, t_dect, ("s_pe", p.counts["s_pe"])],
                      inc="s_dve",
                      out=hten, in0=hc,
                      in1=dec_t.rearrange("p (h o) -> p h o", o=1)
                      .broadcast_to([128, 128, 64]), op=ALU.mult)
        r_done = None
        ms2 = None
        for g in range(8):
            pw4 = psum_wait(ctx, 4)
            pw5 = psum_wait(ctx, 5)
            ms1 = p.emit("pe", "matmul", ps[4][:, :512],
                         B_tok[:, g * 128:(g + 1) * 128],
                         hten[:, g * 1024:g * 1024 + 512],
                         waits=[ev_bt, t_ht] + ([pw4] if pw4 else []),
                         inc="s_pe", start=True, stop=True)
            ms2 = p.emit("pe", "matmul", ps[5][:, :512],
                         B_tok[:, g * 128:(g + 1) * 128],
                         hten[:, g * 1024 + 512:(g + 1) * 1024],
                         waits=[ev_bt, t_ht] + ([pw5] if pw5 else []),
                         inc="s_pe", start=True, stop=True)
            elb = el_b[:, 16 * g:16 * g + 16].rearrange("p (h o) -> p h o", o=1)\
                .broadcast_to([128, 16, 64])
            t_r1 = p.emit("dve", "tensor_tensor",
                          waits=[t_elb, t_rb] + ([r_done] if r_done else []),
                          inc="s_dve",
                          out=R[:, g * 1024:(g + 1) * 1024],
                          in0=R[:, g * 1024:(g + 1) * 1024], in1=elb, op=ALU.mult)
            t_r2 = p.emit("dve", "tensor_add", waits=[t_r1, ms1], inc="s_dve",
                          out=R[:, g * 1024:g * 1024 + 512],
                          in0=R[:, g * 1024:g * 1024 + 512], in1=ps[4][:, :512])
            t_r3 = p.emit("dve", "tensor_add", waits=[t_r1, ms2], inc="s_dve",
                          out=R[:, g * 1024 + 512:(g + 1) * 1024],
                          in0=R[:, g * 1024 + 512:(g + 1) * 1024], in1=ps[5][:, :512])
            psum_claim(ctx, 4, t_r2)
            psum_claim(ctx, 5, t_r3)
            r_done = t_r3
        st = p.dma("pool", mc["ysc_spill"][i], ysc, "yst", waits=[ysc_tt])
        info[i] = {"hid_rd": t_ht, "el_rd": m_al2, "elb_rd": r_done,
                   "dtT_rd": t_hc, "acsT_rd": [t_ecs, r_al2],
                   "decd_rd": t_dect,
                   "dect_rd": t_ht, "ecs_rd": t_t1, "hc_rd": t_ht,
                   "btok_rd": ms2, "tq_rd": ysc_tt, "ysc_st": st,
                   "r_done": r_done, "diff_last": diff_last}
    mc["r_final_tok"] = r_done
    mc["ysc_st_all"] = p.stream_all("yst")
    phase_war(p, ar, stores=("yst",))
    ar.reset(m0)


def emit_mamba_finish(p, ctx, mc):
    ar = ctx["ar"]
    m0 = ar.mark()
    ps = ctx["psum"]
    R = mc["R"]
    S0G = ar.alloc([128, 8192], BF16)
    S0b = ar.alloc([128, 8192], BF16)
    ysb = ar.alloc([128, 8192], BF16)
    sg_b = ar.alloc([128, 8192], BF16)
    yfin = ar.alloc([128, 8192], BF16)
    sq8 = ar.alloc([128, 8192], BF16)
    acsT = ar.alloc([128, 128], F32)
    EcsT = ar.alloc([128, 128], F32)
    g_row = ar.alloc([1, 128], F32)
    el_row = ar.alloc([1, 128], F32)
    tq = ar.alloc([128, 256], F32)
    gsum = ar.alloc([128, 16], F32)
    ones1 = ar.alloc([1, 128], F32)

    t_on = p.emit("dve", "memset", ones1, 1.0, waits=ar.war, inc="s_dve")
    b1 = p.dma("pool", mc["r_bounce_d"], R, "pb", waits=[mc["r_final_tok"]])
    ag = p.raw("pool", lambda g: g.collective_compute(
        "AllGather", ALU.bypass, replica_groups=[list(range(mc["ncores"]))],
        ins=[mc["r_bounce_d"].opt()], outs=[mc["r_all_d"].opt()]),
        waits=[b1], inc="s_cc")
    ncr = mc["ncores"]
    def rd(g):
        prev = (g.partition_id() + (ncr - 1)) % ncr
        return g.dma_start(out=R, in_=mc["r_all_d"][bass.ts(prev, 128), :])
    t_rdR = p.raw("pool", rd, waits=[ag], inc="hin0", inc_n=16)
    t_s0 = p.emit("dve", "tensor_scalar", waits=[t_rdR, mc["flag_tok"]],
                  inc="s_dve", out=S0b, in0=R,
                  scalar1=mc["flag"][:, 0:1], scalar2=None, op0=ALU.mult)
    t_g = p.emit("dve", "memset", g_row, 1.0, waits=ar.war, inc="s_dve")

    info = {}
    for i in range(NCHUNK):
        yw = list(mc["ysc_st_all"]) if i == 0 else [info[i - 1]["y_rd"]]
        ld_y = p.dma("sp", ysb, mc["ysc_spill"][i], "w", waits=yw)
        gw = list(ar.war) if i == 0 else [info[i - 1]["g_rd"]]
        ld_g = p.dma("sp", sg_b, mc["sgate_tok_spill"][i], "x", waits=gw)
        b_acs = p.dma("pool", mc["acs_d"], mc["A_cs"][:, i * 128:(i + 1) * 128],
                      "hin", waits=[mc["A_cs_tok"]] + p.stream_all("hin"))
        pw6 = psum_wait(ctx, 6)
        tr_ac = p.emit("pe", "transpose", ps[6][:, :128],
                       mc["A_cs"][:, i * 128:(i + 1) * 128], mc["ident_f"],
                       waits=[mc["A_cs_tok"]] + ([pw6] if pw6 else []), inc="s_pe")
        ev_ac = p.emit("dve", "tensor_copy", waits=[tr_ac] +
                       ([info[i - 1]["acsT_rd"]] if i >= 1 else []), inc="s_dve",
                       out=acsT, in_=ps[6][:, :128])
        psum_claim(ctx, 6, ev_ac)
        t_ecs = p.emit("act", "activation", EcsT, acsT, AF.Exp,
                       waits=[ev_ac] + ([info[i - 1]["ecs_rd"]] if i >= 1 else []),
                       inc="s_act")
        pw3 = psum_wait(ctx, 3)
        m_g = p.emit("pe", "matmul", ps[3][:, :128], ones1, g_row,
                     waits=[t_g, t_on] + ([pw3] if pw3 else []), inc="s_pe",
                     start=True, stop=True)
        gb = ps[3][:, :128].rearrange("p (h o) -> p h o", o=1).broadcast_to([128, 128, 64])
        t_s0g = p.emit("dve", "tensor_tensor", waits=[t_s0, m_g] +
                       ([info[i - 1]["s0g_rd"]] if i >= 1 else []), inc="s_dve",
                       out=S0G, in0=S0b, in1=gb, op=ALU.mult)
        psum_claim(ctx, 3, t_s0g)
        r3 = p.dma("pool", el_row, mc["acs_d"][:, 127:128], "hin",
                   waits=[b_acs, m_g] + ([info[i - 1]["el2_rd"]] if i >= 1 else []),
                   allow_slow_non_contiguous=True)
        t_el2 = p.emit("act", "activation", el_row, el_row, AF.Exp,
                       waits=[r3], inc="s_act")
        t_g = p.emit("dve", "tensor_mul", waits=[t_g, t_el2, t_s0g], inc="s_dve",
                     out=g_row, in0=g_row, in1=el_row)
        t_ys = None
        mi = None
        for q in range(32):
            g = q // 4
            pw1 = psum_wait(ctx, 1)
            mi = p.emit("pe", "matmul", ps[1][:, :256],
                        mc["CT"][:, g * TOK + i * 128: g * TOK + i * 128 + 128],
                        S0G[:, q * 256:(q + 1) * 256],
                        waits=[t_s0g] + mc["bc_toks"] + ([pw1] if pw1 else []),
                        inc="s_pe", start=True, stop=True)
            ecsb = EcsT[:, 4 * q:4 * q + 4].rearrange("p (h o) -> p h o", o=1)\
                .broadcast_to([128, 4, 64])
            t_t1 = p.emit("dve", "tensor_tensor", waits=[mi, t_ecs] +
                          ([t_ys] if t_ys is not None else []) +
                          ([info[i - 1]["tq_rd"]] if i >= 1 and q == 0 else []),
                          inc="s_dve",
                          out=tq, in0=ps[1][:, :256], in1=ecsb, op=ALU.mult)
            psum_claim(ctx, 1, t_t1)
            t_ys = p.emit("dve", "tensor_add", waits=[t_t1, ld_y], inc="s_dve",
                          out=ysb[:, q * 256:(q + 1) * 256],
                          in0=ysb[:, q * 256:(q + 1) * 256], in1=tq)
        t_yg = p.emit("dve", "tensor_mul", waits=[t_ys, ld_g], inc="s_dve",
                      out=ysb, in0=ysb, in1=sg_b)
        t_sq = p.emit("act", "activation", sq8, ysb, AF.Square,
                      waits=[t_yg] + ([info[i - 1]["sq_rd"]] if i >= 1 else []),
                      inc="s_act")
        t_gs = None
        for g in range(8):
            t_gs = p.emit("dve", "tensor_reduce", gsum[:, g:g + 1],
                          sq8[:, g * 1024:(g + 1) * 1024], AX.X, ALU.add,
                          waits=[t_sq], inc="s_dve")
        t_pre = p.emit("dve", "tensor_scalar", waits=[t_gs], inc="s_dve",
                       out=gsum[:, 8:16], in0=gsum[:, 0:8], scalar1=1.0 / 1024,
                       scalar2=EPS, op0=ALU.mult, op1=ALU.add)
        t_ln = p.emit("act", "activation", gsum[:, 8:16], gsum[:, 8:16], AF.Ln,
                      waits=[t_pre], inc="s_act")
        t_rs = p.emit("act", "activation", gsum[:, 8:16], gsum[:, 8:16], AF.Exp,
                      scale=-0.5, waits=[t_ln], inc="s_act")
        t_ap = None
        st_prev = [info[i - 1]["yst_st"]] if i >= 1 else []
        for g in range(8):
            t_ap = p.emit("dve", "tensor_scalar", waits=[t_rs] +
                          (st_prev + [info[i - 1]["yf_rd"]] if i >= 1 and g == 0 else []),
                          inc="s_dve",
                          out=yfin[:, g * 1024:(g + 1) * 1024],
                          in0=ysb[:, g * 1024:(g + 1) * 1024],
                          scalar1=gsum[:, 8 + g:9 + g], scalar2=None, op0=ALU.mult)
        evt = None
        tr = None
        for ct in range(64):
            bank = 4 + (ct % 2)
            pb = ps[bank].bitcast(BF16)
            pw = psum_wait(ctx, bank)
            tr = p.emit("pe", "transpose", pb[:, :128],
                        yfin[:, ct * 128:(ct + 1) * 128], mc["ident_bf"],
                        waits=[t_ap] + ([pw] if pw else []), inc="s_pe")
            evt = p.emit("dve", "tensor_copy", waits=[tr], inc="s_dve",
                         out=yfin[:, ct * 128:(ct + 1) * 128], in_=pb[:, :128])
            psum_claim(ctx, bank, evt)
        st = p.dma("pool", mc["ych_spill"][i], yfin, "yst", waits=[evt])
        info[i] = {"y_rd": t_ap, "g_rd": t_yg, "acsT_rd": t_ecs, "ecs_rd": t_t1,
                   "s0g_rd": mi, "el2_rd": t_el2, "sq_rd": t_gs, "yf_rd": tr,
                   "yst_st": st, "tq_rd": t_ys}
    mc["ych_st_all"] = p.stream_all("yst")
    phase_war(p, ar, stores=("yst",))
    ar.reset(m0)


def mamba_dram(nc, name="", ncores=8):
    d = {}
    d["hbc_spill"] = nc.dram_tensor(name + "hbc_sp", [80, 128, TOK], BF16).ap()
    d["sgate_tok_spill"] = nc.dram_tensor(name + "sg_sp", [8, 128, 8192], BF16).ap()
    d["hid_tok_spill"] = nc.dram_tensor(name + "hid_sp", [8, 128, 8192], BF16).ap()
    d["ysc_spill"] = nc.dram_tensor(name + "ysc_sp", [8, 128, 8192], BF16).ap()
    d["ych_spill"] = nc.dram_tensor(name + "ych_sp", [8, 128, 8192], BF16).ap()
    d["acs_d"] = nc.dram_tensor(name + "acs_d", [128, 128], F32).ap()
    d["halo_bounce_d"] = nc.dram_tensor(name + "halo_b", [128, 240], F32).ap()
    d["halo_all_d"] = nc.dram_tensor(name + "halo_a", [ncores * 128, 240], F32).ap()
    d["r_bounce_d"] = nc.dram_tensor(name + "r_b", [128, 8192], F32).ap()
    d["r_all_d"] = nc.dram_tensor(name + "r_a", [ncores * 128, 8192], F32).ap()
    return d


def emit_mamba_sublayer(p, ctx, nc, *, x_ap, xout_ap, inproj_cache_ap,
                        outproj_cache_ap, dins, dd, ncores=8, op_w_ready=()):
    ar = ctx["ar"]
    mpers = ar.mark()
    mc = dict(dd)
    mc["ncores"] = ncores
    mc["BT"] = ar.alloc([128, 8 * TOK], BF16)
    mc["CT"] = ar.alloc([128, 8 * TOK], BF16)
    mc["dt_raw"] = ar.alloc([128, TOK], F32)
    mc["dt"] = ar.alloc([128, TOK], F32)
    mc["A_full"] = ar.alloc([128, TOK], F32)
    mc["A_cs"] = ar.alloc([128, TOK], F32)
    mc["sp_a"] = ar.alloc([128, TOK], F32)
    mc["sp_m"] = ar.alloc([128, TOK], F32)
    mc["halo_out"] = ar.alloc([128, 240], F32)
    mc["gnw"] = ar.alloc([128, 64], F32)
    mc["convw"] = ar.alloc([128, 320], F32)
    mc["convb"] = ar.alloc([128, 80], F32)
    mc["dt_bias"] = ar.alloc([128, 1], F32)
    mc["A_log"] = ar.alloc([128, 1], F32)
    mc["cA"] = ar.alloc([128, 1], F32)
    mc["flag"] = ar.alloc([128, 1], F32)
    mc["utri"] = ar.alloc([128, 128], BF16)
    mc["ident_bf"] = ctx["ident_bf"]
    mc["ident_f"] = ctx["ident_f"]
    gamma = ar.alloc([128, NKD], F32)
    mc["dp_row_d"] = dins["dp_row_d"]
    mc["convw_d"] = dins["convw_d"]
    mc["convb_d"] = dins["convb_d"]

    toks = []
    for (dst, key) in ((gamma, "norm_w_g"), (mc["gnw"], "gnw_d"),
                       (mc["dt_bias"], "dt_bias_d"), (mc["A_log"], "A_log_d"),
                       (mc["flag"], "flag_d"), (mc["utri"], "utri_d")):
        toks.append(p.dma("sp", dst, dins[key], "misc", waits=ar.war))
    mc["flag_tok"] = toks[4]
    mc["utri_tok"] = toks[5]
    war0 = list(ar.war)
    ar.war = ar.war + toks

    xnT_mark = ar.mark()
    xnT = ar.alloc([128, NKD * TOK], BF16)
    k_ready, ln_last = emit_ln_transpose(p, ctx, x_ap=x_ap, xnT=xnT,
                                         mamba_rms=True, gamma=gamma,
                                         x_ready=[toks[0]], light=True)
    ar.war = ar.war + [ln_last]
    conv_on_idx, conv_finish = make_conv_stepper(p, ctx, mc)
    ip_tok = emit_inproj(p, ctx, mc, cache_ap=inproj_cache_ap, xnT=xnT,
                         x_ready_toks=[ln_last], hook=conv_on_idx)
    conv_finish()
    phase_war(p, ar, stores=("hout",))
    ar.reset(xnT_mark)
    mc["R"] = ar.alloc([128, 8192], F32)
    emit_mamba_scan(p, ctx, mc)
    emit_mamba_finish(p, ctx, mc)
    ych_view = dd["ych_spill"].rearrange("i p (c m) -> c p i m", m=128)
    emit_ffn_mm2(p, ctx, wd_cache_ap=outproj_cache_ap, h_ap=ych_view,
                 x_ap=x_ap, xout_ap=xout_ap, h_ready_toks=mc["ych_st_all"],
                 nk=64, scale=1.0, w_ready=op_w_ready)
    ar.reset(mpers)
    ar.war = war0 + [p.tok("s_pe"), p.tok("s_act"), p.tok("s_dve"),
                     p.tok("s_pool")] + p.stream_all("xst")
    return p.stream_all("xst")



def emit_l1ffn(p, ctx, *, x_ap, xout_ap, w0_cache_ap, a_d, b_d, b0_d, mask_d,
               n_oc=8, n_tt=8):
    ar = ctx["ar"]
    m0 = ar.mark()
    ps = ctx["psum"]
    xnT = ar.alloc([128, NKD * TOK], BF16)
    k_ready, ln_last = emit_ln_transpose(p, ctx, x_ap=x_ap, xnT=xnT,
                                         x_ready=(), n_tt=n_tt, nkd=NKD)
    AT = ar.alloc([128, NKD * 64], BF16)
    BTl = [ar.alloc([16, 4096], BF16) for _ in range(4)]
    hTl = [ar.alloc([16, TOK], BF16) for _ in range(4)]
    anat = ar.alloc([64, 4096], F32)
    bnat = ar.alloc([128, 512], F32)
    b0r = ar.alloc([1, 4096], F32)
    ones1 = ar.alloc([1, 128], F32)
    w0buf = ar.alloc([128, NKD * 512], BF16)
    htmp = ar.alloc([128, 64], BF16)
    mk = [ar.alloc([128, 512], BF16) for _ in range(4)]
    xb = [ar.alloc([128, 512], F32) for _ in range(2)]
    ac = [ar.alloc([128, 512], F32) for _ in range(3)]
    ob = [ar.alloc([128, 512], F32) for _ in range(2)]

    t_on = p.emit("dve", "memset", ones1, 1.0, waits=ar.war, inc="s_dve")
    l_a = p.dma("sp", anat, a_d, "misc", waits=ar.war)
    l_b0 = p.dma("sp", b0r, b0_d, "misc", waits=ar.war)
    ta = None
    for k in range(NKD):
        bank = 6 + (k % 2)
        pw = psum_wait(ctx, bank)
        tr = p.emit("pe", "transpose", ps[bank][:, :64],
                    anat[:, k * 128:(k + 1) * 128], ctx["ident_f"][0:64, 0:64],
                    waits=[l_a] + ([pw] if pw else []), inc="s_pe")
        ta = p.emit("dve", "tensor_copy", waits=[tr], inc="s_dve",
                    out=AT[:, k * 64:(k + 1) * 64], in_=ps[bank][:, :64])
        psum_claim(ctx, bank, ta)
    tb = None
    for l in range(4):
        l_bn = p.dma("sp", bnat, b_d[l].rearrange("(a p) r -> p a r", p=128),
                     "misc", waits=[tb] if tb else ar.war)
        for a in range(32):
            bank = 6 + (a % 2)
            pw = psum_wait(ctx, bank)
            tr = p.emit("pe", "transpose", ps[bank][0:16, :128],
                        bnat[:, a * 16:(a + 1) * 16], ctx["ident_f"],
                        waits=[l_bn] + ([pw] if pw else []), inc="s_pe")
            tb = p.emit("dve", "tensor_copy", waits=[tr], inc="s_dve",
                        out=BTl[l][:, a * 128:(a + 1) * 128], in_=ps[bank][0:16, :128])
            psum_claim(ctx, bank, tb)
    h_toks = []
    for t in range(n_tt):
        pw = psum_wait(ctx, 4)
        mm = None
        for k in range(NKD):
            mm = p.emit("pe", "matmul", ps[4][:, :64],
                        xnT[:, k * TOK + t * 128: k * TOK + (t + 1) * 128],
                        AT[:, k * 64:(k + 1) * 64],
                        waits=[ta, ln_last] + ([pw] if pw and k == 0 else []),
                        inc=("s_pe" if k == NKD - 1 else None),
                        start=(k == 0), stop=(k == NKD - 1))
        ev = p.emit("dve", "tensor_copy", waits=[mm] + (h_toks[-1:] if h_toks else []),
                    inc="s_dve", out=htmp, in_=ps[4][:, :64])
        psum_claim(ctx, 4, ev)
        ht = None
        for l in range(4):
            bank = 6 + (l % 2)
            pw = psum_wait(ctx, bank)
            tr = p.emit("pe", "transpose", ps[bank].bitcast(BF16)[0:16, :128],
                        htmp[:, l * 16:(l + 1) * 16], ctx["ident_bf"],
                        waits=[ev] + ([pw] if pw else []), inc="s_pe")
            ht = p.emit("dve", "tensor_copy", waits=[tr], inc="s_dve",
                        out=hTl[l][:, t * 128:(t + 1) * 128],
                        in_=ps[bank].bitcast(BF16)[0:16, :128])
            psum_claim(ctx, bank, ht)
        h_toks.append(ht)
    h_all = h_toks[-1]
    info = {}
    w0_rd_prev = None
    for oc in range(n_oc):
        w0_oc = (w0_cache_ap(oc) if callable(w0_cache_ap)
                 else w0_cache_ap[oc].rearrange("k p m -> p k m"))
        if not isinstance(w0_oc, (list, tuple)):
            w0_oc = [w0_oc]
        ld_w = []
        off = 0
        for seg in w0_oc:
            wdt = 1
            for s in seg.shape[1:]:
                wdt *= s
            ld_w.append(p.dma("sp", w0buf[:, off:off + wdt], seg, "w",
                              waits=([w0_rd_prev] if w0_rd_prev
                                     else list(ar.war))))
            off += wdt
        for t in range(n_tt):
            it = oc * n_tt + t
            par = it % 2
            xw = [info[it - 2]["xrd"]] if it - 2 in info else list(ar.war)
            ld_x = p.dma("sp", xb[par],
                         x_ap[t * 128:(t + 1) * 128, oc * 512:(oc + 1) * 512],
                         "x", waits=xw)
            mw = [info[it - 1]["mks"]] if it - 1 in info else []
            ld_m = [p.dma("pool", mk[l],
                          mask_d[l, t * 128:(t + 1) * 128, oc * 512:(oc + 1) * 512],
                          "hin", waits=mw) for l in range(4)]
            bb = 4 + par
            pwb = psum_wait(ctx, bb)
            mm = None
            for k in range(NKD):
                kw = []
                if k == 0:
                    kw = [ln_last] + ([pwb] if pwb else [])
                if k % 4 == 0 and t == 0:
                    kw = kw + [ld_w[min(k // 4, len(ld_w) - 1)]]
                mm = p.emit("pe", "matmul", ps[bb][:, :512],
                            xnT[:, k * TOK + t * 128: k * TOK + (t + 1) * 128],
                            w0buf[:, k * 512:(k + 1) * 512],
                            waits=kw, inc=None, start=(k == 0), stop=False)
            base_mm = p.emit("pe", "matmul", ps[bb][:, :512], ones1,
                             b0r[0:1, oc * 512:(oc + 1) * 512],
                             waits=[l_b0, t_on], inc="s_pe", start=False, stop=True)
            if t == n_tt - 1:
                w0_rd_prev = base_mm
            ymms = []
            for l in range(4):
                pwl = psum_wait(ctx, l)
                ym = p.emit("pe", "matmul", ps[l][:, :512],
                            hTl[l][:, t * 128:(t + 1) * 128],
                            BTl[l][:, oc * 512:(oc + 1) * 512],
                            waits=[h_all, tb] + ([pwl] if pwl else []),
                            inc=("s_pe" if l == 3 else None),
                            start=True, stop=True)
                ymms.append(ym)
            acc_w = [info[it - 1]["acc_rd"]] if it - 1 in info else []
            a0 = p.emit("dve", "tensor_tensor", waits=[ymms[-1]] + ld_m + acc_w,
                        inc="s_dve", out=ac[0], in0=ps[0][:, :512], in1=mk[0],
                        op=ALU.mult)
            a1 = p.emit("dve", "tensor_tensor", waits=[ymms[-1]] + ld_m,
                        inc="s_dve", out=ac[1], in0=ps[1][:, :512], in1=mk[1],
                        op=ALU.mult)
            s0 = p.emit("dve", "tensor_add", waits=[a0, a1], inc="s_dve",
                        out=ac[0], in0=ac[0], in1=ac[1])
            a2 = p.emit("dve", "tensor_tensor", waits=[ymms[-1]] + ld_m + [s0],
                        inc="s_dve", out=ac[1], in0=ps[2][:, :512], in1=mk[2],
                        op=ALU.mult)
            a3 = p.emit("dve", "tensor_tensor", waits=[ymms[-1]] + ld_m,
                        inc="s_dve", out=ac[2], in0=ps[3][:, :512], in1=mk[3],
                        op=ALU.mult)
            s1 = p.emit("dve", "tensor_add", waits=[a2, a3], inc="s_dve",
                        out=ac[1], in0=ac[1], in1=ac[2])
            s2 = p.emit("dve", "tensor_add", waits=[s0, s1], inc="s_dve",
                        out=ac[0], in0=ac[0], in1=ac[1])
            for l in range(4):
                psum_claim(ctx, l, s2)
            ob_w = [info[it - 2]["st"]] if it - 2 in info else []
            e1 = p.emit("dve", "scalar_tensor_tensor",
                        waits=[base_mm, ld_x, s2] + ob_w, inc="s_dve",
                        out=ob[par], in0=ps[bb][:, :512], scalar=0.5, in1=xb[par],
                        op0=ALU.mult, op1=ALU.add)
            psum_claim(ctx, bb, e1)
            e2 = p.emit("dve", "tensor_add", waits=[e1], inc="s_dve",
                        out=ob[par], in0=ob[par], in1=ac[0])
            st = p.dma("pool", xout_ap[t * 128:(t + 1) * 128, oc * 512:(oc + 1) * 512],
                       ob[par], "xst", waits=[e2])
            info[it] = {"st": st, "mks": a3, "acc_rd": e2, "xrd": e1}
    phase_war(p, ar, stores=("xst",))
    ar.reset(m0)
    return p.stream_all("xst")



def _l1_build_T(p, ctx, *, src_ap, dst, nrow_t, ncol_t, ld_buf, bf_buf,
                stream="misc", war=(), name="", ld_buf2=None, bf_buf2=None):
    ar = ctx["ar"]
    ps = ctx["psum"]
    lds = [ld_buf, ld_buf2 if ld_buf2 is not None else ld_buf]
    bfs = [bf_buf[:, :ncol_t * 128],
           (bf_buf2 if bf_buf2 is not None else bf_buf)[:, :ncol_t * 128]]
    nb = 2 if ld_buf2 is not None else 1
    cv_of = {}
    tr_of = {}
    ev = None
    for rb in range(nrow_t):
        par = rb % nb
        ld = p.dma("sp", lds[par], src_ap[rb * 128:(rb + 1) * 128, :], stream,
                   waits=list(war) + ([cv_of[rb - nb]] if rb - nb in cv_of
                                      else []))
        cvt = p.emit("dve", "tensor_copy",
                     waits=[ld] + ([tr_of[rb - nb]] if rb - nb in tr_of
                                   else []),
                     inc="s_dve", out=bfs[par], in_=lds[par])
        cv_of[rb] = cvt
        for k in range(ncol_t):
            bank = 6 + (k % 2)
            pw = psum_wait(ctx, bank)
            tr = p.emit("pe", "transpose", ps[bank].bitcast(BF16)[:, :128],
                        bfs[par][:, k * 128:(k + 1) * 128], ctx["ident_bf"],
                        waits=[cvt] + ([pw] if pw else []), inc="s_pe")
            if k == ncol_t - 1:
                tr_of[rb] = tr
            ev = p.emit("dve", "tensor_copy", waits=[tr], inc="s_dve",
                        out=dst[:, (k * nrow_t + rb) * 128:
                                (k * nrow_t + rb + 1) * 128],
                        in_=ps[bank].bitcast(BF16)[:, :128])
            psum_claim(ctx, bank, ev)
    return ev


def _l1_ln(p, ctx, *, h_s, out_s, w_ap, b_ap, silu, sq_s, ones_col,
           war=(), extra_rd=()):
    ps = ctx["psum"]
    st = ctx["l1_stat"]
    sq_t = None
    for j in range(4):
        sq_t = p.emit("act", "activation", sq_s[:, j * 1024:(j + 1) * 1024],
                      h_s[:, j * 1024:(j + 1) * 1024], AF.Square,
                      waits=list(war) + (list(extra_rd) if j == 0 else []),
                      inc="s_act")
    last = None
    for half in range(2):
        hw_ = list(war)
        for (src_s, dst_off, wtok) in ((h_s, half * 512, None),
                                       (sq_s, 1024 + half * 512, sq_t)):
            pw = psum_wait(ctx, 4)
            mm = None
            for j in range(4):
                mm = p.emit("pe", "matmul", ps[4][0:1, 0:512], ones_col,
                            src_s[:, j * 1024 + half * 512: j * 1024 + (half + 1) * 512],
                            waits=hw_ + ([wtok] if wtok else []) +
                            ([pw] if pw and j == 0 else []),
                            inc=("s_pe" if j == 3 else None),
                            start=(j == 0), stop=(j == 3))
            ev = p.emit("dve", "tensor_copy", waits=[mm], inc="s_dve",
                        out=st[0:1, dst_off:dst_off + 512], in_=ps[4][0:1, 0:512])
            psum_claim(ctx, 4, ev)
        o = half * 512
        t1 = p.emit("dve", "tensor_scalar", waits=[ev], inc="s_dve",
                    out=st[0:1, o:o + 512], in0=st[0:1, o:o + 512],
                    scalar1=1.0 / 512, scalar2=None, op0=ALU.mult)
        t2 = p.emit("dve", "tensor_scalar", waits=[ev], inc="s_dve",
                    out=st[0:1, 1024 + o:1024 + o + 512],
                    in0=st[0:1, 1024 + o:1024 + o + 512],
                    scalar1=1.0 / 512, scalar2=EPS, op0=ALU.mult, op1=ALU.add)
        t3 = p.emit("dve", "tensor_mul", waits=[t1], inc="s_dve",
                    out=st[0:1, 2048:2560], in0=st[0:1, o:o + 512],
                    in1=st[0:1, o:o + 512])
        t4 = p.emit("dve", "tensor_sub", waits=[t2, t3], inc="s_dve",
                    out=st[0:1, 1024 + o:1024 + o + 512],
                    in0=st[0:1, 1024 + o:1024 + o + 512], in1=st[0:1, 2048:2560])
        t5 = p.emit("act", "activation", st[0:1, 1024 + o:1024 + o + 512],
                    st[0:1, 1024 + o:1024 + o + 512], AF.Ln, waits=[t4], inc="s_act")
        t6 = p.emit("act", "activation", st[0:1, 1024 + o:1024 + o + 512],
                    st[0:1, 1024 + o:1024 + o + 512], AF.Exp, scale=-0.5,
                    waits=[t5], inc="s_act")
        pw5 = psum_wait(ctx, 5)
        mb = p.emit("pe", "matmul", ps[5][:, 0:512], ctx["l1_ones1"],
                    st[0:1, o:o + 512], waits=[t1] + ([pw5] if pw5 else []),
                    inc=None, start=True, stop=True)
        pw6 = psum_wait(ctx, 6)
        rb_ = p.emit("pe", "matmul", ps[6][:, 0:512], ctx["l1_ones1"],
                     st[0:1, 1024 + o:1024 + o + 512],
                     waits=[t6] + ([pw6] if pw6 else []), inc="s_pe",
                     start=True, stop=True)
        for j in range(4):
            sl = slice(j * 1024 + half * 512, j * 1024 + (half + 1) * 512)
            tj1 = p.emit("dve", "tensor_sub", waits=[rb_] +
                         (list(extra_rd) if j == 0 and half == 0 else []),
                         inc="s_dve",
                         out=out_s[:, sl], in0=h_s[:, sl], in1=ps[5][:, 0:512])
            tj2 = p.emit("dve", "tensor_tensor", waits=[tj1], inc="s_dve",
                         out=out_s[:, sl], in0=out_s[:, sl],
                         in1=ps[6][:, 0:512], op=ALU.mult)
            last = p.emit("act", "activation", out_s[:, sl], out_s[:, sl],
                          AF.Silu if silu else AF.Identity,
                          scale=w_ap[j], bias=b_ap[j], waits=[tj2], inc="s_act")
        psum_claim(ctx, 5, last)
        psum_claim(ctx, 6, last)
    return last


def _l1_sconv(p, ctx, mc1, *, y_s, h_s, lidx, y_tok):
    ar = ctx["ar"]
    ps = ctx["psum"]
    Call = mc1["Call"]; Sall = mc1["Sall"]; Rall = mc1["Rall"]
    sc = mc1["sc_scr"]
    u_re = mc1["u_re"]; u_im = mc1["u_im"]
    d_re = mc1["d_re"]; d_im = mc1["d_im"]
    rrb = mc1["rrb"]
    carr = mc1["carry"]
    toks = []
    scan_toks = {}
    tab_toks = {}
    for j in range(4):
        cidx = lidx * 4 + j
        C = Call[:, j * 1025:(j + 1) * 1025]
        S = Sall[:, j * 1025:(j + 1) * 1025]
        R = Rall[:, j * 1025:(j + 1) * 1025]
        pr = mc1["phzr"][:, cidx:cidx + 1]
        pi = mc1["phzi"][:, cidx:cidx + 1]
        war_t = [y_tok]
        t1 = p.emit("dve", "tensor_tensor", waits=[mc1["par_tok"]] + war_t,
                    inc="s_dve", out=sc[:, 0:1], in0=pr, in1=pr, op=ALU.mult)
        t2 = p.emit("dve", "scalar_tensor_tensor", waits=[t1], inc="s_dve",
                    out=sc[:, 0:1], in0=pi, scalar=pi, in1=sc[:, 0:1],
                    op0=ALU.mult, op1=ALU.add)
        t2 = p.emit("dve", "tensor_scalar", waits=[t2], inc="s_dve",
                    out=sc[:, 0:1], in0=sc[:, 0:1], scalar1=1e-37, scalar2=None,
                    op0=ALU.add)
        t3 = p.emit("act", "activation", sc[:, 1:2], sc[:, 0:1], AF.Ln,
                    waits=[t2], inc="s_act")
        t4 = p.emit("act", "activation", sc[:, 1:2], sc[:, 1:2], AF.Exp,
                    scale=0.5, waits=[t3], inc="s_act")
        t5 = p.emit("act", "activation", sc[:, 2:3], sc[:, 1:2], AF.Tanh,
                    waits=[t4], inc="s_act")
        t6 = p.emit("dve", "reciprocal", sc[:, 3:4], sc[:, 1:2], waits=[t4],
                    inc="s_dve")
        t7 = p.emit("dve", "tensor_tensor", waits=[t6], inc="s_dve",
                    out=sc[:, 4:5], in0=pr, in1=sc[:, 3:4], op=ALU.mult)
        t8 = p.emit("dve", "tensor_tensor", waits=[t6], inc="s_dve",
                    out=sc[:, 5:6], in0=pi, in1=sc[:, 3:4], op=ALU.mult)
        tc = p.emit("dve", "memset", C[:, 0:1], 1.0, waits=war_t, inc="s_dve")
        ts_ = p.emit("dve", "memset", S[:, 0:1], 0.0, waits=war_t, inc="s_dve")
        tr_ = p.emit("dve", "memset", R[:, 0:1], 1.0, waits=war_t, inc="s_dve")
        base = 1
        ctok, stok, rtok = tc, ts_, tr_
        cb, sb, rb = sc[:, 4:5], sc[:, 5:6], sc[:, 2:3]
        first = True
        while base < 1025:
            n = min(base, 1025 - base)
            if first:
                ctok = p.emit("dve", "tensor_copy", waits=[t7, tc], inc="s_dve",
                              out=C[:, 1:2], in_=sc[:, 4:5])
                stok = p.emit("dve", "tensor_copy", waits=[t8, ts_], inc="s_dve",
                              out=S[:, 1:2], in_=sc[:, 5:6])
                rtok = p.emit("dve", "tensor_copy", waits=[t5, tr_], inc="s_dve",
                              out=R[:, 1:2], in_=sc[:, 2:3])
                first = False
            else:
                b2 = base // 2
                mm1_ = p.emit("dve", "tensor_tensor", waits=[ctok, stok], inc="s_dve",
                              out=sc[:, 17:18], in0=C[:, b2:b2 + 1],
                              in1=C[:, b2:b2 + 1], op=ALU.mult)
                mm2_ = p.emit("dve", "scalar_tensor_tensor", waits=[mm1_], inc="s_dve",
                              out=sc[:, 17:18], in0=S[:, b2:b2 + 1],
                              scalar=S[:, b2:b2 + 1], in1=sc[:, 17:18],
                              op0=ALU.mult, op1=ALU.subtract)
                mm2_ = p.emit("dve", "tensor_scalar", waits=[mm2_], inc="s_dve",
                              out=sc[:, 17:18], in0=sc[:, 17:18], scalar1=-1.0,
                              scalar2=None, op0=ALU.mult)
                mm3_ = p.emit("dve", "tensor_tensor", waits=[ctok, stok], inc="s_dve",
                              out=sc[:, 18:19], in0=S[:, b2:b2 + 1],
                              in1=C[:, b2:b2 + 1], op=ALU.mult)
                mm3_ = p.emit("dve", "tensor_scalar", waits=[mm3_], inc="s_dve",
                              out=sc[:, 18:19], in0=sc[:, 18:19], scalar1=2.0,
                              scalar2=None, op0=ALU.mult)
                mm4_ = p.emit("dve", "tensor_tensor", waits=[rtok], inc="s_dve",
                              out=sc[:, 19:20], in0=R[:, b2:b2 + 1],
                              in1=R[:, b2:b2 + 1], op=ALU.mult)
                cbase = sc[:, 17:18]
                sbase = sc[:, 18:19]
                rbase = sc[:, 19:20]
                m1 = p.emit("dve", "tensor_scalar", waits=[mm2_, mm3_], inc="s_dve",
                            out=mc1["tmp1"][:, 0:n],
                            in0=S[:, 0:n], scalar1=sbase, scalar2=None, op0=ALU.mult)
                ctok2 = p.emit("dve", "scalar_tensor_tensor", waits=[m1], inc="s_dve",
                               out=C[:, base:base + n], in0=C[:, 0:n], scalar=cbase,
                               in1=mc1["tmp1"][:, 0:n], op0=ALU.mult, op1=ALU.subtract)
                m2 = p.emit("dve", "tensor_scalar", waits=[ctok, stok, ctok2],
                            inc="s_dve",
                            out=mc1["tmp1"][:, 0:n], in0=S[:, 0:n], scalar1=cbase,
                            scalar2=None, op0=ALU.mult)
                stok2 = p.emit("dve", "scalar_tensor_tensor", waits=[m2, ctok2],
                               inc="s_dve",
                               out=S[:, base:base + n], in0=C[:, 0:n], scalar=sbase,
                               in1=mc1["tmp1"][:, 0:n], op0=ALU.mult, op1=ALU.add)
                rtok = p.emit("dve", "tensor_scalar", waits=[rtok, mm4_], inc="s_dve",
                              out=R[:, base:base + n], in0=R[:, 0:n], scalar1=rbase,
                              scalar2=None, op0=ALU.mult)
                ctok, stok = ctok2, stok2
            base *= 2
        tabs = (ctok, stok, rtok)
        tab_toks[j] = stok
        uw = [y_tok, ctok, stok]
        t_ur = p.emit("dve", "tensor_tensor", waits=uw + ([scan_toks[j - 1][2]] if j - 1 in scan_toks else []),
                      inc="s_dve",
                      out=u_re, in0=y_s[:, j * 1024:(j + 1) * 1024],
                      in1=C[:, 0:1024], op=ALU.mult)
        t_ui = p.emit("dve", "tensor_tensor", waits=uw, inc="s_dve",
                      out=u_im, in0=y_s[:, j * 1024:(j + 1) * 1024],
                      in1=S[:, 0:1024], op=ALU.mult)
        lr_ = mc1["lcr"][:, cidx:cidx + 1]
        li_ = mc1["lci"][:, cidx:cidx + 1]
        i1 = p.emit("dve", "tensor_tensor", waits=[t7, mc1["par_tok"]], inc="s_dve",
                    out=sc[:, 6:7], in0=lr_, in1=sc[:, 4:5], op=ALU.mult)
        i2 = p.emit("dve", "scalar_tensor_tensor", waits=[i1, t8], inc="s_dve",
                    out=sc[:, 6:7], in0=li_, scalar=sc[:, 5:6], in1=sc[:, 6:7],
                    op0=ALU.mult, op1=ALU.subtract)
        i2 = p.emit("dve", "tensor_scalar", waits=[i2], inc="s_dve",
                    out=sc[:, 6:7], in0=sc[:, 6:7], scalar1=-1.0, scalar2=None,
                    op0=ALU.mult)
        i3 = p.emit("dve", "tensor_tensor", waits=[t8], inc="s_dve",
                    out=sc[:, 7:8], in0=lr_, in1=sc[:, 5:6], op=ALU.mult)
        i4 = p.emit("dve", "scalar_tensor_tensor", waits=[i3, t7], inc="s_dve",
                    out=sc[:, 7:8], in0=li_, scalar=sc[:, 4:5], in1=sc[:, 7:8],
                    op0=ALU.mult, op1=ALU.add)
        i4 = p.emit("dve", "tensor_scalar", waits=[i4], inc="s_dve",
                    out=sc[:, 7:8], in0=sc[:, 7:8], scalar1=-1.0, scalar2=None,
                    op0=ALU.mult)
        rrb_t = p.emit("dve", "tensor_scalar", waits=[t5, mc1["ones_tok"]] +
                       ([scan_toks[j - 1][2]] if j - 1 in scan_toks else []),
                       inc="s_dve",
                       out=rrb, in0=mc1["ones_t"], scalar1=sc[:, 2:3],
                       scalar2=None, op0=ALU.mult)
        s_re = p.emit("dve", "tensor_tensor_scan", waits=[t_ur, rrb_t, i2],
                      inc="s_dve", out=d_re[:, j * 1024:(j + 1) * 1024],
                      data0=rrb, data1=u_re, initial=sc[:, 6:7],
                      op0=ALU.mult, op1=ALU.add)
        s_im = p.emit("dve", "tensor_tensor_scan", waits=[t_ui, rrb_t, i4],
                      inc="s_dve", out=d_im[:, j * 1024:(j + 1) * 1024],
                      data0=rrb, data1=u_im, initial=sc[:, 7:8],
                      op0=ALU.mult, op1=ALU.add)
        c1 = p.emit("dve", "tensor_tensor", waits=[s_re, ctok], inc="s_dve",
                    out=carr[:, 2 * j:2 * j + 1], in0=C[:, 1023:1024],
                    in1=d_re[:, j * 1024 + 1023:j * 1024 + 1024], op=ALU.mult)
        c2 = p.emit("dve", "scalar_tensor_tensor", waits=[c1, s_im, stok], inc="s_dve",
                    out=carr[:, 2 * j:2 * j + 1],
                    in0=d_im[:, j * 1024 + 1023:j * 1024 + 1024],
                    scalar=S[:, 1023:1024], in1=carr[:, 2 * j:2 * j + 1],
                    op0=ALU.mult, op1=ALU.add)
        c3 = p.emit("dve", "tensor_tensor", waits=[s_re, stok], inc="s_dve",
                    out=carr[:, 2 * j + 1:2 * j + 2], in0=S[:, 1023:1024],
                    in1=d_re[:, j * 1024 + 1023:j * 1024 + 1024], op=ALU.mult)
        c4 = p.emit("dve", "scalar_tensor_tensor", waits=[c3, s_im, ctok], inc="s_dve",
                    out=carr[:, 2 * j + 1:2 * j + 2],
                    in0=d_im[:, j * 1024 + 1023:j * 1024 + 1024],
                    scalar=C[:, 1023:1024], in1=carr[:, 2 * j + 1:2 * j + 2],
                    op0=ALU.mult, op1=ALU.subtract)
        c4b = p.emit("dve", "tensor_scalar", waits=[c4], inc="s_dve",
                     out=carr[:, 2 * j + 1:2 * j + 2],
                     in0=carr[:, 2 * j + 1:2 * j + 2], scalar1=-1.0, scalar2=None,
                     op0=ALU.mult)
        scan_toks[j] = (s_re, s_im, c4b, tabs, (i2, i4))
    bt = p.dma("pool", mc1["carr_b_d"], carr[:, 0:8], "pb",
               waits=[scan_toks[j][2] for j in range(4)])
    ag = p.raw("pool", lambda g: g.collective_compute(
        "AllGather", ALU.bypass, replica_groups=[list(range(mc1["ncores"]))],
        ins=[mc1["carr_b_d"].opt()], outs=[mc1["carr_a_d"].opt()]),
        waits=[bt], inc="s_cc")
    allc = mc1["allc"]
    ld_all = p.dma("pool", allc,
                   mc1["carr_a_d"].rearrange("(r p) c -> p r c", p=128),
                   "pb", waits=[ag] + ([mc1["allc_war"]]
                                       if mc1.get("allc_war") else []))
    t_rdc = p.emit("dve", "tensor_scalar", waits=[ld_all, mc1["par_tok"]],
                   inc="s_dve", out=carr[:, 8:16], in0=allc[:, 0:8],
                   scalar1=mc1["rksel"][:, 0:1], scalar2=None, op0=ALU.mult)
    for rr in range(1, mc1["ncores"]):
        t_rdc = p.emit("dve", "scalar_tensor_tensor", waits=[t_rdc], inc="s_dve",
                       out=carr[:, 8:16], in0=allc[:, 8 * rr:8 * rr + 8],
                       scalar=mc1["rksel"][:, rr:rr + 1], in1=carr[:, 8:16],
                       op0=ALU.mult, op1=ALU.add)
    mc1["allc_war"] = t_rdc

    h_last = None
    for j in range(4):
        cidx = lidx * 4 + j
        (s_re, s_im, _c, tabs, inits) = scan_toks[j]
        ctok, stok, rtok = tabs
        pr = mc1["phzr"][:, cidx:cidx + 1]
        pi = mc1["phzi"][:, cidx:cidx + 1]
        t1 = p.emit("dve", "tensor_tensor", waits=[mc1["par_tok"], t_rdc], inc="s_dve",
                    out=sc[:, 8:9], in0=pr, in1=pr, op=ALU.mult)
        t2 = p.emit("dve", "scalar_tensor_tensor", waits=[t1], inc="s_dve",
                    out=sc[:, 8:9], in0=pi, scalar=pi, in1=sc[:, 8:9],
                    op0=ALU.mult, op1=ALU.add)
        t2 = p.emit("dve", "tensor_scalar", waits=[t2], inc="s_dve",
                    out=sc[:, 8:9], in0=sc[:, 8:9], scalar1=1e-37, scalar2=None,
                    op0=ALU.add)
        t3 = p.emit("act", "activation", sc[:, 9:10], sc[:, 8:9], AF.Ln,
                    waits=[t2], inc="s_act")
        t4 = p.emit("act", "activation", sc[:, 9:10], sc[:, 9:10], AF.Exp,
                    scale=0.5, waits=[t3], inc="s_act")
        t6 = p.emit("dve", "reciprocal", sc[:, 10:11], sc[:, 9:10], waits=[t4],
                    inc="s_dve")
        t7 = p.emit("dve", "tensor_tensor", waits=[t6], inc="s_dve",
                    out=sc[:, 11:12], in0=pr, in1=sc[:, 10:11], op=ALU.mult)
        t8 = p.emit("dve", "tensor_tensor", waits=[t6], inc="s_dve",
                    out=sc[:, 12:13], in0=pi, in1=sc[:, 10:11], op=ALU.mult)
        pre = carr[:, 8 + 2 * j:9 + 2 * j]
        pim = carr[:, 9 + 2 * j:10 + 2 * j]
        j1 = p.emit("dve", "tensor_tensor", waits=[t7, t_rdc], inc="s_dve",
                    out=sc[:, 13:14], in0=pre, in1=sc[:, 11:12], op=ALU.mult)
        j2 = p.emit("dve", "scalar_tensor_tensor", waits=[j1, t8], inc="s_dve",
                    out=sc[:, 13:14], in0=pim, scalar=sc[:, 12:13], in1=sc[:, 13:14],
                    op0=ALU.mult, op1=ALU.subtract)
        j2 = p.emit("dve", "tensor_scalar", waits=[j2], inc="s_dve",
                    out=sc[:, 13:14], in0=sc[:, 13:14], scalar1=-1.0, scalar2=None,
                    op0=ALU.mult)
        j3 = p.emit("dve", "tensor_tensor", waits=[t8], inc="s_dve",
                    out=sc[:, 14:15], in0=pre, in1=sc[:, 12:13], op=ALU.mult)
        j4 = p.emit("dve", "scalar_tensor_tensor", waits=[j3, t7], inc="s_dve",
                    out=sc[:, 14:15], in0=pim, scalar=sc[:, 11:12], in1=sc[:, 14:15],
                    op0=ALU.mult, op1=ALU.add)
        j4 = p.emit("dve", "tensor_scalar", waits=[j4], inc="s_dve",
                    out=sc[:, 14:15], in0=sc[:, 14:15], scalar1=-1.0, scalar2=None,
                    op0=ALU.mult)
        lr_ = mc1["lcr"][:, cidx:cidx + 1]
        li_ = mc1["lci"][:, cidx:cidx + 1]
        k1 = p.emit("dve", "tensor_tensor", waits=[t7], inc="s_dve",
                    out=sc[:, 15:16], in0=lr_, in1=sc[:, 11:12], op=ALU.mult)
        k2 = p.emit("dve", "scalar_tensor_tensor", waits=[k1, t8], inc="s_dve",
                    out=sc[:, 15:16], in0=li_, scalar=sc[:, 12:13], in1=sc[:, 15:16],
                    op0=ALU.mult, op1=ALU.subtract)
        k2 = p.emit("dve", "tensor_scalar", waits=[k2], inc="s_dve",
                    out=sc[:, 15:16], in0=sc[:, 15:16], scalar1=-1.0, scalar2=None,
                    op0=ALU.mult)
        k3 = p.emit("dve", "tensor_tensor", waits=[t8], inc="s_dve",
                    out=sc[:, 16:17], in0=lr_, in1=sc[:, 12:13], op=ALU.mult)
        k4 = p.emit("dve", "scalar_tensor_tensor", waits=[k3, t7], inc="s_dve",
                    out=sc[:, 16:17], in0=li_, scalar=sc[:, 11:12], in1=sc[:, 16:17],
                    op0=ALU.mult, op1=ALU.add)
        k4 = p.emit("dve", "tensor_scalar", waits=[k4], inc="s_dve",
                    out=sc[:, 16:17], in0=sc[:, 16:17], scalar1=-1.0, scalar2=None,
                    op0=ALU.mult)
        dr = p.emit("dve", "tensor_sub", waits=[j2, k2], inc="s_dve",
                    out=sc[:, 13:14], in0=sc[:, 13:14], in1=sc[:, 15:16])
        dr = p.emit("dve", "tensor_scalar", waits=[dr, mc1["flag_tok"]], inc="s_dve",
                    out=sc[:, 13:14], in0=sc[:, 13:14], scalar1=mc1["flag"][:, 0:1],
                    scalar2=None, op0=ALU.mult)
        di = p.emit("dve", "tensor_sub", waits=[j4, k4], inc="s_dve",
                    out=sc[:, 14:15], in0=sc[:, 14:15], in1=sc[:, 16:17])
        di = p.emit("dve", "tensor_scalar", waits=[di, mc1["flag_tok"]], inc="s_dve",
                    out=sc[:, 14:15], in0=sc[:, 14:15], scalar1=mc1["flag"][:, 0:1],
                    scalar2=None, op0=ALU.mult)
        Rj = mc1["Rall"][:, j * 1025:(j + 1) * 1025]
        f1 = p.emit("dve", "scalar_tensor_tensor", waits=[dr, scan_toks[j][0]],
                    inc="s_dve",
                    out=d_re[:, j * 1024:(j + 1) * 1024],
                    in0=Rj[:, 1:1025], scalar=sc[:, 13:14],
                    in1=d_re[:, j * 1024:(j + 1) * 1024],
                    op0=ALU.mult, op1=ALU.add)
        f2 = p.emit("dve", "scalar_tensor_tensor", waits=[di, scan_toks[j][1]],
                    inc="s_dve",
                    out=d_im[:, j * 1024:(j + 1) * 1024],
                    in0=Rj[:, 1:1025], scalar=sc[:, 14:15],
                    in1=d_im[:, j * 1024:(j + 1) * 1024],
                    op0=ALU.mult, op1=ALU.add)
        Cj = mc1["Call"][:, j * 1025:(j + 1) * 1025]
        Sj = mc1["Sall"][:, j * 1025:(j + 1) * 1025]
        g1 = p.emit("dve", "tensor_tensor", waits=[f1] + ([h_last] if h_last else []),
                    inc="s_dve",
                    out=u_re, in0=Cj[:, 0:1024],
                    in1=d_re[:, j * 1024:(j + 1) * 1024], op=ALU.mult)
        g2 = p.emit("dve", "tensor_tensor", waits=[f2], inc="s_dve",
                    out=u_im, in0=Sj[:, 0:1024],
                    in1=d_im[:, j * 1024:(j + 1) * 1024], op=ALU.mult)
        g3 = p.emit("dve", "tensor_add", waits=[g1, g2], inc="s_dve",
                    out=u_re, in0=u_re, in1=u_im)
        h_last = p.emit("dve", "scalar_tensor_tensor", waits=[g3, mc1["par_tok"]],
                        inc="s_dve",
                        out=h_s[:, j * 1024:(j + 1) * 1024], in0=u_re,
                        scalar=mc1["scl"][:, cidx:cidx + 1],
                        in1=h_s[:, j * 1024:(j + 1) * 1024],
                        op0=ALU.mult, op1=ALU.add)
    return h_last


def emit_l1attn(p, ctx, nc, *, x_ap, xout_ap, dins, dd1, ncores=8, dbg_ap=None):
    ar = ctx["ar"]
    m0 = ar.mark()
    ps = ctx["psum"]
    mc1 = dict(dd1)
    mc1["ncores"] = ncores
    h_s = ar.alloc([128, 4 * TOK], F32)
    y_s = ar.alloc([128, 4 * TOK], F32)
    sq_s = ar.alloc([128, 4 * TOK], F32)
    ybuf = ar.alloc([128, 4 * TOK], BF16)
    tmp1 = ar.alloc([128, 512], F32)
    u_re = ar.alloc([128, TOK], F32)
    u_im = ar.alloc([128, TOK], F32)
    ones_col = ar.alloc([128, 1], F32)
    ones1 = ar.alloc([1, 128], BF16)
    ones1f = ar.alloc([1, 128], F32)
    sc_scr = ar.alloc([128, 32], F32)
    carr = ar.alloc([128, 16], F32)
    stat = ar.alloc([1, 2560], F32)
    ldw = ar.alloc([128, 4096], F32)
    smalls = {}
    for k in ("bin_d", "l1w_d", "l1b_d", "l2w_d", "l2b_d", "scl_d", "phzr_d",
              "phzi_d", "lcr_d", "lci_d", "b1_d", "b2_d", "lnlw_d", "lnlb_d",
              "flag_d", "rksel_d"):
        shp = {"bin_d": 4, "b1_d": 64, "b2_d": 16, "lnlw_d": 4, "lnlb_d": 4,
               "flag_d": 1, "rksel_d": 8}.get(k, 16)
        smalls[k] = ar.alloc([128, shp], F32)
    allc = ar.alloc([128, 8 * ncores], F32)
    boutr = ar.alloc([1, 4096], BF16)
    pers_mark = ar.mark()

    par_toks = []
    for k in smalls:
        par_toks.append(p.dma("sp", smalls[k], dins[k], "misc", waits=ar.war))
    lb0 = p.dma("pool", boutr, dins["bout_d"], "pb", waits=ar.war)
    t_oc = p.emit("dve", "memset", ones_col, 1.0, waits=ar.war, inc="s_dve")
    t_o1 = p.emit("dve", "memset", ones1, 1.0, waits=ar.war, inc="s_dve")
    t_o1f = p.emit("dve", "memset", ones1f, 1.0, waits=ar.war, inc="s_dve")
    par_tok = par_toks[-1]
    mc1.update(dict(flag=smalls["flag_d"],
                    phzr=smalls["phzr_d"], phzi=smalls["phzi_d"],
                    lcr=smalls["lcr_d"], lci=smalls["lci_d"],
                    scl=smalls["scl_d"], rksel=smalls["rksel_d"],
                    sc_scr=sc_scr, tmp1=tmp1,
                    u_re=u_re, u_im=u_im,
                    carry=carr, allc=allc))
    par_all = p.stream_all("misc")
    t_pb = p.emit("dve", "engine_nop", waits=par_all, inc="s_dve")
    mc1["par_tok"] = t_pb
    mc1["flag_tok"] = t_pb
    ctx["l1_stat"] = stat
    ctx["l1_ones1"] = ones1f

    xnm = pers_mark
    xnT = ar.alloc([128, NKD * TOK], BF16)
    k_ready, ln_last = emit_ln_transpose(p, ctx, x_ap=x_ap, xnT=xnT,
                                         x_ready=(), n_tt=8, nkd=NKD, light=True)
    winT = ar.alloc([128, NKD * 512], BF16)
    wcvt = ar.alloc([128, 2048], BF16)
    wcvt_b = sq_s.bitcast(BF16)[:, :2048]
    wt0 = _l1_build_T(p, ctx, src_ap=dins["win_d"][:, 0:2048],
                      dst=winT[:, 0:16 * 512], nrow_t=4,
                      ncol_t=16, ld_buf=ldw[:, :2048], bf_buf=wcvt, war=ar.war,
                      ld_buf2=ldw[:, 2048:4096], bf_buf2=wcvt_b)
    wt = _l1_build_T(p, ctx, src_ap=dins["win_d"][:, 2048:4096],
                     dst=winT[:, 16 * 512:32 * 512], nrow_t=4,
                     ncol_t=16, ld_buf=ldw[:, :2048], bf_buf=wcvt,
                     war=ar.war + [wt0],
                     ld_buf2=ldw[:, 2048:4096], bf_buf2=wcvt_b)
    ev_h = None
    for ob in range(4):
        for th in range(2):
            bank = ob % 2
            pw = psum_wait(ctx, bank)
            mm = None
            for k in range(NKD):
                mm = p.emit("pe", "matmul", ps[bank][:, :512],
                            winT[:, (k * 4 + ob) * 128:(k * 4 + ob + 1) * 128],
                            xnT[:, k * TOK + th * 512: k * TOK + (th + 1) * 512],
                            waits=([wt, ln_last] + ([pw] if pw else []))
                            if k == 0 else (),
                            inc=("s_pe" if k == NKD - 1 else None),
                            start=(k == 0), stop=(k == NKD - 1))
            ev_h = p.emit("act", "activation",
                          h_s[:, ob * TOK + th * 512: ob * TOK + (th + 1) * 512],
                          ps[bank][:, :512], AF.Identity,
                          scale=1.0, bias=smalls["bin_d"][:, ob:ob + 1],
                          waits=[mm] + par_all, inc="s_act")
            psum_claim(ctx, bank, ev_h)
    ar.reset(xnm)
    h_tok = ev_h
    if dbg_ap is not None:
        dst_ = p.dma("pool", dbg_ap[0], h_s, "pb", waits=[h_tok])
        h_tok = p.emit("dve", "engine_nop", waits=[h_tok, dst_], inc="s_dve")

    for lidx in range(4):
        l1w = [smalls["l1w_d"][:, lidx * 4 + j:lidx * 4 + j + 1] for j in range(4)]
        l1b = [smalls["l1b_d"][:, lidx * 4 + j:lidx * 4 + j + 1] for j in range(4)]
        t_y = _l1_ln(p, ctx, h_s=h_s, out_s=y_s, w_ap=l1w, b_ap=l1b, silu=True,
                     sq_s=sq_s, ones_col=ones_col, war=[h_tok, t_oc] + par_all)
        if dbg_ap is not None and lidx == 0 and dbg_ap.shape[0] > 5:
            dst_ = p.dma("pool", dbg_ap[5], y_s, "pb", waits=[t_y])
            t_y = p.emit("dve", "engine_nop", waits=[t_y, dst_], inc="s_dve")
        mS = ar.mark()
        mc1["Call"] = ar.alloc([128, 4 * 1025], F32)
        mc1["Sall"] = ar.alloc([128, 4 * 1025], F32)
        mc1["Rall"] = ar.alloc([128, 4 * 1025], F32)
        mc1["d_re"] = ar.alloc([128, 4 * TOK], F32)
        mc1["d_im"] = ar.alloc([128, 4 * TOK], F32)
        mc1["rrb"] = ar.alloc([128, TOK], F32)
        mc1["ones_t"] = ar.alloc([128, TOK], F32)
        t_on = p.emit("dve", "memset", mc1["ones_t"], 1.0,
                      waits=ar.war + [t_y], inc="s_dve")
        mc1["ones_tok"] = t_on
        h_tok = _l1_sconv(p, ctx, mc1, y_s=y_s, h_s=h_s, lidx=lidx, y_tok=t_y)
        if dbg_ap is not None and lidx == 0 and dbg_ap.shape[0] > 5:
            d1_ = p.dma("pool", dbg_ap[6], mc1["d_re"], "pb", waits=[h_tok])
            d2_ = p.dma("pool", dbg_ap[7], mc1["Call"][:, :4096], "pb", waits=[h_tok, d1_])
            d3_ = p.dma("pool", dbg_ap[8], mc1["Rall"][:, :4096], "pb", waits=[h_tok, d2_])
            d4_ = p.dma("pool", dbg_ap[9], h_s, "pb", waits=[h_tok, d3_])
            h_tok = p.emit("dve", "engine_nop", waits=[h_tok, d4_], inc="s_dve")
        phase_war(p, ar)
        ar.reset(mS)
        y1_s = ar.alloc([128, 16 * TOK], BF16)
        w1T = ar.alloc([128, 4 * 2048], BF16)
        w2T = ar.alloc([128, 16 * 512], BF16)
        wcvt2 = ar.alloc([128, 4096], BF16)
        w1w = list(ar.war)
        wt1 = _l1_build_T(p, ctx, src_ap=dins["w1_d"][lidx], dst=w1T, nrow_t=16,
                          ncol_t=4, ld_buf=ldw[:, :512], bf_buf=wcvt2, war=w1w,
                          ld_buf2=ldw[:, 512:1024],
                          bf_buf2=wcvt2[:, 1024:2048])
        wt2 = _l1_build_T(p, ctx, src_ap=dins["w2_d"][lidx], dst=w2T, nrow_t=4,
                          ncol_t=16, ld_buf=ldw[:, :2048], bf_buf=wcvt2,
                          war=w1w + [wt1],
                          ld_buf2=ldw[:, 2048:4096],
                          bf_buf2=wcvt2[:, 2048:4096])
        l2w = [smalls["l2w_d"][:, lidx * 4 + j:lidx * 4 + j + 1] for j in range(4)]
        l2b = [smalls["l2b_d"][:, lidx * 4 + j:lidx * 4 + j + 1] for j in range(4)]
        t_y2 = _l1_ln(p, ctx, h_s=h_s, out_s=y_s, w_ap=l2w, b_ap=l2b, silu=False,
                      sq_s=sq_s, ones_col=ones_col, war=[h_tok])
        yb = ybuf
        t_yb = p.emit("dve", "tensor_copy", waits=[t_y2], inc="s_dve",
                      out=yb, in_=y_s)
        mm1l = None
        for Mb in range(16):
            for th in range(2):
                bank = Mb % 2
                pw = psum_wait(ctx, bank)
                mm = None
                for k in range(4):
                    mm = p.emit("pe", "matmul", ps[bank][:, :512],
                                w1T[:, (k * 16 + Mb) * 128:(k * 16 + Mb + 1) * 128],
                                yb[:, k * TOK + th * 512: k * TOK + (th + 1) * 512],
                                waits=([wt1, t_yb] + ([pw] if pw else []))
                                if k == 0 else (),
                                inc=("s_pe" if k == 3 else None),
                                start=(k == 0), stop=(k == 3))
                mm1l = p.emit("act", "activation",
                              y1_s[:, Mb * TOK + th * 512: Mb * TOK + (th + 1) * 512],
                              ps[bank][:, :512], AF.Silu,
                              bias=smalls["b1_d"][:, lidx * 16 + Mb:lidx * 16 + Mb + 1],
                              waits=[mm], inc="s_act")
                psum_claim(ctx, bank, mm1l)
        for ob in range(4):
            for th in range(2):
                bank = 2 + (ob % 2)
                pw = psum_wait(ctx, bank)
                mm = None
                for k in range(16):
                    mm = p.emit("pe", "matmul", ps[bank][:, :512],
                                w2T[:, (k * 4 + ob) * 128:(k * 4 + ob + 1) * 128],
                                y1_s[:, k * TOK + th * 512: k * TOK + (th + 1) * 512],
                                waits=([wt2, mm1l] + ([pw] if pw else []))
                                if k == 0 else (),
                                inc=("s_pe" if k == 15 else None),
                                start=(k == 0), stop=(k == 15))
                e1 = p.emit("act", "activation", u_re[:, th * 512:(th + 1) * 512],
                            ps[bank][:, :512], AF.Identity,
                            bias=smalls["b2_d"][:, lidx * 4 + ob:lidx * 4 + ob + 1],
                            waits=[mm, h_tok], inc="s_act")
                psum_claim(ctx, bank, e1)
                h_tok = p.emit("dve", "tensor_add", waits=[e1, h_tok], inc="s_dve",
                               out=h_s[:, ob * TOK + th * 512: ob * TOK + (th + 1) * 512],
                               in0=h_s[:, ob * TOK + th * 512: ob * TOK + (th + 1) * 512],
                               in1=u_re[:, th * 512:(th + 1) * 512])
        if dbg_ap is not None:
            dst_ = p.dma("pool", dbg_ap[1 + lidx], h_s, "pb", waits=[h_tok])
            h_tok = p.emit("dve", "engine_nop", waits=[h_tok, dst_], inc="s_dve")
        phase_war(p, ar)
        ar.reset(pers_mark)
    lnw = [smalls["lnlw_d"][:, j:j + 1] for j in range(4)]
    lnb = [smalls["lnlb_d"][:, j:j + 1] for j in range(4)]
    t_y = _l1_ln(p, ctx, h_s=h_s, out_s=y_s, w_ap=lnw, b_ap=lnb, silu=False,
                 sq_s=sq_s, ones_col=ones_col, war=[h_tok])
    yb = ybuf
    t_yb = p.emit("dve", "tensor_copy", waits=[t_y], inc="s_dve", out=yb, in_=y_s)
    woutT = ar.alloc([128, 4 * 4096], BF16)
    wcvt3 = ar.alloc([128, 1024], BF16)
    wto = _l1_build_T(p, ctx, src_ap=dins["wout_d"], dst=woutT, nrow_t=32,
                      ncol_t=4, ld_buf=ldw[:, :512], bf_buf=wcvt3[:, :512],
                      war=[("s_pe", p.counts["s_pe"]), ("s_dve", p.counts["s_dve"])],
                      ld_buf2=ldw[:, 512:1024], bf_buf2=wcvt3[:, 512:1024])
    xb = [ar.alloc([128, 512], F32) for _ in range(2)]
    ob_ = [ar.alloc([128, 512], F32) for _ in range(2)]
    info = {}
    for t in range(8):
        for oc in range(8):
            it = t * 8 + oc
            pr = it % 2
            xw = [info[it - 2]["e"]] if it - 2 in info else list(ar.war)
            ld_x = p.dma("sp", xb[pr],
                         x_ap[t * 128:(t + 1) * 128, oc * 512:(oc + 1) * 512],
                         "x", waits=xw)
            bank = 4 + pr
            pw = psum_wait(ctx, bank)
            mm = None
            for k in range(4):
                mm = p.emit("pe", "matmul", ps[bank][:, :512],
                            yb[:, k * TOK + t * 128: k * TOK + (t + 1) * 128],
                            woutT[:, k * 4096 + oc * 512: k * 4096 + (oc + 1) * 512],
                            waits=([wto, t_yb] + ([pw] if pw else []))
                            if k == 0 else (),
                            inc=None, start=(k == 0), stop=False)
            mm = p.emit("pe", "matmul", ps[bank][:, :512], ones1,
                        boutr[0:1, oc * 512:(oc + 1) * 512],
                        waits=[lb0, t_o1], inc="s_pe", start=False, stop=True)
            stw = [info[it - 2]["st"]] if it - 2 in info else []
            e = p.emit("dve", "tensor_add", waits=[mm, ld_x] + stw, inc="s_dve",
                       out=ob_[pr], in0=ps[bank][:, :512], in1=xb[pr])
            psum_claim(ctx, bank, e)
            st = p.dma("pool", xout_ap[t * 128:(t + 1) * 128, oc * 512:(oc + 1) * 512],
                       ob_[pr], "xst", waits=[e])
            info[it] = {"e": e, "st": st}
    phase_war(p, ar, stores=("xst",))
    ar.reset(m0)
    return p.stream_all("xst")



def emit_tcache_lhsT(p, ctx, *, src_ap, dst_ap, n_mb, nkd=NKD, war=(),
                     ld_eng="sp", ld_stream="misc", bufs=None):
    ar = ctx["ar"]
    ps = ctx["psum"]
    m0 = ar.mark()
    if bufs is None:
        ldb = [ar.alloc([128, nkd * 128], F32) for _ in range(2)]
        ebuf = [ar.alloc([128, 128], BF16) for _ in range(4)]
    else:
        ldb, ebuf = bufs
    tr_last = {}
    st_of = {}
    idx = 0
    for mb in range(n_mb):
        par = mb % 2
        ld = p.dma(ld_eng, ldb[par], src_ap[mb * 128:(mb + 1) * 128, :],
                   ld_stream,
                   waits=list(war) + ([tr_last[mb - 2]]
                                      if mb - 2 in tr_last else []))
        for k in range(nkd):
            bank = 6 + (k % 2)
            pw = psum_wait(ctx, bank)
            tr = p.emit("pe", "transpose", ps[bank][:, :128],
                        ldb[par][:, k * 128:(k + 1) * 128], ctx["ident_f"],
                        waits=[ld] + ([pw] if pw else []), inc="s_pe")
            if k == nkd - 1:
                tr_last[mb] = tr
            eb = ebuf[idx % 4]
            evs = p.emit("dve", "tensor_copy",
                         waits=[tr] + ([st_of[idx - 4]]
                                       if idx - 4 in st_of else []),
                         inc="s_dve", out=eb, in_=ps[bank][:, :128])
            psum_claim(ctx, bank, evs)
            st_of[idx] = p.dma("pool",
                               dst_ap[mb][:, k * 128:(k + 1) * 128],
                               eb, "cst", waits=[evs])
            idx += 1
    phase_war(p, ar, stores=("cst",))
    ar.reset(m0)
    return p.stream_all("cst")


def emit_tcache_rhs(p, ctx, *, src_ap, dst_ap, n_kc, n_ob, n_oc, gnw_ap=None,
                    war=(), ld_eng="sp", ld_stream="misc", bufs=None):
    ar = ctx["ar"]
    ps = ctx["psum"]
    m0 = ar.mark()
    W = n_kc * 128
    if bufs is None:
        ldb = [ar.alloc([128, W], F32) for _ in range(2)]
        ebuf = [ar.alloc([128, 128], BF16) for _ in range(4)]
    else:
        ldb, ebuf = bufs
    trl = {}
    st_of = {}
    idx = 0
    for ob in range(n_ob):
        par = ob % 2
        ld = p.dma(ld_eng, ldb[par], src_ap[ob * 128:(ob + 1) * 128, :],
                   ld_stream,
                   waits=list(war) + ([trl[ob - 2]] if ob - 2 in trl else []))
        for kc in range(n_kc):
            bank = 6 + (kc % 2)
            pw = psum_wait(ctx, bank)
            tr = p.emit("pe", "transpose", ps[bank][:, :128],
                        ldb[par][:, kc * 128:(kc + 1) * 128], ctx["ident_f"],
                        waits=[ld] + ([pw] if pw else []), inc="s_pe")
            if kc == n_kc - 1:
                trl[ob] = tr
            eb = ebuf[idx % 4]
            evw = [tr] + ([st_of[idx - 4]] if idx - 4 in st_of else [])
            if gnw_ap is not None:
                ev = p.emit("dve", "tensor_scalar", waits=evw + [gnw_ap[1]],
                            inc="s_dve", out=eb,
                            in0=ps[bank][:, :128],
                            scalar1=gnw_ap[0][kc][:, 0:1], scalar2=None,
                            op0=ALU.mult)
            else:
                ev = p.emit("dve", "tensor_copy", waits=evw, inc="s_dve",
                            out=eb, in_=ps[bank][:, :128])
            psum_claim(ctx, bank, ev)
            st_of[idx] = p.dma("pool",
                               dst_ap[ob // 4, kc][:, (ob % 4) * 128:
                                                   (ob % 4 + 1) * 128],
                               eb, "cst", waits=[ev])
            idx += 1
    phase_war(p, ar, stores=("cst",))
    ar.reset(m0)
    return p.stream_all("cst")


def emit_allgather(p, *, in_ap, out_ap, ncores, waits=()):
    ag = p.raw("pool", lambda g: g.collective_compute(
        "AllGather", ALU.bypass, replica_groups=[list(range(ncores))],
        ins=[in_ap.opt()], outs=[out_ap.opt()]),
        waits=list(waits), inc="s_cc")
    return ag


NC = 8


def _setup(nc, p):
    ident_d = nc.dram_tensor("ident_d", [128, 128], BF16, kind="ExternalInput").ap()
    identf_d = nc.dram_tensor("identf_d", [128, 128], F32, kind="ExternalInput").ap()
    ident_s = nc.alloc_sbuf_tensor("ident_s", [128, 128], BF16)
    identf_s = nc.alloc_sbuf_tensor("identf_s", [128, 128], F32)
    ctx = make_ctx(nc, p)
    ctx["ident_bf"] = ident_s.ap()
    ctx["ident_f"] = identf_s.ap()
    i1 = p.dma("sp", ident_s.ap(), ident_d, "misc")
    i2 = p.dma("sp", identf_s.ap(), identf_d, "misc")
    ctx["ar"].war = [i1, i2]
    return ctx


def build_launch_ffn(ncores=NC):
    nc = bass.Bass("TRN2", target_bir_lowering=False, debug=False)
    p = Prog(nc)
    ctx = _setup(nc, p)
    ar = ctx["ar"]
    E = lambda n, s, d: nc.dram_tensor(n, s, d, kind="ExternalInput").ap()
    x_in = E("x", [TOK, D], F32)
    qw_gu = E("qw_gu", [512, 2816], U32)
    qz_gu = E("qz_gu", [32, 352], U32)
    sc_gu = E("sc_gu", [32, 2816], F32)
    gidx_gu = E("gidx_gu", [4096], I32)
    qw_d = E("qw_d", [176, 4096], U32)
    qz_d = E("qz_d", [86, 512], U32)
    sc_d = E("sc_d", [86, 4096], F32)
    gidx_d = E("gidx_d", [1408], I32)
    mask_in = E("mask", [128, 1], U32)
    pow2_in = E("pow2", [128, 1], F32)
    iota32 = E("iota32", [32, 1], F32)
    iota86 = E("iota86", [86, 1], F32)
    x_out = nc.dram_tensor("x_out", [TOK, D], F32, kind="ExternalOutput").ap()

    wgu_slice = nc.dram_tensor("wgu_sl", [11, 128, 8192], BF16).ap()
    wgu_all = nc.dram_tensor("wgu_all", [ncores * 11 * 128, 8192], BF16).ap()
    wd_slice = nc.dram_tensor("wd_sl", [8, 11, 128, 512], BF16).ap()
    wd_all = nc.dram_tensor("wd_all", [ncores * 8 * 11 * 128, 512], BF16).ap()
    h_spill = nc.dram_tensor("h_spill", [86, 128, TOK], BF16).ap()

    m0 = ar.mark()
    mask_s = ar.alloc([128, 1], U32)
    pow2_s = ar.alloc([128, 1], F32)
    iota32_s = ar.alloc([32, 1], F32)
    iota86_s = ar.alloc([86, 1], F32)
    oneh = ar.alloc([32, NKD * 128], BF16)
    oneh_d = ar.alloc([86, 11 * 128], BF16)
    rep32 = ar.alloc([32, 128], I32)
    rep86 = ar.alloc([86, 128], I32)
    t1 = p.dma("sp", mask_s, mask_in, "misc")
    t1b = p.dma("sp", pow2_s, pow2_in, "misc")
    t2 = p.dma("sp", iota32_s, iota32, "misc")
    t3 = p.dma("sp", iota86_s, iota86, "misc")
    war0 = list(ar.war)
    ar.war = war0 + [t1, t1b, t2, t3]
    ctx["oneh_tok"] = emit_onehot_tables(p, ctx, gidx_ap=gidx_gu, iota_ap=iota32_s,
                                         oneh=oneh, n_groups=32, nkd=NKD, rep=rep32)
    ctx["oneh_d_tok"] = emit_onehot_tables(p, ctx, gidx_ap=gidx_d, iota_ap=iota86_s,
                                           oneh=oneh_d, n_groups=86, nkd=11, rep=rep86)
    ar.war = ar.war + [ctx["oneh_tok"], ctx["oneh_d_tok"]]
    emit_dequant_gu(p, ctx, qw_ap=qw_gu, qz_ap=qz_gu, sc_ap=sc_gu, oneh=oneh,
                    mask_ap=mask_s, pow2_ap=pow2_s, out_slice_ap=wgu_slice)
    gu_sts = p.stream_all("cst")
    ag1 = emit_allgather(p, in_ap=wgu_slice.rearrange("a p m -> (a p) m"),
                         out_ap=wgu_all, ncores=ncores, waits=gu_sts)
    emit_dequant_wd(p, ctx, qw_ap=qw_d, sc_full_ap=sc_d, qz_full_ap=qz_d,
                    oneh_d=oneh_d, mask_ap=mask_s, pow2_ap=pow2_s,
                    out_slice_ap=wd_slice)
    wd_sts = p.stream_all("cst")
    ag2 = emit_allgather(p, in_ap=wd_slice.rearrange("a b p m -> (a b p) m"),
                         out_ap=wd_all, ncores=ncores, waits=wd_sts)
    ar.reset(m0)
    ar.war = war0 + [p.tok("s_pe"), p.tok("s_act"), p.tok("s_dve"), ag1, ag2]

    xnT = ar.alloc([128, NKD * TOK], BF16)
    k_ready, ln_last = emit_ln_transpose(p, ctx, x_ap=x_in, xnT=xnT, x_ready=())
    wgu_v = wgu_all.rearrange("(a p) m -> a p m", p=128)
    h_toks = emit_ffn_mm1(p, ctx, wcache_ap=wgu_v, xnT=xnT, h_out_ap=h_spill,
                          x_ready_toks=[ln_last, ag1], n_pairs=86)
    wd_v = wd_all.rearrange("(r o k p) m -> r o k p m", o=8, k=11, p=128)
    st_toks = emit_ffn_mm2(p, ctx,
                           wd_cache_ap=lambda oc, k: wd_v[k // 11, oc, k % 11],
                           h_ap=h_spill, x_ap=x_in, xout_ap=x_out,
                           h_ready_toks=h_toks + [ag2], nk=86, w_ready=[ag2])
    p.barrier("pool", st_toks)
    p.replay(alloc_sems(nc, prog_sems(p)))
    return nc


def build_launch_mamba(ncores=NC):
    nc = bass.Bass("TRN2", target_bir_lowering=False, debug=False)
    p = Prog(nc)
    ctx = _setup(nc, p)
    ar = ctx["ar"]
    E = lambda n, s, d: nc.dram_tensor(n, s, d, kind="ExternalInput").ap()
    x_in = E("x", [TOK, D], F32)
    ip_rows = E("ip_rows", [19 * 128, 4096], F32)
    op_cols = E("op_cols", [8192, 512], F32)
    gnw_sl = E("gnw_sl", [128, 8], F32)
    dins = {k: E(k, s, dt) for k, s, dt in (
        ("norm_w_g", [128, NKD], F32), ("convw_d", [128, 320], F32),
        ("convb_d", [128, 80], F32), ("dt_bias_d", [128, 1], F32),
        ("A_log_d", [128, 1], F32), ("dp_row_d", [1, 8192], F32),
        ("gnw_d", [128, 64], F32), ("flag_d", [128, 1], F32),
        ("utri_d", [128, 128], BF16))}
    x_out = nc.dram_tensor("x_out", [TOK, D], F32, kind="ExternalOutput").ap()

    ipc_slice = nc.dram_tensor("ipc_sl", [19, 128, 4096], BF16).ap()
    ipc_all = nc.dram_tensor("ipc_all", [ncores * 19 * 128, 4096], BF16).ap()
    opc_slice = nc.dram_tensor("opc_sl", [8, 8, 128, 512], BF16).ap()
    opc_all = nc.dram_tensor("opc_all", [ncores * 8 * 8 * 128, 512], BF16).ap()

    m0 = ar.mark()
    gnw_s = ar.alloc([128, 8], F32)
    g1 = p.dma("sp", gnw_s, gnw_sl, "misc", waits=ar.war)
    war0 = list(ar.war)
    cldb = [ar.alloc([128, 4096], F32) for _ in range(2)]
    ccvb = [ar.alloc([128, 4096], BF16) for _ in range(2)]
    chunks = cache_chunks(ip_rows=ip_rows, ipc_slice=ipc_slice,
                          op_cols=op_cols, opc_slice=opc_slice)
    hook = make_cache_hook(p, chunks=chunks, cldb=cldb, ccvb=ccvb,
                           gnw_s=gnw_s, g1=g1, war=ar.war)
    for n in range(len(chunks) + 2):
        hook(n)
    sts = p.stream_all("cs")
    ag1 = emit_allgather(p, in_ap=ipc_slice.rearrange("a p m -> (a p) m"),
                         out_ap=ipc_all, ncores=ncores, waits=sts)
    ag2 = emit_allgather(p, in_ap=opc_slice.rearrange("a b p m -> (a b p) m"),
                         out_ap=opc_all, ncores=ncores, waits=sts)
    ar.war = ar.war + sts
    ar.reset(m0)
    ar.war = war0 + [p.tok("s_pe"), p.tok("s_act"), p.tok("s_dve"), ag1, ag2]

    dd = mamba_dram(nc, ncores=ncores)
    ipc_v = ipc_all.rearrange("(a p) m -> a p m", p=128)
    opc_v = opc_all.rearrange("(r o k p) m -> r o k p m", o=8, k=8, p=128)
    st = emit_mamba_sublayer(p, ctx, nc, x_ap=x_in, xout_ap=x_out,
                             inproj_cache_ap=ipc_v,
                             outproj_cache_ap=lambda oc, k: opc_v[k // 8, oc, k % 8],
                             dins=dins, dd=dd, ncores=ncores, op_w_ready=[ag2])
    p.barrier("pool", st)
    p.replay(alloc_sems(nc, prog_sems(p)))
    return nc


def build_launch_l1ffn(ncores=NC):
    nc = bass.Bass("TRN2", target_bir_lowering=False, debug=False)
    p = Prog(nc)
    ctx = _setup(nc, p)
    ar = ctx["ar"]
    E = lambda n, s, d: nc.dram_tensor(n, s, d, kind="ExternalInput").ap()
    x_in = E("x", [TOK, D], F32)
    w0_cols = E("w0_cols", [4096, 512], F32)
    a_d = E("a_d", [64, 4096], F32)
    b_d = E("b_d", [4, 4096, 16], F32)
    b0_d = E("b0_d", [1, 4096], F32)
    mask_d = E("mask_d", [4, TOK, 4096], BF16)
    x_out = nc.dram_tensor("x_out", [TOK, D], F32, kind="ExternalOutput").ap()
    w0_slice = nc.dram_tensor("w0_sl", [8, 4, 128, 512], BF16).ap()
    w0_all = nc.dram_tensor("w0_all", [ncores * 8 * 4 * 128, 512], BF16).ap()
    m0 = ar.mark()
    war0 = list(ar.war)
    cldb = [ar.alloc([128, 4096], F32) for _ in range(2)]
    ccvb = [ar.alloc([128, 4096], BF16) for _ in range(2)]
    chunks = cache_chunks(w0_cols=w0_cols, w0_slice=w0_slice)
    hook = make_cache_hook(p, chunks=chunks, cldb=cldb, ccvb=ccvb,
                           war=ar.war)
    for n in range(len(chunks) + 2):
        hook(n)
    sts = p.stream_all("cs")
    ag = emit_allgather(p, in_ap=w0_slice.rearrange("a b p m -> (a b p) m"),
                        out_ap=w0_all, ncores=ncores, waits=sts)
    ar.reset(m0)
    ar.war = war0 + [p.tok("s_pe"), p.tok("s_act"), p.tok("s_dve"), ag] + sts
    w0_v = w0_all.rearrange("(r o k p) m -> r o k p m", o=8, k=4, p=128)
    st = emit_l1ffn(p, ctx, x_ap=x_in, xout_ap=x_out,
                    w0_cache_ap=lambda oc: [
                        w0_v[r, oc].rearrange("k p m -> p k m")
                        for r in range(ncores)],
                    a_d=a_d, b_d=b_d, b0_d=b0_d, mask_d=mask_d)
    p.barrier("pool", st)
    p.replay(alloc_sems(nc, prog_sems(p)))
    return nc


def build_launch_l1attn(ncores=NC):
    nc = bass.Bass("TRN2", target_bir_lowering=False, debug=False)
    p = Prog(nc)
    ctx = _setup(nc, p)
    E = lambda n, s, d: nc.dram_tensor(n, s, d, kind="ExternalInput").ap()
    x_in = E("x", [TOK, D], F32)
    specs = dict(win_d=([512, 4096], F32), bin_d=([128, 4], F32),
                 l1w_d=([128, 16], F32), l1b_d=([128, 16], F32),
                 l2w_d=([128, 16], F32), l2b_d=([128, 16], F32),
                 scl_d=([128, 16], F32), phzr_d=([128, 16], F32),
                 phzi_d=([128, 16], F32), lcr_d=([128, 16], F32),
                 lci_d=([128, 16], F32), w1_d=([4, 2048, 512], F32),
                 b1_d=([128, 64], F32), w2_d=([4, 512, 2048], F32),
                 b2_d=([128, 16], F32), lnlw_d=([128, 4], F32),
                 lnlb_d=([128, 4], F32), wout_d=([4096, 512], F32), rksel_d=([128, 8], F32),
                 bout_d=([1, 4096], F32), flag_d=([128, 1], F32))
    dins = {k: E(k, s, dt) for k, (s, dt) in specs.items()}
    x_out = nc.dram_tensor("x_out", [TOK, D], F32, kind="ExternalOutput").ap()
    dd1 = dict(carr_b_d=nc.dram_tensor("carr_b", [128, 8], F32).ap(),
               carr_a_d=nc.dram_tensor("carr_a", [ncores * 128, 8], F32).ap())
    st = emit_l1attn(p, ctx, nc, x_ap=x_in, xout_ap=x_out, dins=dins, dd1=dd1,
                     ncores=ncores)
    p.barrier("pool", st)
    p.replay(alloc_sems(nc, prog_sems(p)))
    return nc


def make_cache_hook(p, *, chunks, cldb, ccvb, gnw_s=None, g1=None, war=()):
    cw = {"ld": {}, "cv": {}, "st": {}}

    def hook(n):
        m = n - 2
        if 0 <= m < len(chunks):
            kind, csrc, cdst, ckc = chunks[m]
            wdt = 4096 if kind == "ip" else 1024
            par = m % 2
            cvw = [cw["ld"][m]] + ([cw["st"][m - 2]] if m - 2 in cw["st"]
                                   else [])
            if kind == "op":
                cv = None
                for h in range(2):
                    cv = p.emit("dve", "tensor_scalar",
                                waits=cvw + [g1], inc="s_dve",
                                out=ccvb[par][:, h * 512:(h + 1) * 512],
                                in0=cldb[par][:, h * 512:(h + 1) * 512],
                                scalar1=gnw_s[:, ckc + h:ckc + h + 1],
                                scalar2=None, op0=ALU.mult)
            else:
                cv = None
                for h in range(0, wdt, 1024):
                    cv = p.emit("dve", "tensor_copy",
                                waits=(cvw if h == 0 else [cv]), inc="s_dve",
                                out=ccvb[par][:, h:h + 1024],
                                in_=cldb[par][:, h:h + 1024])
            cw["cv"][m] = cv
            if kind == "ip":
                cw["st"][m] = p.dma("act", cdst, ccvb[par][:, :wdt], "cs",
                                    waits=[cv])
            else:
                cw["st"][m] = p.dma("act",
                                    cdst.rearrange("k c m -> c k m"),
                                    ccvb[par][:, :1024], "cs", waits=[cv])
        if n < len(chunks):
            kind, csrc, cdst, ckc = chunks[n]
            wdt = 4096 if kind == "ip" else 1024
            par = n % 2
            cw["ld"][n] = p.dma("act", cldb[par][:, :wdt], csrc, "cl",
                                waits=list(war) +
                                ([cw["cv"][n - 2]] if n - 2 in cw["cv"]
                                 else []))
    return hook


def cache_chunks(*, ip_rows=None, ipc_slice=None, op_cols=None,
                 opc_slice=None, w0_cols=None, w0_slice=None):
    chunks = []
    if ip_rows is not None:
        for mb in range(19):
            chunks.append(("ip", ip_rows[mb * 128:(mb + 1) * 128, :],
                           ipc_slice[mb], None))
    if op_cols is not None:
        for oc in range(8):
            for kc in range(0, 8, 2):
                chunks.append(("op",
                               op_cols[(oc * 8 + kc) * 128:
                                       (oc * 8 + kc + 2) * 128, :]
                               .rearrange("(k c) m -> c k m", k=2),
                               opc_slice[oc, kc:kc + 2], kc))
    if w0_cols is not None:
        for oc in range(8):
            for kc in range(0, 4, 2):
                chunks.append(("w0",
                               w0_cols[(oc * 4 + kc) * 128:
                                       (oc * 4 + kc + 2) * 128, :]
                               .rearrange("(k c) m -> c k m", k=2),
                               w0_slice[oc, kc:kc + 2], None))
    return chunks


def build_fused(ncores=NC):
    nc = bass.Bass("TRN2", target_bir_lowering=False, debug=False)
    p = Prog(nc)
    ctx = _setup(nc, p)
    ar = ctx["ar"]
    E = lambda n, s, d: nc.dram_tensor(n, s, d, kind="ExternalInput").ap()
    x_in = E("x", [TOK, D], F32)
    qw_gu = E("qw_gu", [512, 2816], U32)
    qz_gu = E("qz_gu", [32, 352], U32)
    sc_gu = E("sc_gu", [32, 2816], F32)
    gidx_gu = E("gidx_gu", [4096], I32)
    qw_d = E("qw_d", [176, 4096], U32)
    qz_d = E("qz_d", [86, 512], U32)
    sc_d = E("sc_d", [86, 4096], F32)
    gidx_d = E("gidx_d", [1408], I32)
    mask_in = E("mask", [128, 1], U32)
    pow2_in = E("pow2", [128, 1], F32)
    iota32 = E("iota32", [32, 1], F32)
    iota86 = E("iota86", [86, 1], F32)
    ip_rows = E("ip_rows", [19 * 128, 4096], F32)
    op_cols = E("op_cols", [8192, 512], F32)
    gnw_sl = E("gnw_sl", [128, 8], F32)
    mdins = {k: E(k, s, dt) for k, s, dt in (
        ("norm_w_g", [128, NKD], F32), ("convw_d", [128, 320], F32),
        ("convb_d", [128, 80], F32), ("dt_bias_d", [128, 1], F32),
        ("A_log_d", [128, 1], F32), ("dp_row_d", [1, 8192], F32),
        ("gnw_d", [128, 64], F32), ("flag_d", [128, 1], F32),
        ("utri_d", [128, 128], BF16))}
    w0_cols = E("w0_cols", [4096, 512], F32)
    a_d = E("a_d", [64, 4096], F32)
    b_d = E("b_d", [4, 4096, 16], F32)
    b0_d = E("b0_d", [1, 4096], F32)
    mask_d = E("mask_d", [4, TOK, 4096], BF16)
    aspecs = dict(win_d=([512, 4096], F32), bin_d=([128, 4], F32),
                  l1w_d=([128, 16], F32), l1b_d=([128, 16], F32),
                  l2w_d=([128, 16], F32), l2b_d=([128, 16], F32),
                  scl_d=([128, 16], F32), phzr_d=([128, 16], F32),
                  phzi_d=([128, 16], F32), lcr_d=([128, 16], F32),
                  lci_d=([128, 16], F32), w1_d=([4, 2048, 512], F32),
                  b1_d=([128, 64], F32), w2_d=([4, 512, 2048], F32),
                  b2_d=([128, 16], F32), lnlw_d=([128, 4], F32),
                  lnlb_d=([128, 4], F32), wout_d=([4096, 512], F32), rksel_d=([128, 8], F32),
                  bout_d=([1, 4096], F32))
    adins = {k: E(k, s, dt) for k, (s, dt) in aspecs.items()}
    adins["flag_d"] = mdins["flag_d"]
    x_out = nc.dram_tensor("x_out", [TOK, D], F32, kind="ExternalOutput").ap()

    wgu_slice = nc.dram_tensor("wgu_sl", [11, 128, 8192], BF16).ap()
    wgu_all = nc.dram_tensor("wgu_all", [11 * ncores * 128, 8192], BF16).ap()
    wd_slice = nc.dram_tensor("wd_sl", [8, 11, 128, 512], BF16).ap()
    wd_all = nc.dram_tensor("wd_all", [ncores * 8 * 11 * 128, 512], BF16).ap()
    h_spill = nc.dram_tensor("h_spill", [86, 128, TOK], BF16).ap()
    ipc_slice = nc.dram_tensor("ipc_sl", [19, 128, 4096], BF16).ap()
    ipc_all = nc.dram_tensor("ipc_all", [ncores * 19 * 128, 4096], BF16).ap()
    opc_slice = nc.dram_tensor("opc_sl", [8, 8, 128, 512], BF16).ap()
    opc_all = nc.dram_tensor("opc_all", [ncores * 8 * 8 * 128, 512], BF16).ap()
    w0_slice = nc.dram_tensor("w0_sl", [8, 4, 128, 512], BF16).ap()
    w0_all = nc.dram_tensor("w0_all", [ncores * 8 * 4 * 128, 512], BF16).ap()
    xA = nc.dram_tensor("xA", [TOK, D], F32).ap()
    xB = nc.dram_tensor("xB", [TOK, D], F32).ap()
    dd = mamba_dram(nc, ncores=ncores)
    dd1 = dict(carr_b_d=nc.dram_tensor("carr_b", [128, 8], F32).ap(),
               carr_a_d=nc.dram_tensor("carr_a", [ncores * 128, 8], F32).ap())

    wgu_v2 = wgu_all.rearrange("(j r p) m -> j r p m", r=ncores, p=128)
    wd_v = wd_all.rearrange("(r o k p) m -> r o k p m", o=8, k=11, p=128)
    ipc_v = ipc_all.rearrange("(a p) m -> a p m", p=128)
    opc_v = opc_all.rearrange("(r o k p) m -> r o k p m", o=8, k=8, p=128)
    w0_v = w0_all.rearrange("(r o k p) m -> r o k p m", o=8, k=4, p=128)

    def gu_pairs(ag_toks=None):
        out = []
        for j in range(11):
            for r in range(ncores):
                gp = 11 * r + j
                if gp < 86:
                    out.append((gp, wgu_v2[j, r],
                                (ag_toks[j],) if ag_toks else ()))
        return out

    m0 = ar.mark()
    mask_s = ar.alloc([128, 1], U32)
    pow2_s = ar.alloc([128, 1], F32)
    iota32_s = ar.alloc([32, 1], F32)
    iota86_s = ar.alloc([86, 1], F32)
    oneh = ar.alloc([32, NKD * 128], BF16)
    oneh_d = ar.alloc([86, 11 * 128], BF16)
    rep32 = ar.alloc([32, 128], I32)
    rep86 = ar.alloc([86, 128], I32)
    t1 = p.dma("sp", mask_s, mask_in, "misc")
    t1b = p.dma("sp", pow2_s, pow2_in, "misc")
    t2 = p.dma("sp", iota32_s, iota32, "misc")
    t3 = p.dma("sp", iota86_s, iota86, "misc")
    war0 = list(ar.war)
    ar.war = war0 + [t1, t1b, t2, t3]
    ctx["oneh_tok"] = emit_onehot_tables(p, ctx, gidx_ap=gidx_gu, iota_ap=iota32_s,
                                         oneh=oneh, n_groups=32, nkd=NKD, rep=rep32)
    ctx["oneh_d_tok"] = emit_onehot_tables(p, ctx, gidx_ap=gidx_d, iota_ap=iota86_s,
                                           oneh=oneh_d, n_groups=86, nkd=11, rep=rep86)
    ar.war = ar.war + [ctx["oneh_tok"], ctx["oneh_d_tok"]]
    emit_dequant_gu(p, ctx, qw_ap=qw_gu, qz_ap=qz_gu, sc_ap=sc_gu, oneh=oneh,
                    mask_ap=mask_s, pow2_ap=pow2_s, out_slice_ap=wgu_slice)
    gu_sts = p.stream_all("cst")
    emit_dequant_wd(p, ctx, qw_ap=qw_d, sc_full_ap=sc_d, qz_full_ap=qz_d,
                    oneh_d=oneh_d, mask_ap=mask_s, pow2_ap=pow2_s,
                    out_slice_ap=wd_slice)
    wd_sts = p.stream_all("cst")
    ar.reset(m0)
    xnT1 = ar.alloc([128, NKD * TOK], BF16)
    _, ln_last1 = emit_ln_transpose(p, ctx, x_ap=x_in, xnT=xnT1, x_ready=())
    ar.war = ar.war + wd_sts
    cldb = [ar.alloc([128, 4096], F32) for _ in range(2)]
    ccvb = [ar.alloc([128, 4096], BF16) for _ in range(2)]
    gnw_s = ar.alloc([128, 8], F32)
    g1 = p.dma("pool", gnw_s, gnw_sl, "pb", waits=ar.war)
    war_early = list(ar.war)
    chunks = cache_chunks(ip_rows=ip_rows, ipc_slice=ipc_slice,
                          op_cols=op_cols, opc_slice=opc_slice,
                          w0_cols=w0_cols, w0_slice=w0_slice)
    cache_hook = make_cache_hook(p, chunks=chunks, cldb=cldb, ccvb=ccvb,
                                 gnw_s=gnw_s, g1=g1, war=war_early)

    ag_gu = [emit_allgather(p, in_ap=wgu_slice[j],
                            out_ap=wgu_all[j * ncores * 128:
                                           (j + 1) * ncores * 128, :],
                            ncores=ncores, waits=gu_sts)
             for j in range(11)]
    ag2 = emit_allgather(p, in_ap=wd_slice.rearrange("a b p m -> (a b p) m"),
                         out_ap=wd_all, ncores=ncores, waits=wd_sts)
    h_toks = emit_ffn_mm1(p, ctx, xnT=xnT1, h_out_ap=h_spill,
                          x_ready_toks=[ln_last1], pairs=gu_pairs(ag_gu),
                          hook=cache_hook)
    cs_tails = p.stream_all("cs")
    ag3 = emit_allgather(p, in_ap=ipc_slice.rearrange("a p m -> (a p) m"),
                         out_ap=ipc_all, ncores=ncores, waits=cs_tails)
    ag4 = emit_allgather(p, in_ap=opc_slice.rearrange("a b p m -> (a b p) m"),
                         out_ap=opc_all, ncores=ncores, waits=cs_tails)
    ag5 = emit_allgather(p, in_ap=w0_slice.rearrange("a b p m -> (a b p) m"),
                         out_ap=w0_all, ncores=ncores, waits=cs_tails)
    emit_ffn_mm2(p, ctx,
                 wd_cache_ap=lambda oc, k: wd_v[k // 11, oc, k % 11],
                 h_ap=h_spill, x_ap=x_in, xout_ap=xA,
                 h_ready_toks=h_toks + [ag2], nk=86, w_ready=[ag2])
    ar.war = ar.war + cs_tails
    ar.reset(m0)
    ags = ag_gu + [ag2, ag3, ag4, ag5]

    def ffn_sub(xi, xo):
        m = ar.mark()
        xnT = ar.alloc([128, NKD * TOK], BF16)
        _, ln_last = emit_ln_transpose(p, ctx, x_ap=xi, xnT=xnT, x_ready=())
        h_toks = emit_ffn_mm1(p, ctx, xnT=xnT, h_out_ap=h_spill,
                              x_ready_toks=[ln_last], pairs=gu_pairs())
        emit_ffn_mm2(p, ctx,
                     wd_cache_ap=lambda oc, k: wd_v[k // 11, oc, k % 11],
                     h_ap=h_spill, x_ap=xi, xout_ap=xo,
                     h_ready_toks=h_toks, nk=86)
        ar.reset(m)

    def l1ffn_sub(xi, xo):
        emit_l1ffn(p, ctx, x_ap=xi, xout_ap=xo,
                   w0_cache_ap=lambda oc: [
                       w0_v[r, oc].rearrange("k p m -> p k m")
                       for r in range(ncores)],
                   a_d=a_d, b_d=b_d, b0_d=b0_d, mask_d=mask_d)

    ar.war = ar.war + [ag3, ag4]
    emit_mamba_sublayer(p, ctx, nc, x_ap=xA, xout_ap=xB,
                        inproj_cache_ap=ipc_v,
                        outproj_cache_ap=lambda oc, k: opc_v[k // 8, oc, k % 8],
                        dins=mdins, dd=dd, ncores=ncores, op_w_ready=[ag4])
    ffn_sub(xB, xA)
    ar.war = ar.war + [ag5]
    l1ffn_sub(xA, xB)
    emit_l1attn(p, ctx, nc, x_ap=xB, xout_ap=xA,
                dins=adins, dd1=dd1, ncores=ncores)
    ar.war = ar.war + [ag5]
    l1ffn_sub(xA, x_out)
    p.barrier("pool", p.stream_all("xst"))
    p.replay(alloc_sems(nc, prog_sems(p)))
    return nc


_PROGS = {}


def _progs():
    if not _PROGS:
        _PROGS["fused"] = build_fused()
    return _PROGS


def _consts():
    import ml_dtypes
    c = {}
    c["ident_d"] = np.eye(128, dtype=np.float32).astype(ml_dtypes.bfloat16)
    c["identf_d"] = np.eye(128, dtype=np.float32)
    c["mask"] = (np.uint32(0xF) << (4 * (np.arange(128, dtype=np.uint32) % 8)))[:, None]
    c["pow2"] = (2.0 ** (-4.0 * (np.arange(128) % 8)))[:, None].astype(np.float32)
    c["iota32"] = np.arange(32, dtype=np.float32)[:, None]
    c["iota86"] = np.arange(86, dtype=np.float32)[:, None]
    c["utri_d"] = np.triu(np.ones((128, 128), np.float32)).astype(ml_dtypes.bfloat16)
    return c


def _s16(a):
    return np.ascontiguousarray(a.reshape(4, 4, 128).transpose(2, 0, 1)
                                .reshape(128, 16)).astype(np.float32)


def _prep_maps(Z):
    import ml_dtypes
    import jax
    C = _consts()
    with jax.default_device(jax.devices("cpu")[0]):
        u = np.asarray(jax.random.uniform(jax.random.key(42), (4, 2048, 4096)))

    ffn_maps, mam_maps, l1f_maps, l1a_maps = [], [], [], []
    qweight_gu = Z["L0_ffn_qweight_gu"]; qzeros_gu = Z["L0_ffn_qzeros_gu"]
    scales_gu = Z["L0_ffn_scales_gu"].astype(np.float32)
    qweight_d = Z["L0_ffn_qweight_down"]; qzeros_d = Z["L0_ffn_qzeros_down"]
    scales_d = Z["L0_ffn_scales_down"].astype(np.float32)
    g_idx_gu = Z["L0_ffn_g_idx_gu"].astype(np.int32)
    g_idx_d = Z["L0_ffn_g_idx_down"].astype(np.int32)
    W_ip = Z["L0_attn_in_proj_w"].astype(np.float32)
    W_op = Z["L0_attn_out_proj_w"].astype(np.float32)
    gnorm_w = Z["L0_attn_gnorm_w"].astype(np.float32)
    W0 = Z["L1_ffn_W0"].astype(np.float32)

    mam_base = dict(
        norm_w_g=np.ascontiguousarray(Z["L0_attn_norm_w"].reshape(NKD, 128).T)
        .astype(np.float32),
        convw_d=np.ascontiguousarray(Z["L0_attn_conv_w"].reshape(80, 128, 4)
                                     .transpose(1, 0, 2).reshape(128, 320))
        .astype(np.float32),
        convb_d=np.ascontiguousarray(Z["L0_attn_conv_b"].reshape(80, 128).T)
        .astype(np.float32),
        dt_bias_d=Z["L0_attn_dt_bias"][:, None].astype(np.float32),
        A_log_d=Z["L0_attn_A_log"][:, None].astype(np.float32),
        dp_row_d=np.repeat(Z["L0_attn_Dp"], 64)[None, :].astype(np.float32),
        gnw_d=np.ascontiguousarray(gnorm_w.reshape(64, 128).T).astype(np.float32),
        utri_d=C["utri_d"], ident_d=C["ident_d"], identf_d=C["identf_d"])
    g = lambda k: Z["L1_attn_" + k]
    l1a_base = dict(
        win_d=g("Win").astype(np.float32),
        bin_d=np.ascontiguousarray(g("bin").reshape(4, 128).T).astype(np.float32),
        l1w_d=_s16(g("ln1_w")), l1b_d=_s16(g("ln1_b")),
        l2w_d=_s16(g("ln2_w")), l2b_d=_s16(g("ln2_b")),
        scl_d=_s16(g("sc_lin")), phzr_d=_s16(g("phz_re")), phzi_d=_s16(g("phz_im")),
        lcr_d=_s16(g("lc_re")), lci_d=_s16(g("lc_im")),
        w1_d=g("W1").astype(np.float32),
        b1_d=np.ascontiguousarray(g("b1").reshape(4, 16, 128).transpose(2, 0, 1)
                                  .reshape(128, 64)).astype(np.float32),
        w2_d=g("W2").astype(np.float32), b2_d=_s16(g("b2")),
        lnlw_d=np.ascontiguousarray(g("lnl_w").reshape(4, 128).T).astype(np.float32),
        lnlb_d=np.ascontiguousarray(g("lnl_b").reshape(4, 128).T).astype(np.float32),
        wout_d=g("Wout").astype(np.float32),
        bout_d=g("bout")[None, :].astype(np.float32),
        ident_d=C["ident_d"], identf_d=C["identf_d"])
    l1f_base = dict(a_d=Z["L1_ffn_A"].reshape(64, 4096).astype(np.float32),
                    b_d=Z["L1_ffn_B"].astype(np.float32),
                    b0_d=Z["L1_ffn_b0"][None, :].astype(np.float32),
                    ident_d=C["ident_d"], identf_d=C["identf_d"])

    for r in range(NC):
        qw = np.zeros((512, 2816), np.uint32)
        sc = np.zeros((32, 2816), np.float32)
        qz = np.zeros((32, 352), np.uint32)
        g0, g1 = 1408 * r, min(1408 * (r + 1), 11008)
        n = g1 - g0
        qw[:, :n] = qweight_gu[:, g0:g1]
        qw[:, 1408:1408 + n] = qweight_gu[:, 11008 + g0:11008 + g1]
        sc[:, :n] = scales_gu[:, g0:g1]
        sc[:, 1408:1408 + n] = scales_gu[:, 11008 + g0:11008 + g1]
        nz = n // 8
        qz[:, :nz] = qzeros_gu[:, g0 // 8:g1 // 8]
        qz[:, 176:176 + nz] = qzeros_gu[:, 1376 + g0 // 8:1376 + g1 // 8]
        qwd = np.zeros((176, 4096), np.uint32)
        gid = np.zeros((1408,), np.int32)
        qwd[:(g1 - g0) // 8] = qweight_d[g0 // 8:g1 // 8]
        gid[:g1 - g0] = g_idx_d[g0:g1]
        ffn_maps.append(dict(qw_gu=qw, qz_gu=qz, sc_gu=sc, gidx_gu=g_idx_gu,
                             qw_d=qwd, qz_d=qzeros_d, sc_d=scales_d, gidx_d=gid,
                             mask=C["mask"], pow2=C["pow2"],
                             iota32=C["iota32"], iota86=C["iota86"],
                             ident_d=C["ident_d"], identf_d=C["identf_d"]))
        ipr = np.zeros((19 * 128, 4096), np.float32)
        r0, r1 = 19 * 128 * r, min(19 * 128 * (r + 1), 18560)
        ipr[:r1 - r0] = W_ip[r0:r1]
        m = dict(mam_base)
        m["ip_rows"] = np.ascontiguousarray(
            ipr.reshape(19, 128, 32, 128).transpose(0, 3, 2, 1)
            .reshape(19 * 128, 4096))
        m["op_cols"] = np.ascontiguousarray(
            W_op[:, 1024 * r:1024 * (r + 1)].reshape(8, 512, 8, 128)
            .transpose(0, 2, 3, 1).reshape(8192, 512))
        m["gnw_sl"] = np.ascontiguousarray(
            gnorm_w[1024 * r:1024 * (r + 1)].reshape(8, 128).T)
        m["flag_d"] = np.full((128, 1), float(r % 2), np.float32)
        mam_maps.append(m)
        mf = dict(l1f_base)
        mf["w0_cols"] = np.ascontiguousarray(
            W0[:, 512 * r:512 * (r + 1)].reshape(8, 512, 4, 128)
            .transpose(0, 2, 3, 1).reshape(4096, 512))
        useg = u[r // 2, (r % 2) * 1024:(r % 2 + 1) * 1024]
        m4 = np.zeros((4, TOK, 4096), np.float32)
        for l in range(4):
            m4[l] = 4.0 * ((useg >= 0.25 * l) & (useg < 0.25 * (l + 1)))
        mf["mask_d"] = m4.astype(ml_dtypes.bfloat16)
        l1f_maps.append(mf)
        ma = dict(l1a_base)
        ma["flag_d"] = np.full((128, 1), float(r % 2), np.float32)
        rk = np.zeros((128, 8), np.float32)
        rk[:, (r + NC - 1) % NC] = 1.0
        ma["rksel_d"] = rk
        l1a_maps.append(ma)
    return ffn_maps, mam_maps, l1f_maps, l1a_maps


def kernel(**Z):
    P = _progs()
    ffn_maps, mam_maps, l1f_maps, l1a_maps = _prep_maps(Z)
    x = np.ascontiguousarray(Z["L0_attn_x"]).astype(np.float32)
    maps = []
    for r in range(NC):
        fm = {**ffn_maps[r], **mam_maps[r], **l1f_maps[r], **l1a_maps[r]}
        fm["x"] = np.ascontiguousarray(
            x[r // 2, (r % 2) * 1024:(r % 2 + 1) * 1024])
        maps.append(fm)
    res = run_bass_kernel_spmd(P["fused"], maps, core_ids=list(range(NC)))
    out = np.zeros((4, 2048, 4096), np.float32)
    for c in range(NC):
        out[c // 2, (c % 2) * 1024:(c % 2 + 1) * 1024] = \
            np.asarray(res.results[c]["x_out"])
    return out
